# version 17
# baseline (speedup 1.0000x reference)
import contextlib
import numpy as np
import ml_dtypes
import concourse.bass as bass
import concourse.mybir as mybir
from concourse.bass_utils import run_bass_kernel_spmd

F32 = mybir.dt.float32
BF16 = mybir.dt.bfloat16
I32 = mybir.dt.int32
AF = mybir.ActivationFunctionType
ALU = mybir.AluOpType
AX = mybir.AxisListType
bfdt = ml_dtypes.bfloat16

S, E, H, D, C = 4096, 1024, 16, 64, 64
HID = 3072
TT = 512
NT = S // TT
NSL = TT // 128
NCH = TT // C
NEG = -1.0e30
RQ = 1.0 / 127.0

_DEBUG_TAPS = None
_N_TILES = NT
_CHUNK_ON = True
_CHUNK_SUB = 9
_SEQ_ON = True
_EPI_ON = True
_PHASEB_ON = True


def host_consts():
    c = {}
    eh = np.zeros((16, 1024), np.float32)
    for h in range(16):
        eh[h, h*64:(h+1)*64] = 1.0
    c["ehead16"] = eh.astype(bfdt)
    lt = np.zeros((48, 512), np.float32)
    lt[32:48] = -1.0
    c["lhs_tmpl"] = lt
    rt = np.zeros((48, 1024), np.float32)
    rt[0:16] = eh
    c["rhs_tmpl"] = rt
    t_idx = (np.arange(128) % 64)[:, None].astype(np.float32)
    s_idx = (np.arange(1024) % 64)[None, :].astype(np.float32)
    c["maskA"] = np.where(s_idx < t_idx, 0.0, NEG).astype(bfdt)
    c["maskAT"] = np.where(t_idx < s_idx, 0.0, NEG).astype(bfdt)
    c["maskMT"] = np.where(t_idx <= s_idx, 0.0, NEG).astype(bfdt)
    c["identP"] = np.where(s_idx == t_idx, 1.0, 0.0).astype(bfdt)
    sm = np.ones((16, 512), np.float32)
    sm[:, ::C] = 0.0
    c["scanmask"] = sm.astype(bfdt)
    ehm = np.zeros((128, 128), np.float32)
    for mc in range(8):
        ehm[0:64, mc*16 + 2*mc] = 1.0
        ehm[64:128, mc*16 + 2*mc + 1] = 1.0
    c["ehm"] = ehm
    c["identF"] = np.eye(128, dtype=np.float32)
    par = np.zeros((16, 128), np.float32)
    par[0::2, 0:64] = 1.0
    par[1::2, 64:128] = 1.0
    c["parity"] = par
    ep = np.zeros((16, 512), np.float32)
    for hp in range(16):
        ep[hp, (hp//2)*64:(hp//2+1)*64] = 1.0
    c["epair"] = ep.astype(bfdt)
    return c


_POS_NAMES = {
    "tensor_copy": ("out", "in_"),
    "tensor_tensor": ("out", "in0", "in1"),
    "tensor_scalar": ("out", "in0", "scalar1", "scalar2"),
    "tensor_scalar_mul": ("out", "in0", "scalar1"),
    "tensor_scalar_add": ("out", "in0", "scalar1"),
    "tensor_scalar_sub": ("out", "in0", "scalar1"),
    "tensor_scalar_min": ("out", "in0", "scalar1"),
    "tensor_scalar_max": ("out", "in0", "scalar1"),
    "scalar_tensor_tensor": ("out", "in0", "scalar", "in1"),
    "tensor_reduce": ("out", "in_"),
    "reciprocal": ("out", "in_"),
    "memset": ("out", "constant"),
    "tensor_tensor_scan": ("out", "data0", "data1", "initial"),
    "activation": ("out", "in_", "func", "bias", "scale", "alpha", "accum_out"),
    "copy": ("out", "in_"),
}
_WRITE_KEYS = {"out", "accum_out", "dest", "ap"}
_OPNAMES = set(_POS_NAMES)


def build_program():
    nc = bass.Bass("TRN2", target_bir_lowering=False)

    x_in = nc.dram_tensor("x", [S, E], F32, kind="ExternalInput").ap()
    xT_in = nc.dram_tensor("xT", [E, S], F32, kind="ExternalInput").ap()
    w_in = {}
    for nm in ["wq", "wk", "wv", "wgate", "wout"]:
        w_in[nm] = nc.dram_tensor(nm, [E, E], F32, kind="ExternalInput").ap()
    wa_in = nc.dram_tensor("wa", [E, 16], F32, kind="ExternalInput").ap()
    wb_in = nc.dram_tensor("wb", [E, 16], F32, kind="ExternalInput").ap()
    alog_in = nc.dram_tensor("alog", [16, 1], F32, kind="ExternalInput").ap()
    dtb_in = nc.dram_tensor("dtb", [16, 1], F32, kind="ExternalInput").ap()
    rmsw_in = nc.dram_tensor("rmsw_c", [128, 1], F32, kind="ExternalInput").ap()
    fgT_in = nc.dram_tensor("fgT", [E, HID], I32, kind="ExternalInput").ap()
    fuT_in = nc.dram_tensor("fuT", [E, HID], I32, kind="ExternalInput").ap()
    fdT_in = nc.dram_tensor("fdT", [HID, E], I32, kind="ExternalInput").ap()
    cns_in = {}
    for nm, shp, dt in [("ehead16", [16, 1024], BF16), ("lhs_tmpl", [48, 512], F32),
                        ("rhs_tmpl", [48, 1024], F32), ("maskA", [128, 1024], BF16),
                        ("maskAT", [128, 1024], BF16), ("maskMT", [128, 1024], BF16),
                        ("identP", [128, 1024], BF16), ("scanmask", [16, 512], BF16),
                        ("ehm", [128, 128], F32), ("parity", [16, 128], F32),
                        ("epair", [16, 512], BF16), ("identF", [128, 128], F32)]:
        cns_in[nm] = nc.dram_tensor(nm, shp, dt, kind="ExternalInput").ap()
    out_t = nc.dram_tensor("out", [S, E], F32, kind="ExternalOutput").ap()

    kfm_h = nc.dram_tensor("kfm_h", [E, S], BF16, kind="Internal").ap()
    og_h = nc.dram_tensor("og_h", [S, E], BF16, kind="Internal").ap()
    x2_h = nc.dram_tensor("x2_h", [S, E], BF16, kind="Internal").ap()

    sems, cnt, wm = {}, {}, {}
    WRT, RDS = {}, {}
    SCRIPT = {k: [] for k in ("sy", "gp", "pe", "dv", "sc")}

    def op(ename, method, *args, **kwargs):
        item = ["op", method, args, kwargs, []]
        SCRIPT[ename].append(item)
        return item

    def inc(item, name, step=1):
        item[4].append((name, step))
        cnt[name] += step

    def wait(ename, name, val):
        if val <= 0:
            return
        if wm.get((ename, name), -1) >= val:
            return
        SCRIPT[ename].append(("w", name, val))
        wm[(ename, name)] = val

    _REGIONAL = {"t_U", "t_NA", "t_Z0", "t_B1", "t_Z1", "t_P0", "t_P1", "t_P2",
                 "t_MTa", "t_MTb", "t_R", "kfm", "qfm", "bc4", "wq_s", "wk_s"}

    def regkeys(ap):
        name = ap.tensor.name
        if name not in _REGIONAL:
            return (name,)
        dims = ap.ap
        W = dims[0][0]
        pb = ap.offset // W
        pc = dims[0][1]
        cb = ap.offset % W
        ext = 1
        for st_, sz_ in dims[1:]:
            ext += (sz_ - 1) * abs(st_)
        phs = set()
        if pb < 64:
            phs.add(0)
        if pb + pc > 64:
            phs.add(1)
        chs = set()
        half = W // 2
        if cb < half:
            chs.add(0)
        if cb + ext > half:
            chs.add(1)
        return tuple(f"{name}/{p}{c}" for p in phs for c in chs)

    def dep_wait(ename, wnames, rnames):
        for n in rnames | wnames:
            t = WRT.get(n)
            if t:
                wait(ename, t[0], t[1])
        for n in wnames:
            for s_, v_ in RDS.get(n, {}).items():
                wait(ename, s_, v_)

    def dep_record(sem, val, wnames, rnames):
        for n in wnames:
            WRT[n] = (sem, val)
            RDS[n] = {}
        for n in rnames:
            d = RDS.setdefault(n, {})
            if d.get(sem, -1) < val:
                d[sem] = val

    def classify(opname, args, kwargs):
        wnames, rnames = set(), set()
        names = _POS_NAMES.get(opname, ())
        items = list(zip(names, args)) + list(kwargs.items())
        for k, v in items:
            if isinstance(v, bass.AP):
                (wnames if k in _WRITE_KEYS else rnames).update(regkeys(v))
        return wnames, rnames

    _pe_win = []
    _pe_gid = [0]
    _pe_idx = [0]

    class P_:
        def __init__(self, ename, sem):
            self._ename = ename
            self._sem = sem

        def __getattr__(self, name):
            if name in _OPNAMES:
                def wrapped(*a, **k):
                    wn, rn = classify(name, a, k)
                    dep_wait(self._ename, wn, rn)
                    item = op(self._ename, name, *a, **k)
                    inc(item, self._sem, 1)
                    dep_record(self._sem, cnt[self._sem], wn, rn)
                    return item
                return wrapped
            raise AttributeError(name)

    class PE_:
        def __init__(self):
            self._pw, self._pr = set(), set()
            self._last = None

        def matmul(self, out, lhsT, rhs, **k):
            wn = set(regkeys(out))
            rn = set(regkeys(lhsT)) | set(regkeys(rhs))
            dep_wait("pe", wn, rn)
            oF = out.ap[0][0]
            pb, pc = out.offset // oF, out.ap[0][1]
            cb = out.offset % oF
            ncol = 1
            for _, sz in out.ap[1:]:
                ncol *= sz
            lF = lhsT.ap[0][0]
            rb, rc = lhsT.offset // lF, lhsT.ap[0][1]
            banks = set(range(cb // 512, (cb + ncol - 1) // 512 + 1))
            gid = _pe_gid[0]
            idx = _pe_idx[0]
            _pe_idx[0] += 1
            keep, drain = [], False
            for X in _pe_win:
                xt, xp0, xp1, xr0, xr1, xb, xg, xi = X
                if xi < idx - 4:
                    continue
                if not (xr1 <= rb or rb + rc <= xr0):
                    continue
                if (xt == out.tensor.name and not (xp1 <= pb or pb + pc <= xp0)
                        and (xb & banks)):
                    if xg != gid:
                        drain = True
                    else:
                        raise RuntimeError(
                            f"intra-group PE hazard on {xt} parts {pb}:{pb+pc}")
                keep.append(X)
            if drain:
                wait("pe", "pe", cnt["pe"])
                keep = []
            del _pe_win[:]
            _pe_win.extend(keep)
            _pe_win.append((out.tensor.name, pb, pb + pc, rb, rb + rc, banks, gid, idx))
            item = op("pe", "matmul", out, lhsT, rhs, **k)
            self._last = item
            self._pw |= wn
            self._pr |= rn
            return item

        def transpose(self, out, in_, identity, **k):
            wn = set(regkeys(out))
            rn = set(regkeys(in_)) | set(regkeys(identity))
            dep_wait("pe", wn, rn)
            oF = out.ap[0][0]
            pb, pc = out.offset // oF, out.ap[0][1]
            cb = out.offset % oF
            ncol = 1
            for _, sz in out.ap[1:]:
                ncol *= sz
            banks = set(range(cb // 512, (cb + ncol - 1) // 512 + 1))
            item = op("pe", "transpose", out, in_, identity, **k)
            self._last = item
            del _pe_win[:]
            _pe_win.append((out.tensor.name, pb, pb + pc, 0, 128, banks,
                            _pe_gid[0], _pe_idx[0]))
            _pe_idx[0] += 1
            self._pw |= wn
            self._pr |= rn
            return item

        def flush(self):
            if self._last is None:
                return
            inc(self._last, "pe", 1)
            dep_record("pe", cnt["pe"], self._pw, self._pr)
            self._pw, self._pr = set(), set()
            self._last = None
            _pe_gid[0] += 1

    QSEMS = {"sy": ["ldA0", "ldA1", "ldA2", "ldA3"],
             "gpL": ["ldB0", "ldB1", "ldB2"],
             "gpS": ["so0", "so1"]}
    qidx = {k: 0 for k in QSEMS}

    class DM_:
        def __init__(self, ename, qkey):
            self._ename = ename
            self._qkey = qkey

        def dma_start(self, out, in_, **k):
            grp = QSEMS[self._qkey]
            sem = grp[qidx[self._qkey] % len(grp)]
            qidx[self._qkey] += 1
            if cnt[sem] > 0:
                wait(self._ename, sem, cnt[sem])
            wn, rn = set(regkeys(out)), set(regkeys(in_))
            dep_wait(self._ename, wn, rn)
            item = op(self._ename, "dma_start", out=out, in_=in_, **k)
            inc(item, sem, 16)
            dep_record(sem, cnt[sem], wn, rn)
            return item

    class DMGP:
        def __init__(self):
            self._dma = DM_("gp", "gpL")
            self._cmp = P_("gp", "pool")

        def dma_start(self, out, in_, **k):
            return self._dma.dma_start(out, in_, **k)

        def __getattr__(self, name):
            if name in _OPNAMES:
                return getattr(self._cmp, name)
            raise AttributeError(name)

    with contextlib.ExitStack() as st:
        for s in ["ldA0", "ldA1", "ldA2", "ldA3", "ldB0", "ldB1", "ldB2",
                  "so0", "so1", "pe", "dve", "act", "pool", "bar"]:
            sems[s] = st.enter_context(nc.semaphore(s))
            cnt[s] = 0

        blk = st.enter_context(nc.Block())

        sy, gp = DM_("sy", "sy"), DMGP()
        pe, dv, sc = PE_(), P_("dv", "dve"), P_("sc", "act")

        def replay(eng, script):
            for it_ in script:
                if it_[0] == "w":
                    eng.wait_ge(sems[it_[1]], it_[2])
                else:
                    _, name, args, kwargs, incs = it_
                    instr = getattr(eng, name)(*args, **kwargs)
                    for s_, step in incs:
                        instr.then_inc(sems[s_], step)

        def emit_all():
            blk.sync(lambda e: replay(e, SCRIPT["sy"]))
            blk.gpsimd(lambda e: replay(e, SCRIPT["gp"]))
            blk.tensor(lambda e: replay(e, SCRIPT["pe"]))
            blk.vector(lambda e: replay(e, SCRIPT["dv"]))
            blk.scalar(lambda e: replay(e, SCRIPT["sc"]))
            for k in SCRIPT:
                SCRIPT[k] = []

        def tap(name, sb_ap, shape, dtype=F32):
            if not _DEBUG_TAPS or name not in _DEBUG_TAPS:
                return
            dr = nc.dram_tensor("tap_" + name, shape, dtype, kind="ExternalOutput").ap()
            DM_("gp", "gpS").dma_start(dr, sb_ap)

        stA = contextlib.ExitStack()
        sbA = lambda n, s_, d: stA.enter_context(nc.sbuf_tensor(n, s_, d))
        W = {}
        for nm in ["wq", "wk", "wv", "wgate", "wout"]:
            W[nm] = sbA(nm + "_s", [128, 8*1024], BF16)
        wa_s = sbA("wa_s", [128, 8*16], BF16)
        wb_s = sbA("wb_s", [128, 8*16], BF16)
        CN = {}
        for nm, shp, dt in [("ehead16", [16, 1024], BF16),
                            ("maskA", [128, 1024], BF16), ("maskAT", [128, 1024], BF16),
                            ("maskMT", [128, 1024], BF16), ("identP", [128, 1024], BF16),
                            ("scanmask", [16, 512], BF16), ("ehm", [128, 128], F32),
                            ("parity", [16, 128], F32), ("epair", [16, 512], BF16),
                            ("identF", [128, 128], F32)]:
            CN[nm] = sbA("c_" + nm, shp, dt)
        ST = {}
        for nm in ["p", "m", "n"]:
            ST[nm] = sbA("st_" + nm, [48, 512], F32)
        RH = sbA("rh", [48, 1024], F32)
        rmswc = sbA("rmswc", [128, 1], F32)
        SM = {}
        for nm in ["alog", "dtb", "negea"]:
            SM[nm] = sbA("s_" + nm, [16, 1], F32)
        S0 = sbA("S0", [128, 512], F32)
        S0b = sbA("S0b", [128, 512], BF16)
        xbsl = sbA("xbsl", [128, 1024], BF16)
        xfm2 = [sbA("xfmA", [128, 8*512], BF16), sbA("xfmB", [128, 8*512], BF16)]
        qfm = sbA("qfm", [128, 8*512], BF16)
        kfm = sbA("kfm", [128, 8*512], BF16)
        ktm = [sbA("ktmA", [128, 1024], BF16), sbA("ktmB", [128, 1024], BF16)]
        vtm2 = [sbA("vtmA", [128, 1024], BF16), sbA("vtmB", [128, 1024], BF16)]
        otm = [sbA("otmA", [128, 1024], BF16), sbA("otmB", [128, 1024], BF16)]
        scr1 = sbA("scr1", [128, 1024], F32)
        escr = sbA("escr", [128, 1024], F32)
        scrb = sbA("scrb", [128, 1024], BF16)
        colsc = sbA("colsc", [128, 64], F32)
        sSd = sbA("sSd", [128, 512], F32)
        vec4 = sbA("vec4", [128, 512], F32)
        bc4 = sbA("bc4", [128, 128], F32)
        sEg1 = sbA("sEg1", [16, 1], F32)
        VV = {}
        for nm in ["g", "lnk", "lnb", "r", "ek", "s1", "s2"]:
            VV[nm] = sbA("v_" + nm, [16, 512], F32)
        sEgr = VV["s2"]
        TL = {}
        for nm in ["NA", "Z0", "B1", "Z1", "P0", "P1", "P2",
                   "MTa", "MTb", "E", "R", "U"]:
            TL[nm] = sbA("t_" + nm, [128, 1024], BF16)
        TL["B2"] = TL["NA"]
        TL["Z2"] = TL["Z0"]
        TL["t2"] = TL["E"]
        TL["Ku"] = TL["R"]
        TL["Eg"] = TL["B1"]
        t_Gm = scr1

        p1 = stA.enter_context(nc.psum_tensor("p1", [128, 1024], F32))
        p2 = stA.enter_context(nc.psum_tensor("p2", [128, 1024], F32))
        pS = stA.enter_context(nc.psum_tensor("pS", [128, 1024], F32))
        pP1 = stA.enter_context(nc.psum_tensor("pP1", [128, 512], F32))
        pP2 = stA.enter_context(nc.psum_tensor("pP2", [128, 512], F32))

        for nm in ["wq", "wk"]:
            for hl in range(2):
                gp.dma_start(
                    W[nm][:, hl*4096:(hl+1)*4096].rearrange("p (kc n) -> p kc n", n=1024),
                    w_in[nm][hl*512:(hl+1)*512, :].rearrange("(kc p) n -> p kc n", p=128))
        for nm in ["ehead16", "maskA", "maskAT", "maskMT", "identP", "scanmask",
                   "ehm", "parity", "epair", "identF"]:
            sy.dma_start(CN[nm][:, :], cns_in[nm])
        for nm in ["p", "m", "n"]:
            sy.dma_start(ST[nm][:, :], cns_in["lhs_tmpl"])
        sy.dma_start(RH[:, :], cns_in["rhs_tmpl"])
        sy.dma_start(rmswc[:, :], rmsw_in)
        sy.dma_start(SM["alog"][:, :], alog_in)
        sy.dma_start(SM["dtb"][:, :], dtb_in)

        sc.activation(out=SM["negea"][:, :], in_=SM["alog"][:, :], func=AF.Exp)
        dv.tensor_scalar_mul(out=SM["negea"][:, :], in0=SM["negea"][:, :], scalar1=-1.0)
        dv.memset(S0[:, :], 0.0)
        dv.memset(S0b[:, :], 0.0)
        dv.memset(vec4[:, :], 0.0)

        def xfm_load(it):
            t0 = it * TT
            xf = xfm2[it % 2]
            def ld(xf=xf, t0=t0):
                for ec in range(8):
                    gp.dma_start(xf[:, ec*512:(ec+1)*512],
                                 xT_in[ec*128:(ec+1)*128, t0:t0+TT])
            return ld

        def t0a_stages(it, skip_load=False):
            t0 = it * TT
            xfm = xfm2[it % 2]
            out = []
            if not skip_load:
                out.append(xfm_load(it))
            for gi, (wsrc, dstf) in enumerate([(W["wq"], qfm), (W["wk"], kfm)]):
                for mc in range(8):
                    if gi == 0 and mc < 2:
                        bank = p1[:, 0:512] if mc == 0 else p1[:, 512:1024]
                    else:
                        bank = pP1[:, :] if mc % 2 == 0 else pP2[:, :]
                    def f(wsrc=wsrc, mc=mc, bank=bank, dstf=dstf):
                        for kc in range(8):
                            pe.matmul(bank,
                                      wsrc[:, kc*1024 + mc*128:kc*1024 + (mc+1)*128],
                                      xfm[:, kc*512:(kc+1)*512],
                                      start=(kc == 0), stop=(kc == 7))
                        pe.flush()
                        dv.tensor_copy(dstf[:, mc*512:(mc+1)*512], bank)
                    out.append(f)
            def kx():
                sy.dma_start(kfm_h[:, t0:t0+TT].rearrange("(mc p) t -> p mc t", p=128),
                             kfm[:, :].rearrange("p (mc t) -> p mc t", t=512))
            out.append(kx)
            return out

        def t0b_stages(it):
            t0 = it * TT
            xfm = xfm2[it % 2]
            out = []
            g, lnb = VV["g"], VV["lnb"]
            lnk, r, ek = VV["lnk"], VV["r"], VV["ek"]
            s1, s2 = VV["s1"], VV["s2"]
            def dt_chain():
                for kc in range(8):
                    pe.matmul(pP1[0:16, :], wa_s[:, kc*16:(kc+1)*16],
                              xfm[:, kc*512:(kc+1)*512], start=(kc == 0), stop=(kc == 7))
                pe.flush()
                dv.tensor_copy(s1[:, :], pP1[0:16, :])
                sc.activation(out=s2[:, :], in_=s1[:, :], func=AF.Abs, bias=SM["dtb"][:, :])
                sc.activation(out=ek[:, :], in_=s2[:, :], func=AF.Exp, scale=-1.0)
                sc.activation(out=s2[:, :], in_=ek[:, :], func=AF.Ln, bias=1.0)
                sc.activation(out=ek[:, :], in_=s1[:, :], func=AF.Relu, bias=SM["dtb"][:, :])
                dv.tensor_tensor(out=s2[:, :], in0=s2[:, :], in1=ek[:, :], op=ALU.add)
                dv.tensor_scalar_mul(out=s1[:, :], in0=s2[:, :], scalar1=SM["negea"][:, :])
                dv.tensor_tensor_scan(out=g[:, :], data0=CN["scanmask"][:, :],
                                      data1=s1[:, :], initial=0.0,
                                      op0=ALU.mult, op1=ALU.add)
            out.append(dt_chain)
            def beta_chain():
                for kc in range(8):
                    pe.matmul(pP2[0:16, :], wb_s[:, kc*16:(kc+1)*16],
                              xfm[:, kc*512:(kc+1)*512], start=(kc == 0), stop=(kc == 7))
                pe.flush()
                dv.tensor_copy(s1[:, :], pP2[0:16, :])
                sc.activation(out=s2[:, :], in_=s1[:, :], func=AF.Abs)
                sc.activation(out=r[:, :], in_=s2[:, :], func=AF.Exp, scale=-1.0)
                sc.activation(out=s2[:, :], in_=r[:, :], func=AF.Ln, bias=1.0)
                dv.tensor_scalar_min(out=r[:, :], in0=s1[:, :], scalar1=0.0)
                dv.tensor_tensor(out=lnb[:, :], in0=r[:, :], in1=s2[:, :], op=ALU.subtract)
            out.append(beta_chain)

            def norms():
                for which, srcf in [("k", kfm), ("q", qfm)]:
                    for mc in range(8):
                        sq = scr1[:, 0:512] if mc % 2 == 0 else scr1[:, 512:1024]
                        sc.activation(out=sq, in_=srcf[:, mc*512:(mc+1)*512],
                                      func=AF.Square)
                        pe.matmul(pP1[0:16, :], CN["ehm"][:, mc*16:(mc+1)*16], sq,
                                  start=(mc == 0), stop=(mc == 7))
                        pe.flush()
                    sc.activation(out=VV["s2"][:, :], in_=pP1[0:16, :], func=AF.Ln)
                    if which == "k":
                        dv.tensor_scalar_mul(out=VV["lnk"][:, :], in0=VV["s2"][:, :],
                                             scalar1=0.5)
                    else:
                        dv.tensor_scalar(out=VV["s1"][:, :], in0=VV["s2"][:, :],
                                         scalar1=0.5, scalar2=float(0.5*np.log(D)),
                                         op0=ALU.mult, op1=ALU.add)
                        dv.tensor_tensor(out=VV["r"][:, :], in0=VV["g"][:, :],
                                         in1=VV["s1"][:, :], op=ALU.subtract)
            out.append(norms)
            def stacks():
                dv.tensor_tensor(out=ST["p"][0:16, :], in0=g[:, :], in1=lnk[:, :],
                                 op=ALU.subtract)
                dv.tensor_tensor(out=ST["n"][0:16, :], in0=g[:, :], in1=lnk[:, :],
                                 op=ALU.add)
                dv.tensor_tensor(out=ST["m"][0:16, :], in0=ST["n"][0:16, :], in1=lnb[:, :],
                                 op=ALU.subtract)
                base = g[0:16, 63:64]
                ge_view = bass.AP(base.tensor, base.offset, [[512, 16], [64, 8], [0, 64]])
                dv.tensor_tensor(out=ek[:, :].rearrange("p (c_ s_) -> p c_ s_", c_=8),
                                 in0=ge_view,
                                 in1=g[:, :].rearrange("p (c_ s_) -> p c_ s_", c_=8),
                                 op=ALU.subtract)
                dv.tensor_tensor(out=ek[:, :], in0=ek[:, :], in1=lnk[:, :], op=ALU.subtract)
                gp.tensor_copy(vec4[0:16, :], ST["p"][0:16, :])
                gp.tensor_copy(vec4[32:48, :], VV["lnb"][:, :])
                gp.tensor_copy(vec4[64:80, :], VV["r"][:, :])
                gp.tensor_copy(vec4[96:112, :], VV["ek"][:, :])
            out.append(stacks)
            return out

        def prep_stages(sl, it):
            t0 = it * TT
            c0, c1 = 2*sl, 2*sl + 1
            MT = TL["MTa"] if sl % 2 == 0 else TL["MTb"]
            PX = TL["P1"] if sl % 2 == 0 else TL["P2"]
            out = []

            def vproj(sl=sl, t0=t0):
                xfm = xfm2[it % 2]
                for nh in range(2):
                    for kc in range(8):
                        pe.matmul(p1[:, nh*512:(nh+1)*512],
                                  xfm[:, kc*512 + sl*128:kc*512 + (sl+1)*128],
                                  W["wv"][:, kc*1024 + nh*512:kc*1024 + (nh+1)*512],
                                  start=(kc == 0), stop=(kc == 7))
                    pe.flush()
                dv.tensor_copy(vtm2[sl % 2][:, :], p1[:, :])
            out.append(vproj)

            def ktx(sl=sl, t0=t0):
                sy.dma_start(ktm[sl % 2][:, :],
                             kfm_h[:, t0+sl*128:t0+(sl+1)*128], transpose=True)
            out.append(ktx)

            def kk():
                order = ([(h, half) for h in range(0, 16, 2) for half in (0, 1)]
                         + [(h, half) for h in range(1, 16, 2) for half in (1, 0)])
                for h, half in order:
                    b = 64 * (h % 2)
                    j = h // 2
                    cch = c0 if half == 0 else c1
                    pe.matmul(p1[64*half:64*half + 64, h*64:(h+1)*64],
                              kfm[b:b+64, j*512 + cch*64:j*512 + (cch+1)*64],
                              kfm[b:b+64, j*512 + cch*64:j*512 + (cch+1)*64],
                              start=True, stop=True, tile_position=(b, 64*half))
                pe.flush()
            out.append(kk)

            def gbuild(lhs_nm, rbvec, mask_nm, rev, kk_ap, dst):
                for half, cch in [(0, c0), (1, c1)]:
                    if rbvec == "r":
                        srcv = VV["r"][0:16, cch*64:(cch+1)*64]
                    else:
                        srcv = ST[rbvec][0:16, cch*64:(cch+1)*64]
                    vw = bass.AP(srcv.tensor, srcv.offset,
                                 [[512, 16], [0, 16], [1, 64]])
                    gp.tensor_tensor(
                        out=RH[32:48, :].rearrange("p (hh s_) -> p hh s_", hh=16),
                        in0=vw,
                        in1=CN["ehead16"][:, :].rearrange("p (hh s_) -> p hh s_", hh=16),
                        op=ALU.mult)
                    for nh in range(2):
                        pe.matmul(p2[64*half:64*half + 64, nh*512:(nh+1)*512],
                                  ST[lhs_nm][:, cch*64:(cch+1)*64],
                                  RH[:, nh*512:(nh+1)*512],
                                  start=True, stop=True, tile_position=(0, 64*half))
                    pe.flush()
                if rev:
                    dv.tensor_tensor(out=t_Gm[:, :], in0=CN[mask_nm][:, :],
                                     in1=p2[:, :], op=ALU.subtract)
                else:
                    dv.tensor_tensor(out=t_Gm[:, :], in0=p2[:, :],
                                     in1=CN[mask_nm][:, :], op=ALU.add)
                sc.activation(out=TL["Eg"][:, 0:512], in_=t_Gm[:, 0:512], func=AF.Exp)
                sc.activation(out=TL["Eg"][:, 512:1024], in_=t_Gm[:, 512:1024],
                              func=AF.Exp)
                for cl in (slice(0, 512), slice(512, 1024)):
                    if dst is MT:
                        dv.tensor_tensor(out=dst[:, cl], in0=kk_ap[:, cl],
                                         in1=TL["Eg"][:, cl], op=ALU.mult)
                    else:
                        dv.scalar_tensor_tensor(out=dst[:, cl], in0=kk_ap[:, cl],
                                                scalar=-1.0, in1=TL["Eg"][:, cl],
                                                op0=ALU.mult, op1=ALU.mult)

            out.append(lambda: gbuild("p", "m", "maskA", False, p1, TL["NA"]))
            out.append(lambda: gbuild("m", "p", "maskAT", True, p1, TL["Z0"]))

            def qkt():
                order = ([(h, half) for h in range(0, 16, 2) for half in (0, 1)]
                         + [(h, half) for h in range(1, 16, 2) for half in (1, 0)])
                for h, half in order:
                    b = 64 * (h % 2)
                    j = h // 2
                    cch = c0 if half == 0 else c1
                    pe.matmul(p1[64*half:64*half + 64, h*64:(h+1)*64],
                              kfm[b:b+64, j*512 + cch*64:j*512 + (cch+1)*64],
                              qfm[b:b+64, j*512 + cch*64:j*512 + (cch+1)*64],
                              start=True, stop=True, tile_position=(b, 64*half))
                pe.flush()
            out.append(qkt)
            out.append(lambda: gbuild("n", "r", "maskMT", True, p1, MT))

            def p0init():
                gp.tensor_tensor(out=TL["P0"][:, :], in0=TL["Z0"][:, :],
                                 in1=CN["identP"][:, :], op=ALU.add)
            out.append(p0init)

            seqs = []
            Bc, Zc, Pc = TL["NA"], TL["Z0"], TL["P0"]
            Bn, Zn, Pn = TL["B1"], TL["Z1"], PX
            for lev in range(1, 6):
                def fb(Bc=Bc, Zc=Zc, Bn=Bn):
                    for h in range(16):
                        for half in (0, 1):
                            b = 64 * half
                            pe.matmul(p1[b:b+64, h*64:(h+1)*64],
                                      Zc[b:b+64, h*64:(h+1)*64],
                                      Bc[b:b+64, h*64:(h+1)*64],
                                      start=True, stop=True, tile_position=(b, b))
                    pe.flush()
                    dv.tensor_copy(Bn[:, 0:512], p1[:, 0:512])
                    dv.tensor_copy(Bn[:, 512:1024], p1[:, 512:1024])
                out.append(fb)
                if lev < 5:
                    def fz(Bc=Bc, Zc=Zc, Zn=Zn):
                        for h in range(16):
                            for half in (0, 1):
                                b = 64 * half
                                pe.matmul(p1[b:b+64, h*64:(h+1)*64],
                                          Bc[b:b+64, h*64:(h+1)*64],
                                          Zc[b:b+64, h*64:(h+1)*64],
                                          start=True, stop=True, tile_position=(b, b))
                        pe.flush()
                        sc.activation(out=Zn[:, 0:512], in_=p1[:, 0:512],
                                      func=AF.Identity)
                        sc.activation(out=Zn[:, 512:1024], in_=p1[:, 512:1024],
                                      func=AF.Identity)
                    out.append(fz)
                def fp(Bn=Bn, Pc=Pc, Pn=Pn):
                    for h in range(16):
                        for half in (0, 1):
                            b = 64 * half
                            pe.matmul(p2[b:b+64, h*64:(h+1)*64],
                                      Bn[b:b+64, h*64:(h+1)*64],
                                      Pc[b:b+64, h*64:(h+1)*64],
                                      start=True, stop=True, tile_position=(b, b))
                    pe.flush()
                    dv.tensor_tensor(out=Pn[:, 0:512], in0=p2[:, 0:512],
                                     in1=Pc[:, 0:512], op=ALU.add)
                    dv.tensor_tensor(out=Pn[:, 512:1024], in0=p2[:, 512:1024],
                                     in1=Pc[:, 512:1024], op=ALU.add)
                out.append(fp)
                Bc, Bn = Bn, (TL["B2"] if Bn is TL["B1"] else TL["B1"])
                Zc, Zn = Zn, (TL["Z2"] if Zn is TL["Z1"] else TL["Z1"])
                Pc, Pn = Pn, (TL["P0"] if Pn is PX else PX)
            return out, Pc, MT

        def seq_stages(sl, Pfin, MT):
            c0, c1 = 2*sl, 2*sl + 1
            out = []

            def s_tr(c0=c0):
                pe.transpose(pS[:, 0:128], vec4[:, c0*64:c0*64 + 128],
                             CN["identF"][:, :])
                pe.flush()
                sc.activation(out=bc4[:, 0:64], in_=pS[:, 0:64], func=AF.Exp)
                sc.activation(out=bc4[:, 64:128], in_=pS[:, 64:128], func=AF.Exp)
            out.append(s_tr)

            for hf, cch in [(0, c0), (1, c1)]:
                def bce(grp, b0=64*hf):
                    b = bc4[b0:b0+64, grp:grp+16]
                    return bass.AP(b.tensor, b.offset, [[128, 64], [1, 16], [0, 64]])

                b0 = 64 * hf
                hs = slice(b0, b0 + 64)
                css = slice(cch*64, (cch+1)*64)

                def s_ks0(cch=cch, b0=b0, hs=hs, css=css, sl=sl, bce=bce):
                    ob = 64 - b0
                    for h in list(range(0, 16, 2)) + [-1] + list(range(1, 16, 2)):
                        if h < 0:
                            pe.matmul(pS[ob:ob+64, 0:64], kfm[64:128, 0:64],
                                      S0b[64:128, 0:64], start=True, stop=True,
                                      tile_position=(64, ob))
                            continue
                        b = 64 * (h % 2)
                        j = h // 2
                        pe.matmul(pS[b0:b0+64, h*64:(h+1)*64],
                                  kfm[b:b+64, j*512 + cch*64:j*512 + (cch+1)*64],
                                  S0b[b:b+64, j*64:(j+1)*64],
                                  start=True, stop=True, tile_position=(b, b0))
                    pe.flush()
                    dv.tensor_tensor(out=TL["t2"][hs, :].rearrange("p (h s_) -> p h s_", s_=64),
                                     in0=pS[hs, :].rearrange("p (h s_) -> p h s_", s_=64),
                                     in1=bce(0), op=ALU.mult)
                    dv.tensor_tensor(out=TL["R"][hs, :],
                                     in0=vtm2[sl % 2][hs, :],
                                     in1=TL["t2"][hs, :], op=ALU.subtract)
                out.append(s_ks0)

                def s_tapply(b0=b0, hs=hs, css=css, Pfin=Pfin, bce=bce):
                    for h in range(16):
                        pe.matmul(pS[b0:b0+64, h*64:(h+1)*64],
                                  Pfin[b0:b0+64, h*64:(h+1)*64],
                                  TL["R"][b0:b0+64, h*64:(h+1)*64],
                                  start=True, stop=True, tile_position=(b0, b0))
                    pe.flush()
                    for ci in (0, 1):
                        cl = slice(ci*512, (ci+1)*512)
                        bv = bce(32)
                        bv = bass.AP(bv.tensor, bv.offset + ci*8,
                                     [[128, 64], [1, 8], [0, 64]])
                        dv.tensor_tensor(out=TL["U"][hs, cl].rearrange("p (h s_) -> p h s_", s_=64),
                                         in0=pS[hs, cl].rearrange("p (h s_) -> p h s_", s_=64),
                                         in1=bv, op=ALU.mult)
                out.append(s_tapply)

                def s_qs0(cch=cch, b0=b0, hs=hs, css=css, bce=bce):
                    ob = 64 - b0
                    for h in list(range(0, 16, 2)) + [-1] + list(range(1, 16, 2)):
                        if h < 0:
                            pe.matmul(pS[ob:ob+64, 0:64], qfm[64:128, 0:64],
                                      S0b[64:128, 0:64], start=True, stop=True,
                                      tile_position=(64, ob))
                            continue
                        b = 64 * (h % 2)
                        j = h // 2
                        pe.matmul(pS[b0:b0+64, h*64:(h+1)*64],
                                  qfm[b:b+64, j*512 + cch*64:j*512 + (cch+1)*64],
                                  S0b[b:b+64, j*64:(j+1)*64],
                                  start=True, stop=True, tile_position=(b, b0))
                    pe.flush()
                    dv.tensor_tensor(out=TL["t2"][hs, :].rearrange("p (h s_) -> p h s_", s_=64),
                                     in0=pS[hs, :].rearrange("p (h s_) -> p h s_", s_=64),
                                     in1=bce(64), op=ALU.mult)
                out.append(s_qs0)

                def s_oapply(b0=b0, hs=hs, sl=sl, MT=MT):
                    for h in range(16):
                        pe.matmul(pS[b0:b0+64, h*64:(h+1)*64],
                                  MT[b0:b0+64, h*64:(h+1)*64],
                                  TL["U"][b0:b0+64, h*64:(h+1)*64],
                                  start=True, stop=True, tile_position=(b0, b0))
                    pe.flush()
                    dv.tensor_tensor(out=otm[sl % 2][hs, :], in0=pS[hs, :],
                                     in1=TL["t2"][hs, :], op=ALU.add)
                out.append(s_oapply)

                def s_bc_ek(cch=cch, b0=b0, hs=hs, css=css, sl=sl, bce=bce):
                    gp.tensor_tensor(out=TL["Ku"][hs, :].rearrange("p (h s_) -> p h s_", s_=64),
                                     in0=ktm[sl % 2][hs, :].rearrange("p (h s_) -> p h s_", s_=64),
                                     in1=bce(96), op=ALU.mult)
                out.append(s_bc_ek)

                def s_supd(cch=cch, b0=b0, hs=hs, css=css, sl=sl):
                    sc.activation(out=sEg1[:, 0:1],
                                  in_=VV["g"][0:16, cch*64+63:cch*64+64], func=AF.Exp)
                    ebase = sEg1[0:16, 0:1]
                    eview = bass.AP(ebase.tensor, ebase.offset,
                                    [[ebase.ap[0][0], 16], [0, 512]])
                    dv.tensor_tensor(out=sEgr[:, :], in0=CN["epair"][:, :],
                                     in1=eview, op=ALU.mult)
                    pe.matmul(pS[:, 0:512], CN["parity"][:, :], sEgr[:, :],
                              start=True, stop=True)
                    pe.flush()
                    for h in range(16):
                        b = 64 * (h % 2)
                        j = h // 2
                        pe.matmul(pS[b:b+64, 512 + j*64:512 + (j+1)*64],
                                  TL["Ku"][b0:b0+64, h*64:(h+1)*64],
                                  TL["U"][b0:b0+64, h*64:(h+1)*64],
                                  start=True, stop=True, tile_position=(b0, b))
                    pe.flush()
                    dv.tensor_tensor(out=sSd[:, :], in0=S0[:, :], in1=pS[:, 0:512],
                                     op=ALU.mult)
                    dv.tensor_tensor(out=S0[:, :], in0=pS[:, 512:1024], in1=sSd[:, :],
                                     op=ALU.add)
                    sc.activation(out=S0b[:, :], in_=S0[:, :], func=AF.Identity)
                out.append(s_supd)
            return out

        def epi1_stages(sl, it):
            t0 = it * TT
            xfm = xfm2[it % 2]
            om = otm[sl % 2]
            out = []
            def rms(om=om, sl=sl, t0=t0):
                gp.dma_start(xbsl[:, :], x_in[t0+sl*128:t0+(sl+1)*128, :])
                sc.activation(out=escr[:, :], in_=om[:, :], func=AF.Square)
                dv.tensor_reduce(out=colsc[:, 0:16],
                                 in_=escr[:, :].rearrange("p (h s_) -> p h s_", s_=64),
                                 axis=AX.X, op=ALU.add)
                dv.tensor_scalar(out=colsc[:, 16:32], in0=colsc[:, 0:16],
                                 scalar1=1.0/64.0, scalar2=1e-6,
                                 op0=ALU.mult, op1=ALU.add)
                sc.activation(out=colsc[:, 32:48], in_=colsc[:, 16:32], func=AF.Ln)
                sc.activation(out=colsc[:, 48:64], in_=colsc[:, 32:48], func=AF.Exp,
                              scale=-0.5)
                rbase = colsc[:, 48:64]
                rstdv = bass.AP(rbase.tensor, rbase.offset,
                                [[rbase.ap[0][0], 128], [1, 16], [0, 64]])
                dv.tensor_tensor(out=scrb[:, :].rearrange("p (h s_) -> p h s_", s_=64),
                                 in0=om[:, :].rearrange("p (h s_) -> p h s_", s_=64),
                                 in1=rstdv, op=ALU.mult)
            out.append(rms)
            def gates(om=om, sl=sl, t0=t0):
                for nh, bank in [(0, pP1), (1, pP2)]:
                    for kc in range(8):
                        pe.matmul(bank[:, :],
                                  xfm[:, kc*512 + sl*128:kc*512 + (sl+1)*128],
                                  W["wgate"][:, kc*1024 + nh*512:kc*1024 + (nh+1)*512],
                                  start=(kc == 0), stop=(kc == 7))
                    pe.flush()
                sc.activation(out=om[:, 0:512], in_=pP1[:, :], func=AF.Silu)
                sc.activation(out=om[:, 512:1024], in_=pP2[:, :], func=AF.Silu)
                gp.tensor_tensor(out=scrb[:, :], in0=scrb[:, :], in1=om[:, :],
                                 op=ALU.mult)
                sy.dma_start(og_h[t0+sl*128:t0+(sl+1)*128, :], scrb[:, :])
                for ec in range(8):
                    sy.dma_start(om[:, ec*128:(ec+1)*128],
                                 og_h[t0+sl*128:t0+(sl+1)*128, ec*128:(ec+1)*128],
                                 transpose=True)
            out.append(gates)
            return out

        def epi2_stages(sl, it):
            t0 = it * TT
            om = otm[sl % 2]
            out = []
            def wout(om=om, sl=sl, t0=t0):
                for nh, bank in [(0, pP1), (1, pP2)]:
                    for kc in range(8):
                        pe.matmul(bank[:, :],
                                  om[:, kc*128:(kc+1)*128],
                                  W["wout"][:, kc*1024 + nh*512:kc*1024 + (nh+1)*512],
                                  start=(kc == 0), stop=(kc == 7))
                    pe.flush()
                dv.tensor_tensor(out=escr[:, 0:512], in0=pP1[:, :],
                                 in1=xbsl[:, 0:512], op=ALU.add)
                dv.tensor_tensor(out=escr[:, 512:1024], in0=pP2[:, :],
                                 in1=xbsl[:, 512:1024], op=ALU.add)
                sc.activation(out=scrb[:, :], in_=escr[:, :], func=AF.Square,
                              accum_out=colsc[:, 0:1])
                dv.tensor_reduce(out=colsc[:, 1:2], in_=escr[:, :], axis=AX.X,
                                 op=ALU.add)
                dv.tensor_scalar_mul(out=colsc[:, 2:3], in0=colsc[:, 1:2],
                                     scalar1=1.0/1024.0)
                dv.tensor_scalar_mul(out=colsc[:, 3:4], in0=colsc[:, 2:3],
                                     scalar1=colsc[:, 2:3])
                dv.tensor_scalar(out=colsc[:, 4:5], in0=colsc[:, 0:1],
                                 scalar1=1.0/1024.0, scalar2=1e-5,
                                 op0=ALU.mult, op1=ALU.add)
                dv.tensor_tensor(out=colsc[:, 4:5], in0=colsc[:, 4:5],
                                 in1=colsc[:, 3:4], op=ALU.subtract)
                sc.activation(out=colsc[:, 5:6], in_=colsc[:, 4:5], func=AF.Ln)
                sc.activation(out=colsc[:, 6:7], in_=colsc[:, 5:6], func=AF.Exp,
                              scale=-0.5)
                dv.tensor_scalar(out=scrb[:, :], in0=escr[:, :],
                                 scalar1=colsc[:, 2:3], scalar2=colsc[:, 6:7],
                                 op0=ALU.subtract, op1=ALU.mult)
                sy.dma_start(x2_h[t0+sl*128:t0+(sl+1)*128, :], scrb[:, :])
            out.append(wout)
            return out

        def weave(A, B):
            if not A:
                for f in B:
                    f()
                return
            nb, na = len(B), len(A)
            bi = 0
            for i, fa in enumerate(A):
                want = ((i + 1) * nb) // na
                while bi < want:
                    B[bi]()
                    bi += 1
                fa()
            while bi < nb:
                B[bi]()
                bi += 1

        prep_next = None
        for it in range(_N_TILES):
            for f in t0a_stages(it, skip_load=(it > 0)):
                f()
            if it == 0:
                for nm in ["wv", "wgate", "wout"]:
                    gp.dma_start(W[nm][:, :].rearrange("p (kc n) -> p kc n", n=1024),
                                 w_in[nm].rearrange("(kc p) n -> p kc n", p=128))
                gp.dma_start(wa_s[:, :].rearrange("p (kc c_) -> p kc c_", c_=16),
                             wa_in.rearrange("(kc p) n -> p kc n", p=128))
                gp.dma_start(wb_s[:, :].rearrange("p (kc c_) -> p kc c_", c_=16),
                             wb_in.rearrange("(kc p) n -> p kc n", p=128))
                for kc in range(8):
                    dv.tensor_copy(scrb[:, :], W["wout"][:, kc*1024:(kc+1)*1024])
                    dv.tensor_scalar_mul(out=W["wout"][:, kc*1024:(kc+1)*1024],
                                         in0=scrb[:, :], scalar1=rmswc[:, :])
            if it > 0:
                for f in epi2_stages(3, it - 1):
                    f()
            stg, Pfin, MT = prep_stages(0, it)
            weave(t0b_stages(it), stg[0:3])
            for f in stg[3:]:
                f()
            for sl in range(NSL):
                QB = []
                if sl >= 1:
                    QB += epi1_stages(sl - 1, it)
                if sl < 3:
                    stg2, Pfin2, MT2 = prep_stages(sl + 1, it)
                    QB += stg2
                elif it + 1 < _N_TILES:
                    QB.append(xfm_load(it + 1))
                weave(seq_stages(sl, Pfin, MT), QB)
                if sl >= 1:
                    for f in epi2_stages(sl - 1, it):
                        f()
                if sl < 3:
                    Pfin, MT = Pfin2, MT2
            for f in epi1_stages(3, it):
                f()
            if it == _N_TILES - 1:
                for f in epi2_stages(3, it):
                    f()

        dve_A, act_A, pool_A, pe_A = cnt["dve"], cnt["act"], cnt["pool"], cnt["pe"]
        ldA_A = {s: cnt[s] for s in QSEMS["sy"]}
        ldB_A = {s: cnt[s] for s in QSEMS["gpL"]}
        wait("pe", "dve", dve_A)
        wait("pe", "act", act_A)
        inc(op("pe", "nop"), "bar", 1)
        wait("dv", "pe", pe_A)
        inc(op("dv", "engine_nop"), "bar", 1)
        wait("sc", "dve", dve_A)
        inc(op("sc", "nop"), "bar", 1)
        for s, v in ldB_A.items():
            wait("gp", s, v)
        wait("gp", "pool", pool_A)
        inc(op("gp", "engine_nop"), "bar", 1)
        for s, v in ldA_A.items():
            wait("sy", s, v)
        inc(op("sy", "nop"), "bar", 1)
        BAR = cnt["bar"]

        emit_all()
        stA.close()

        stB = contextlib.ExitStack()
        sbB = lambda n, s_, d: stB.enter_context(nc.sbuf_tensor(n, s_, d))
        wgT = sbB("wgT", [128, 8*3072], BF16)
        wuT = sbB("wuT", [128, 8*3072], BF16)
        wdT = sbB("wdT", [128, 24*1024], BF16)
        x2fm = sbB("x2fm", [128, 8*512], BF16)
        x2tm = sbB("x2tm", [128, 4*1024], BF16)
        h_s = sbB("h_s", [128, 24*512], BF16)
        b_scr1 = sbB("b_scr1", [128, 1024], F32)
        b_scr2 = sbB("b_scr2", [128, 1024], F32)
        b_scrb = sbB("b_scrb", [128, 512], BF16)
        b_col = sbB("b_col", [128, 16], F32)
        bpA = stB.enter_context(nc.psum_tensor("bpA", [128, 512], F32))
        bpB = stB.enter_context(nc.psum_tensor("bpB", [128, 512], F32))
        bpC = stB.enter_context(nc.psum_tensor("bpC", [128, 512], F32))
        bpD = stB.enter_context(nc.psum_tensor("bpD", [128, 512], F32))

        if _PHASEB_ON:
            wait("gp", "bar", BAR)
            for ec in range(8):
                gp.dma_start(wgT[:, ec*3072:(ec+1)*3072],
                             fgT_in[ec*128:(ec+1)*128, :])
            for ec in range(8):
                gp.dma_start(wuT[:, ec*3072:(ec+1)*3072],
                             fuT_in[ec*128:(ec+1)*128, :])

        for it in range(NT if _PHASEB_ON else 0):
            t0 = it * TT
            wait("sy", "bar", BAR)
            for ec in range(8):
                sy.dma_start(x2fm[:, ec*512:(ec+1)*512],
                             x2_h[t0:t0+TT, ec*128:(ec+1)*128], transpose=True)
            gp.dma_start(x2tm[:, :].rearrange("p (sl e) -> p sl e", e=1024),
                         x2_h[t0:t0+TT, :].rearrange("(sl p) e -> p sl e", p=128))
            if it == 0:
                for hc in range(24):
                    gp.dma_start(wdT[:, hc*1024:(hc+1)*1024],
                                 fdT_in[hc*128:(hc+1)*128, :])

            for hc in range(24):
                for ec in range(8):
                    pe.matmul(bpA[:, :], wgT[:, ec*3072 + hc*128:ec*3072 + (hc+1)*128],
                              x2fm[:, ec*512:(ec+1)*512], start=(ec == 0), stop=(ec == 7))
                pe.flush()
                for ec in range(8):
                    pe.matmul(bpB[:, :], wuT[:, ec*3072 + hc*128:ec*3072 + (hc+1)*128],
                              x2fm[:, ec*512:(ec+1)*512], start=(ec == 0), stop=(ec == 7))
                pe.flush()
                sc.activation(out=b_scrb[:, :], in_=bpA[:, :], func=AF.Silu, scale=RQ)
                dv.scalar_tensor_tensor(out=h_s[:, hc*512:(hc+1)*512], in0=bpB[:, :],
                                        scalar=RQ*RQ, in1=b_scrb[:, :],
                                        op0=ALU.mult, op1=ALU.mult)

            for sl in range(NSL):
                for nh, bank in [(0, bpC), (1, bpD)]:
                    for hc in range(24):
                        pe.matmul(bank[:, :],
                                  h_s[:, hc*512 + sl*128:hc*512 + (sl+1)*128],
                                  wdT[:, hc*1024 + nh*512:hc*1024 + (nh+1)*512],
                                  start=(hc == 0), stop=(hc == 23))
                    pe.flush()
                dv.tensor_tensor(out=b_scr1[:, 0:512], in0=bpC[:, :],
                                 in1=x2tm[:, sl*1024:sl*1024 + 512], op=ALU.add)
                dv.tensor_tensor(out=b_scr1[:, 512:1024], in0=bpD[:, :],
                                 in1=x2tm[:, sl*1024 + 512:(sl+1)*1024], op=ALU.add)
                sc.activation(out=b_scr2[:, :], in_=b_scr1[:, :], func=AF.Square,
                              accum_out=b_col[:, 0:1])
                dv.tensor_reduce(out=b_col[:, 1:2], in_=b_scr1[:, :], axis=AX.X,
                                 op=ALU.add)
                dv.tensor_scalar_mul(out=b_col[:, 2:3], in0=b_col[:, 1:2],
                                     scalar1=1.0/1024.0)
                dv.tensor_scalar_mul(out=b_col[:, 3:4], in0=b_col[:, 2:3],
                                     scalar1=b_col[:, 2:3])
                dv.tensor_scalar(out=b_col[:, 4:5], in0=b_col[:, 0:1],
                                 scalar1=1.0/1024.0, scalar2=1e-5,
                                 op0=ALU.mult, op1=ALU.add)
                dv.tensor_tensor(out=b_col[:, 4:5], in0=b_col[:, 4:5],
                                 in1=b_col[:, 3:4], op=ALU.subtract)
                sc.activation(out=b_col[:, 5:6], in_=b_col[:, 4:5], func=AF.Ln)
                sc.activation(out=b_col[:, 6:7], in_=b_col[:, 5:6], func=AF.Exp,
                              scale=-0.5)
                dv.tensor_scalar(out=b_scr2[:, :], in0=b_scr1[:, :],
                                 scalar1=b_col[:, 2:3], scalar2=b_col[:, 6:7],
                                 op0=ALU.subtract, op1=ALU.mult)
                DM_("gp", "gpS").dma_start(out_t[t0+sl*128:t0+(sl+1)*128, :],
                                           b_scr2[:, :])

        for s in QSEMS["gpS"]:
            wait("gp", s, cnt[s])

        emit_all()
        stB.close()

    return nc


def _shard_inputs(inputs):
    cns = host_consts()
    rmsw = np.asarray(inputs["L0_attn_rms_w"])
    base = {
        "wq": np.ascontiguousarray(inputs["L0_attn_Wq"]),
        "wk": np.ascontiguousarray(inputs["L0_attn_Wk"]),
        "wv": np.ascontiguousarray(inputs["L0_attn_Wv"]),
        "wgate": np.ascontiguousarray(inputs["L0_attn_Wgate"]),
        "wout": np.ascontiguousarray(inputs["L0_attn_Wout"]),
        "wa": np.ascontiguousarray(inputs["L0_attn_Walpha"]),
        "wb": np.ascontiguousarray(inputs["L0_attn_Wbeta"]),
        "alog": np.ascontiguousarray(np.asarray(inputs["L0_attn_A_log"]).reshape(16, 1)),
        "dtb": np.ascontiguousarray(np.asarray(inputs["L0_attn_dt_bias"]).reshape(16, 1)),
        "rmsw_c": np.ascontiguousarray(np.tile(rmsw, 2).reshape(128, 1).astype(np.float32)),
        "fgT": np.ascontiguousarray(np.asarray(inputs["L0_ffn_w_gate"]).T),
        "fuT": np.ascontiguousarray(np.asarray(inputs["L0_ffn_w_up"]).T),
        "fdT": np.ascontiguousarray(np.asarray(inputs["L0_ffn_w_down"]).T),
    }
    for k, v in cns.items():
        base[k] = v
    x = np.asarray(inputs["L0_attn_x"])
    in_maps = []
    for b in range(8):
        m = dict(base)
        m["x"] = np.ascontiguousarray(x[b])
        m["xT"] = np.ascontiguousarray(x[b].T)
        in_maps.append(m)
    return in_maps


def kernel(**inputs):
    in_maps = _shard_inputs(inputs)
    nc = build_program()
    res = run_bass_kernel_spmd(nc, in_maps, core_ids=list(range(8)))
    return np.stack([r["out"] for r in res.results], axis=0)


# version 18
# speedup vs baseline: 1.0176x; 1.0176x over previous
import contextlib
import numpy as np
import ml_dtypes
import concourse.bass as bass
import concourse.mybir as mybir
from concourse.bass_utils import run_bass_kernel_spmd

F32 = mybir.dt.float32
BF16 = mybir.dt.bfloat16
I32 = mybir.dt.int32
AF = mybir.ActivationFunctionType
ALU = mybir.AluOpType
AX = mybir.AxisListType
bfdt = ml_dtypes.bfloat16

S, E, H, D, C = 4096, 1024, 16, 64, 64
HID = 3072
TT = 512
NT = S // TT
NSL = TT // 128
NCH = TT // C
NEG = -1.0e30
RQ = 1.0 / 127.0

_DEBUG_TAPS = None
_N_TILES = NT
_CHUNK_ON = True
_CHUNK_SUB = 9
_SEQ_ON = True
_EPI_ON = True
_PHASEB_ON = True


def host_consts():
    c = {}
    eh = np.zeros((16, 1024), np.float32)
    for h in range(16):
        eh[h, h*64:(h+1)*64] = 1.0
    c["ehead16"] = eh.astype(bfdt)
    lt = np.zeros((48, 512), np.float32)
    lt[32:48] = -1.0
    c["lhs_tmpl"] = lt
    rt = np.zeros((48, 1024), np.float32)
    rt[0:16] = eh
    c["rhs_tmpl"] = rt
    t_idx = (np.arange(128) % 64)[:, None].astype(np.float32)
    s_idx = (np.arange(1024) % 64)[None, :].astype(np.float32)
    c["maskA"] = np.where(s_idx < t_idx, 0.0, NEG).astype(bfdt)
    c["maskAT"] = np.where(t_idx < s_idx, 0.0, NEG).astype(bfdt)
    c["maskMT"] = np.where(t_idx <= s_idx, 0.0, NEG).astype(bfdt)
    c["identP"] = np.where(s_idx == t_idx, 1.0, 0.0).astype(bfdt)
    sm = np.ones((16, 512), np.float32)
    sm[:, ::C] = 0.0
    c["scanmask"] = sm.astype(bfdt)
    ehm = np.zeros((128, 128), np.float32)
    for mc in range(8):
        ehm[0:64, mc*16 + 2*mc] = 1.0
        ehm[64:128, mc*16 + 2*mc + 1] = 1.0
    c["ehm"] = ehm
    c["identF"] = np.eye(128, dtype=np.float32)
    par = np.zeros((16, 128), np.float32)
    par[0::2, 0:64] = 1.0
    par[1::2, 64:128] = 1.0
    c["parity"] = par
    ep = np.zeros((16, 512), np.float32)
    for hp in range(16):
        ep[hp, (hp//2)*64:(hp//2+1)*64] = 1.0
    c["epair"] = ep.astype(bfdt)
    return c


_POS_NAMES = {
    "tensor_copy": ("out", "in_"),
    "tensor_tensor": ("out", "in0", "in1"),
    "tensor_scalar": ("out", "in0", "scalar1", "scalar2"),
    "tensor_scalar_mul": ("out", "in0", "scalar1"),
    "tensor_scalar_add": ("out", "in0", "scalar1"),
    "tensor_scalar_sub": ("out", "in0", "scalar1"),
    "tensor_scalar_min": ("out", "in0", "scalar1"),
    "tensor_scalar_max": ("out", "in0", "scalar1"),
    "scalar_tensor_tensor": ("out", "in0", "scalar", "in1"),
    "tensor_reduce": ("out", "in_"),
    "reciprocal": ("out", "in_"),
    "memset": ("out", "constant"),
    "tensor_tensor_scan": ("out", "data0", "data1", "initial"),
    "activation": ("out", "in_", "func", "bias", "scale", "alpha", "accum_out"),
    "copy": ("out", "in_"),
}
_WRITE_KEYS = {"out", "accum_out", "dest", "ap"}
_OPNAMES = set(_POS_NAMES)


def build_program():
    nc = bass.Bass("TRN2", target_bir_lowering=False)

    x_in = nc.dram_tensor("x", [S, E], F32, kind="ExternalInput").ap()
    xT_in = nc.dram_tensor("xT", [E, S], F32, kind="ExternalInput").ap()
    w_in = {}
    for nm in ["wq", "wk", "wv", "wgate", "wout"]:
        w_in[nm] = nc.dram_tensor(nm, [E, E], F32, kind="ExternalInput").ap()
    wa_in = nc.dram_tensor("wa", [E, 16], F32, kind="ExternalInput").ap()
    wb_in = nc.dram_tensor("wb", [E, 16], F32, kind="ExternalInput").ap()
    alog_in = nc.dram_tensor("alog", [16, 1], F32, kind="ExternalInput").ap()
    dtb_in = nc.dram_tensor("dtb", [16, 1], F32, kind="ExternalInput").ap()
    rmsw_in = nc.dram_tensor("rmsw_c", [128, 1], F32, kind="ExternalInput").ap()
    fgT_in = nc.dram_tensor("fgT", [E, HID], I32, kind="ExternalInput").ap()
    fuT_in = nc.dram_tensor("fuT", [E, HID], I32, kind="ExternalInput").ap()
    fdT_in = nc.dram_tensor("fdT", [HID, E], I32, kind="ExternalInput").ap()
    cns_in = {}
    for nm, shp, dt in [("ehead16", [16, 1024], BF16), ("lhs_tmpl", [48, 512], F32),
                        ("rhs_tmpl", [48, 1024], F32), ("maskA", [128, 1024], BF16),
                        ("maskAT", [128, 1024], BF16), ("maskMT", [128, 1024], BF16),
                        ("identP", [128, 1024], BF16), ("scanmask", [16, 512], BF16),
                        ("ehm", [128, 128], F32), ("parity", [16, 128], F32),
                        ("epair", [16, 512], BF16), ("identF", [128, 128], F32)]:
        cns_in[nm] = nc.dram_tensor(nm, shp, dt, kind="ExternalInput").ap()
    out_t = nc.dram_tensor("out", [S, E], F32, kind="ExternalOutput").ap()

    kfm_h = nc.dram_tensor("kfm_h", [E, S], BF16, kind="Internal").ap()
    og_h = nc.dram_tensor("og_h", [S, E], BF16, kind="Internal").ap()
    x2_h = nc.dram_tensor("x2_h", [S, E], BF16, kind="Internal").ap()

    sems, cnt, wm = {}, {}, {}
    WRT, RDS = {}, {}
    SCRIPT = {k: [] for k in ("sy", "gp", "pe", "dv", "sc")}

    def op(ename, method, *args, **kwargs):
        item = ["op", method, args, kwargs, []]
        SCRIPT[ename].append(item)
        return item

    def inc(item, name, step=1):
        item[4].append((name, step))
        cnt[name] += step

    def wait(ename, name, val):
        if val <= 0:
            return
        if wm.get((ename, name), -1) >= val:
            return
        SCRIPT[ename].append(("w", name, val))
        wm[(ename, name)] = val

    _REGIONAL = {"t_U", "t_NA", "t_Z0", "t_B1", "t_Z1", "t_P0", "t_P1", "t_P2",
                 "t_MTa", "t_MTb", "t_R", "kfm", "qfm", "bc4"}

    def regkeys(ap):
        name = ap.tensor.name
        if name not in _REGIONAL:
            return (name,)
        dims = ap.ap
        W = dims[0][0]
        pb = ap.offset // W
        pc = dims[0][1]
        cb = ap.offset % W
        ext = 1
        for st_, sz_ in dims[1:]:
            ext += (sz_ - 1) * abs(st_)
        phs = set()
        if pb < 64:
            phs.add(0)
        if pb + pc > 64:
            phs.add(1)
        chs = set()
        half = W // 2
        if cb < half:
            chs.add(0)
        if cb + ext > half:
            chs.add(1)
        return tuple(f"{name}/{p}{c}" for p in phs for c in chs)

    def dep_wait(ename, wnames, rnames):
        for n in rnames | wnames:
            t = WRT.get(n)
            if t:
                wait(ename, t[0], t[1])
        for n in wnames:
            for s_, v_ in RDS.get(n, {}).items():
                wait(ename, s_, v_)

    def dep_record(sem, val, wnames, rnames):
        for n in wnames:
            WRT[n] = (sem, val)
            RDS[n] = {}
        for n in rnames:
            d = RDS.setdefault(n, {})
            if d.get(sem, -1) < val:
                d[sem] = val

    def classify(opname, args, kwargs):
        wnames, rnames = set(), set()
        names = _POS_NAMES.get(opname, ())
        items = list(zip(names, args)) + list(kwargs.items())
        for k, v in items:
            if isinstance(v, bass.AP):
                (wnames if k in _WRITE_KEYS else rnames).update(regkeys(v))
        return wnames, rnames

    _pe_win = []
    _pe_gid = [0]
    _pe_idx = [0]

    class P_:
        def __init__(self, ename, sem):
            self._ename = ename
            self._sem = sem

        def __getattr__(self, name):
            if name in _OPNAMES:
                def wrapped(*a, **k):
                    wn, rn = classify(name, a, k)
                    dep_wait(self._ename, wn, rn)
                    item = op(self._ename, name, *a, **k)
                    inc(item, self._sem, 1)
                    dep_record(self._sem, cnt[self._sem], wn, rn)
                    return item
                return wrapped
            raise AttributeError(name)

    class PE_:
        def __init__(self):
            self._pw, self._pr = set(), set()
            self._last = None

        def matmul(self, out, lhsT, rhs, **k):
            wn = set(regkeys(out))
            rn = set(regkeys(lhsT)) | set(regkeys(rhs))
            dep_wait("pe", wn, rn)
            oF = out.ap[0][0]
            pb, pc = out.offset // oF, out.ap[0][1]
            cb = out.offset % oF
            ncol = 1
            for _, sz in out.ap[1:]:
                ncol *= sz
            lF = lhsT.ap[0][0]
            rb, rc = lhsT.offset // lF, lhsT.ap[0][1]
            banks = set(range(cb // 512, (cb + ncol - 1) // 512 + 1))
            gid = _pe_gid[0]
            idx = _pe_idx[0]
            _pe_idx[0] += 1
            keep, drain = [], False
            for X in _pe_win:
                xt, xp0, xp1, xr0, xr1, xb, xg, xi = X
                if xi < idx - 4:
                    continue
                if not (xr1 <= rb or rb + rc <= xr0):
                    continue
                if (xt == out.tensor.name and not (xp1 <= pb or pb + pc <= xp0)
                        and (xb & banks)):
                    if xg != gid:
                        drain = True
                    else:
                        raise RuntimeError(
                            f"intra-group PE hazard on {xt} parts {pb}:{pb+pc}")
                keep.append(X)
            if drain:
                wait("pe", "pe", cnt["pe"])
                keep = []
            del _pe_win[:]
            _pe_win.extend(keep)
            _pe_win.append((out.tensor.name, pb, pb + pc, rb, rb + rc, banks, gid, idx))
            item = op("pe", "matmul", out, lhsT, rhs, **k)
            self._last = item
            self._pw |= wn
            self._pr |= rn
            return item

        def transpose(self, out, in_, identity, **k):
            wn = set(regkeys(out))
            rn = set(regkeys(in_)) | set(regkeys(identity))
            dep_wait("pe", wn, rn)
            oF = out.ap[0][0]
            pb, pc = out.offset // oF, out.ap[0][1]
            cb = out.offset % oF
            ncol = 1
            for _, sz in out.ap[1:]:
                ncol *= sz
            banks = set(range(cb // 512, (cb + ncol - 1) // 512 + 1))
            item = op("pe", "transpose", out, in_, identity, **k)
            self._last = item
            del _pe_win[:]
            _pe_win.append((out.tensor.name, pb, pb + pc, 0, 128, banks,
                            _pe_gid[0], _pe_idx[0]))
            _pe_idx[0] += 1
            self._pw |= wn
            self._pr |= rn
            return item

        def flush(self):
            if self._last is None:
                return
            inc(self._last, "pe", 1)
            dep_record("pe", cnt["pe"], self._pw, self._pr)
            self._pw, self._pr = set(), set()
            self._last = None
            _pe_gid[0] += 1

    QSEMS = {"sy": ["ldA0", "ldA1", "ldA2", "ldA3"],
             "gpL": ["ldB0", "ldB1", "ldB2"],
             "gpS": ["so0", "so1"]}
    qidx = {k: 0 for k in QSEMS}

    class DM_:
        def __init__(self, ename, qkey):
            self._ename = ename
            self._qkey = qkey

        def dma_start(self, out, in_, **k):
            grp = QSEMS[self._qkey]
            sem = grp[qidx[self._qkey] % len(grp)]
            qidx[self._qkey] += 1
            if cnt[sem] > 0:
                wait(self._ename, sem, cnt[sem])
            wn, rn = set(regkeys(out)), set(regkeys(in_))
            dep_wait(self._ename, wn, rn)
            item = op(self._ename, "dma_start", out=out, in_=in_, **k)
            inc(item, sem, 16)
            dep_record(sem, cnt[sem], wn, rn)
            return item

    class DMGP:
        def __init__(self):
            self._dma = DM_("gp", "gpL")
            self._cmp = P_("gp", "pool")

        def dma_start(self, out, in_, **k):
            return self._dma.dma_start(out, in_, **k)

        def __getattr__(self, name):
            if name in _OPNAMES:
                return getattr(self._cmp, name)
            raise AttributeError(name)

    with contextlib.ExitStack() as st:
        for s in ["ldA0", "ldA1", "ldA2", "ldA3", "ldB0", "ldB1", "ldB2",
                  "so0", "so1", "pe", "dve", "act", "pool", "bar"]:
            sems[s] = st.enter_context(nc.semaphore(s))
            cnt[s] = 0

        blk = st.enter_context(nc.Block())

        sy, gp = DM_("sy", "sy"), DMGP()
        pe, dv, sc = PE_(), P_("dv", "dve"), P_("sc", "act")

        def replay(eng, script):
            for it_ in script:
                if it_[0] == "w":
                    eng.wait_ge(sems[it_[1]], it_[2])
                else:
                    _, name, args, kwargs, incs = it_
                    instr = getattr(eng, name)(*args, **kwargs)
                    for s_, step in incs:
                        instr.then_inc(sems[s_], step)

        def emit_all():
            blk.sync(lambda e: replay(e, SCRIPT["sy"]))
            blk.gpsimd(lambda e: replay(e, SCRIPT["gp"]))
            blk.tensor(lambda e: replay(e, SCRIPT["pe"]))
            blk.vector(lambda e: replay(e, SCRIPT["dv"]))
            blk.scalar(lambda e: replay(e, SCRIPT["sc"]))
            for k in SCRIPT:
                SCRIPT[k] = []

        def tap(name, sb_ap, shape, dtype=F32):
            if not _DEBUG_TAPS or name not in _DEBUG_TAPS:
                return
            dr = nc.dram_tensor("tap_" + name, shape, dtype, kind="ExternalOutput").ap()
            DM_("gp", "gpS").dma_start(dr, sb_ap)

        stA = contextlib.ExitStack()
        sbA = lambda n, s_, d: stA.enter_context(nc.sbuf_tensor(n, s_, d))
        W = {}
        for nm in ["wq", "wk", "wv", "wgate", "wout"]:
            W[nm] = sbA(nm + "_s", [128, 8*1024], BF16)
        wa_s = sbA("wa_s", [128, 8*16], BF16)
        wb_s = sbA("wb_s", [128, 8*16], BF16)
        CN = {}
        for nm, shp, dt in [("ehead16", [16, 1024], BF16),
                            ("maskA", [128, 1024], BF16), ("maskAT", [128, 1024], BF16),
                            ("maskMT", [128, 1024], BF16), ("identP", [128, 1024], BF16),
                            ("scanmask", [16, 512], BF16), ("ehm", [128, 128], F32),
                            ("parity", [16, 128], F32), ("epair", [16, 512], BF16),
                            ("identF", [128, 128], F32)]:
            CN[nm] = sbA("c_" + nm, shp, dt)
        ST = {}
        for nm in ["p", "m", "n"]:
            ST[nm] = sbA("st_" + nm, [48, 512], F32)
        RH = sbA("rh", [48, 1024], F32)
        rmswc = sbA("rmswc", [128, 1], F32)
        SM = {}
        for nm in ["alog", "dtb", "negea"]:
            SM[nm] = sbA("s_" + nm, [16, 1], F32)
        S0 = sbA("S0", [128, 512], F32)
        S0b = sbA("S0b", [128, 512], BF16)
        xbsl = sbA("xbsl", [128, 1024], BF16)
        xfm2 = [sbA("xfmA", [128, 8*512], BF16), sbA("xfmB", [128, 8*512], BF16)]
        qfm = sbA("qfm", [128, 8*512], BF16)
        kfm = sbA("kfm", [128, 8*512], BF16)
        ktm = [sbA("ktmA", [128, 1024], BF16), sbA("ktmB", [128, 1024], BF16)]
        vtm2 = [sbA("vtmA", [128, 1024], BF16), sbA("vtmB", [128, 1024], BF16)]
        otm = [sbA("otmA", [128, 1024], BF16), sbA("otmB", [128, 1024], BF16)]
        scr1 = sbA("scr1", [128, 1024], F32)
        escr = sbA("escr", [128, 1024], F32)
        scrb = sbA("scrb", [128, 1024], BF16)
        colsc = sbA("colsc", [128, 64], F32)
        sSd = sbA("sSd", [128, 512], F32)
        vec4 = sbA("vec4", [128, 512], F32)
        bc4 = sbA("bc4", [128, 128], F32)
        sEg1 = sbA("sEg1", [16, 1], F32)
        VV = {}
        for nm in ["g", "lnk", "lnb", "r", "ek", "s1", "s2"]:
            VV[nm] = sbA("v_" + nm, [16, 512], F32)
        sEgr = VV["s2"]
        TL = {}
        for nm in ["NA", "Z0", "B1", "Z1", "P0", "P1", "P2",
                   "MTa", "MTb", "E", "R", "U"]:
            TL[nm] = sbA("t_" + nm, [128, 1024], BF16)
        TL["B2"] = TL["NA"]
        TL["Z2"] = TL["Z0"]
        TL["t2"] = TL["E"]
        TL["Ku"] = TL["R"]
        TL["Eg"] = TL["B1"]
        t_Gm = scr1

        p1 = stA.enter_context(nc.psum_tensor("p1", [128, 1024], F32))
        p2 = stA.enter_context(nc.psum_tensor("p2", [128, 1024], F32))
        pS = stA.enter_context(nc.psum_tensor("pS", [128, 1024], F32))
        pP1 = stA.enter_context(nc.psum_tensor("pP1", [128, 512], F32))
        pP2 = stA.enter_context(nc.psum_tensor("pP2", [128, 512], F32))

        for nm in ["wq", "wk"]:
            gp.dma_start(W[nm][:, :].rearrange("p (kc n) -> p kc n", n=1024),
                         w_in[nm].rearrange("(kc p) n -> p kc n", p=128))
        for nm in ["ehead16", "maskA", "maskAT", "maskMT", "identP", "scanmask",
                   "ehm", "parity", "epair", "identF"]:
            sy.dma_start(CN[nm][:, :], cns_in[nm])
        for nm in ["p", "m", "n"]:
            sy.dma_start(ST[nm][:, :], cns_in["lhs_tmpl"])
        sy.dma_start(RH[:, :], cns_in["rhs_tmpl"])
        sy.dma_start(rmswc[:, :], rmsw_in)
        sy.dma_start(SM["alog"][:, :], alog_in)
        sy.dma_start(SM["dtb"][:, :], dtb_in)

        sc.activation(out=SM["negea"][:, :], in_=SM["alog"][:, :], func=AF.Exp)
        dv.tensor_scalar_mul(out=SM["negea"][:, :], in0=SM["negea"][:, :], scalar1=-1.0)
        dv.memset(S0[:, :], 0.0)
        dv.memset(S0b[:, :], 0.0)
        dv.memset(vec4[:, :], 0.0)

        def xfm_load(it):
            t0 = it * TT
            xf = xfm2[it % 2]
            def ld(xf=xf, t0=t0):
                for ec in range(8):
                    gp.dma_start(xf[:, ec*512:(ec+1)*512],
                                 xT_in[ec*128:(ec+1)*128, t0:t0+TT])
            return ld

        def t0a_stages(it, skip_load=False):
            t0 = it * TT
            xfm = xfm2[it % 2]
            out = []
            if not skip_load:
                out.append(xfm_load(it))
            for gi, (wsrc, dstf) in enumerate([(W["wq"], qfm), (W["wk"], kfm)]):
                for mc in range(8):
                    if gi == 0 and mc < 2:
                        bank = p1[:, 0:512] if mc == 0 else p1[:, 512:1024]
                    else:
                        bank = pP1[:, :] if mc % 2 == 0 else pP2[:, :]
                    def f(wsrc=wsrc, mc=mc, bank=bank, dstf=dstf):
                        for kc in range(8):
                            pe.matmul(bank,
                                      wsrc[:, kc*1024 + mc*128:kc*1024 + (mc+1)*128],
                                      xfm[:, kc*512:(kc+1)*512],
                                      start=(kc == 0), stop=(kc == 7))
                        pe.flush()
                        dv.tensor_copy(dstf[:, mc*512:(mc+1)*512], bank)
                    out.append(f)
            def kx():
                sy.dma_start(kfm_h[:, t0:t0+TT].rearrange("(mc p) t -> p mc t", p=128),
                             kfm[:, :].rearrange("p (mc t) -> p mc t", t=512))
            out.append(kx)
            return out

        def t0b_stages(it):
            t0 = it * TT
            xfm = xfm2[it % 2]
            out = []
            g, lnb = VV["g"], VV["lnb"]
            lnk, r, ek = VV["lnk"], VV["r"], VV["ek"]
            s1, s2 = VV["s1"], VV["s2"]
            def dt_chain():
                for kc in range(8):
                    pe.matmul(pP1[0:16, :], wa_s[:, kc*16:(kc+1)*16],
                              xfm[:, kc*512:(kc+1)*512], start=(kc == 0), stop=(kc == 7))
                pe.flush()
                dv.tensor_copy(s1[:, :], pP1[0:16, :])
                sc.activation(out=s2[:, :], in_=s1[:, :], func=AF.Abs, bias=SM["dtb"][:, :])
                sc.activation(out=ek[:, :], in_=s2[:, :], func=AF.Exp, scale=-1.0)
                sc.activation(out=s2[:, :], in_=ek[:, :], func=AF.Ln, bias=1.0)
                sc.activation(out=ek[:, :], in_=s1[:, :], func=AF.Relu, bias=SM["dtb"][:, :])
                dv.tensor_tensor(out=s2[:, :], in0=s2[:, :], in1=ek[:, :], op=ALU.add)
                dv.tensor_scalar_mul(out=s1[:, :], in0=s2[:, :], scalar1=SM["negea"][:, :])
                dv.tensor_tensor_scan(out=g[:, :], data0=CN["scanmask"][:, :],
                                      data1=s1[:, :], initial=0.0,
                                      op0=ALU.mult, op1=ALU.add)
            out.append(dt_chain)
            def beta_chain():
                for kc in range(8):
                    pe.matmul(pP2[0:16, :], wb_s[:, kc*16:(kc+1)*16],
                              xfm[:, kc*512:(kc+1)*512], start=(kc == 0), stop=(kc == 7))
                pe.flush()
                dv.tensor_copy(s1[:, :], pP2[0:16, :])
                sc.activation(out=s2[:, :], in_=s1[:, :], func=AF.Abs)
                sc.activation(out=r[:, :], in_=s2[:, :], func=AF.Exp, scale=-1.0)
                sc.activation(out=s2[:, :], in_=r[:, :], func=AF.Ln, bias=1.0)
                dv.tensor_scalar_min(out=r[:, :], in0=s1[:, :], scalar1=0.0)
                dv.tensor_tensor(out=lnb[:, :], in0=r[:, :], in1=s2[:, :], op=ALU.subtract)
            out.append(beta_chain)

            def norms():
                for which, srcf in [("k", kfm), ("q", qfm)]:
                    for mc in range(8):
                        sq = scr1[:, 0:512] if mc % 2 == 0 else scr1[:, 512:1024]
                        sc.activation(out=sq, in_=srcf[:, mc*512:(mc+1)*512],
                                      func=AF.Square)
                        pe.matmul(pP1[0:16, :], CN["ehm"][:, mc*16:(mc+1)*16], sq,
                                  start=(mc == 0), stop=(mc == 7))
                        pe.flush()
                    sc.activation(out=VV["s2"][:, :], in_=pP1[0:16, :], func=AF.Ln)
                    if which == "k":
                        dv.tensor_scalar_mul(out=VV["lnk"][:, :], in0=VV["s2"][:, :],
                                             scalar1=0.5)
                    else:
                        dv.tensor_scalar(out=VV["s1"][:, :], in0=VV["s2"][:, :],
                                         scalar1=0.5, scalar2=float(0.5*np.log(D)),
                                         op0=ALU.mult, op1=ALU.add)
                        dv.tensor_tensor(out=VV["r"][:, :], in0=VV["g"][:, :],
                                         in1=VV["s1"][:, :], op=ALU.subtract)
            out.append(norms)
            def stacks():
                dv.tensor_tensor(out=ST["p"][0:16, :], in0=g[:, :], in1=lnk[:, :],
                                 op=ALU.subtract)
                dv.tensor_tensor(out=ST["n"][0:16, :], in0=g[:, :], in1=lnk[:, :],
                                 op=ALU.add)
                dv.tensor_tensor(out=ST["m"][0:16, :], in0=ST["n"][0:16, :], in1=lnb[:, :],
                                 op=ALU.subtract)
                base = g[0:16, 63:64]
                ge_view = bass.AP(base.tensor, base.offset, [[512, 16], [64, 8], [0, 64]])
                dv.tensor_tensor(out=ek[:, :].rearrange("p (c_ s_) -> p c_ s_", c_=8),
                                 in0=ge_view,
                                 in1=g[:, :].rearrange("p (c_ s_) -> p c_ s_", c_=8),
                                 op=ALU.subtract)
                dv.tensor_tensor(out=ek[:, :], in0=ek[:, :], in1=lnk[:, :], op=ALU.subtract)
                gp.tensor_copy(vec4[0:16, :], ST["p"][0:16, :])
                gp.tensor_copy(vec4[32:48, :], VV["lnb"][:, :])
                gp.tensor_copy(vec4[64:80, :], VV["r"][:, :])
                gp.tensor_copy(vec4[96:112, :], VV["ek"][:, :])
            out.append(stacks)
            return out

        def prep_stages(sl, it):
            t0 = it * TT
            c0, c1 = 2*sl, 2*sl + 1
            MT = TL["MTa"] if sl % 2 == 0 else TL["MTb"]
            PX = TL["P1"] if sl % 2 == 0 else TL["P2"]
            out = []

            def vproj(sl=sl, t0=t0):
                xfm = xfm2[it % 2]
                for nh in range(2):
                    for kc in range(8):
                        pe.matmul(p1[:, nh*512:(nh+1)*512],
                                  xfm[:, kc*512 + sl*128:kc*512 + (sl+1)*128],
                                  W["wv"][:, kc*1024 + nh*512:kc*1024 + (nh+1)*512],
                                  start=(kc == 0), stop=(kc == 7))
                    pe.flush()
                dv.tensor_copy(vtm2[sl % 2][:, :], p1[:, :])
            out.append(vproj)

            def ktx(sl=sl, t0=t0):
                sy.dma_start(ktm[sl % 2][:, :],
                             kfm_h[:, t0+sl*128:t0+(sl+1)*128], transpose=True)
            out.append(ktx)

            def kk():
                order = ([(h, half) for h in range(0, 16, 2) for half in (0, 1)]
                         + [(h, half) for h in range(1, 16, 2) for half in (1, 0)])
                for h, half in order:
                    b = 64 * (h % 2)
                    j = h // 2
                    cch = c0 if half == 0 else c1
                    pe.matmul(p1[64*half:64*half + 64, h*64:(h+1)*64],
                              kfm[b:b+64, j*512 + cch*64:j*512 + (cch+1)*64],
                              kfm[b:b+64, j*512 + cch*64:j*512 + (cch+1)*64],
                              start=True, stop=True, tile_position=(b, 64*half))
                pe.flush()
            out.append(kk)

            def gbuild(lhs_nm, rbvec, mask_nm, rev, kk_ap, dst):
                for half, cch in [(0, c0), (1, c1)]:
                    if rbvec == "r":
                        srcv = VV["r"][0:16, cch*64:(cch+1)*64]
                    else:
                        srcv = ST[rbvec][0:16, cch*64:(cch+1)*64]
                    vw = bass.AP(srcv.tensor, srcv.offset,
                                 [[512, 16], [0, 16], [1, 64]])
                    gp.tensor_tensor(
                        out=RH[32:48, :].rearrange("p (hh s_) -> p hh s_", hh=16),
                        in0=vw,
                        in1=CN["ehead16"][:, :].rearrange("p (hh s_) -> p hh s_", hh=16),
                        op=ALU.mult)
                    for nh in range(2):
                        pe.matmul(p2[64*half:64*half + 64, nh*512:(nh+1)*512],
                                  ST[lhs_nm][:, cch*64:(cch+1)*64],
                                  RH[:, nh*512:(nh+1)*512],
                                  start=True, stop=True, tile_position=(0, 64*half))
                    pe.flush()
                if rev:
                    dv.tensor_tensor(out=t_Gm[:, :], in0=CN[mask_nm][:, :],
                                     in1=p2[:, :], op=ALU.subtract)
                else:
                    dv.tensor_tensor(out=t_Gm[:, :], in0=p2[:, :],
                                     in1=CN[mask_nm][:, :], op=ALU.add)
                sc.activation(out=TL["Eg"][:, 0:512], in_=t_Gm[:, 0:512], func=AF.Exp)
                sc.activation(out=TL["Eg"][:, 512:1024], in_=t_Gm[:, 512:1024],
                              func=AF.Exp)
                for cl in (slice(0, 512), slice(512, 1024)):
                    if dst is MT:
                        dv.tensor_tensor(out=dst[:, cl], in0=kk_ap[:, cl],
                                         in1=TL["Eg"][:, cl], op=ALU.mult)
                    else:
                        dv.scalar_tensor_tensor(out=dst[:, cl], in0=kk_ap[:, cl],
                                                scalar=-1.0, in1=TL["Eg"][:, cl],
                                                op0=ALU.mult, op1=ALU.mult)

            out.append(lambda: gbuild("p", "m", "maskA", False, p1, TL["NA"]))
            out.append(lambda: gbuild("m", "p", "maskAT", True, p1, TL["Z0"]))

            def qkt():
                order = ([(h, half) for h in range(0, 16, 2) for half in (0, 1)]
                         + [(h, half) for h in range(1, 16, 2) for half in (1, 0)])
                for h, half in order:
                    b = 64 * (h % 2)
                    j = h // 2
                    cch = c0 if half == 0 else c1
                    pe.matmul(p1[64*half:64*half + 64, h*64:(h+1)*64],
                              kfm[b:b+64, j*512 + cch*64:j*512 + (cch+1)*64],
                              qfm[b:b+64, j*512 + cch*64:j*512 + (cch+1)*64],
                              start=True, stop=True, tile_position=(b, 64*half))
                pe.flush()
            out.append(qkt)
            out.append(lambda: gbuild("n", "r", "maskMT", True, p1, MT))

            def p0init():
                gp.tensor_tensor(out=TL["P0"][:, :], in0=TL["Z0"][:, :],
                                 in1=CN["identP"][:, :], op=ALU.add)
            out.append(p0init)

            seqs = []
            Bc, Zc, Pc = TL["NA"], TL["Z0"], TL["P0"]
            Bn, Zn, Pn = TL["B1"], TL["Z1"], PX
            for lev in range(1, 6):
                def fb(Bc=Bc, Zc=Zc, Bn=Bn):
                    for h in range(16):
                        for half in (0, 1):
                            b = 64 * half
                            pe.matmul(p1[b:b+64, h*64:(h+1)*64],
                                      Zc[b:b+64, h*64:(h+1)*64],
                                      Bc[b:b+64, h*64:(h+1)*64],
                                      start=True, stop=True, tile_position=(b, b))
                    pe.flush()
                    dv.tensor_copy(Bn[:, 0:512], p1[:, 0:512])
                    dv.tensor_copy(Bn[:, 512:1024], p1[:, 512:1024])
                out.append(fb)
                if lev < 5:
                    def fz(Bc=Bc, Zc=Zc, Zn=Zn):
                        for h in range(16):
                            for half in (0, 1):
                                b = 64 * half
                                pe.matmul(p1[b:b+64, h*64:(h+1)*64],
                                          Bc[b:b+64, h*64:(h+1)*64],
                                          Zc[b:b+64, h*64:(h+1)*64],
                                          start=True, stop=True, tile_position=(b, b))
                        pe.flush()
                        sc.activation(out=Zn[:, 0:512], in_=p1[:, 0:512],
                                      func=AF.Identity)
                        sc.activation(out=Zn[:, 512:1024], in_=p1[:, 512:1024],
                                      func=AF.Identity)
                    out.append(fz)
                def fp(Bn=Bn, Pc=Pc, Pn=Pn):
                    for h in range(16):
                        for half in (0, 1):
                            b = 64 * half
                            pe.matmul(p2[b:b+64, h*64:(h+1)*64],
                                      Bn[b:b+64, h*64:(h+1)*64],
                                      Pc[b:b+64, h*64:(h+1)*64],
                                      start=True, stop=True, tile_position=(b, b))
                    pe.flush()
                    dv.tensor_tensor(out=Pn[:, 0:512], in0=p2[:, 0:512],
                                     in1=Pc[:, 0:512], op=ALU.add)
                    dv.tensor_tensor(out=Pn[:, 512:1024], in0=p2[:, 512:1024],
                                     in1=Pc[:, 512:1024], op=ALU.add)
                out.append(fp)
                Bc, Bn = Bn, (TL["B2"] if Bn is TL["B1"] else TL["B1"])
                Zc, Zn = Zn, (TL["Z2"] if Zn is TL["Z1"] else TL["Z1"])
                Pc, Pn = Pn, (TL["P0"] if Pn is PX else PX)
            return out, Pc, MT

        def seq_stages(sl, Pfin, MT):
            c0, c1 = 2*sl, 2*sl + 1
            out = []

            def s_tr(c0=c0):
                pe.transpose(pS[:, 0:128], vec4[:, c0*64:c0*64 + 128],
                             CN["identF"][:, :])
                pe.flush()
                sc.activation(out=bc4[:, 0:64], in_=pS[:, 0:64], func=AF.Exp)
                sc.activation(out=bc4[:, 64:128], in_=pS[:, 64:128], func=AF.Exp)
            out.append(s_tr)

            for hf, cch in [(0, c0), (1, c1)]:
                def bce(grp, b0=64*hf):
                    b = bc4[b0:b0+64, grp:grp+16]
                    return bass.AP(b.tensor, b.offset, [[128, 64], [1, 16], [0, 64]])

                b0 = 64 * hf
                hs = slice(b0, b0 + 64)
                css = slice(cch*64, (cch+1)*64)

                def s_ks0(cch=cch, b0=b0, hs=hs, css=css, sl=sl, bce=bce):
                    ob = 64 - b0
                    for h in list(range(0, 16, 2)) + [-1] + list(range(1, 16, 2)):
                        if h < 0:
                            pe.matmul(pS[ob:ob+64, 0:64], kfm[64:128, 0:64],
                                      S0b[64:128, 0:64], start=True, stop=True,
                                      tile_position=(64, ob))
                            continue
                        b = 64 * (h % 2)
                        j = h // 2
                        pe.matmul(pS[b0:b0+64, h*64:(h+1)*64],
                                  kfm[b:b+64, j*512 + cch*64:j*512 + (cch+1)*64],
                                  S0b[b:b+64, j*64:(j+1)*64],
                                  start=True, stop=True, tile_position=(b, b0))
                    pe.flush()
                    dv.tensor_tensor(out=TL["t2"][hs, :].rearrange("p (h s_) -> p h s_", s_=64),
                                     in0=pS[hs, :].rearrange("p (h s_) -> p h s_", s_=64),
                                     in1=bce(0), op=ALU.mult)
                    dv.tensor_tensor(out=TL["R"][hs, :],
                                     in0=vtm2[sl % 2][hs, :],
                                     in1=TL["t2"][hs, :], op=ALU.subtract)
                out.append(s_ks0)

                def s_tapply(b0=b0, hs=hs, css=css, Pfin=Pfin, bce=bce):
                    for h in range(16):
                        pe.matmul(pS[b0:b0+64, h*64:(h+1)*64],
                                  Pfin[b0:b0+64, h*64:(h+1)*64],
                                  TL["R"][b0:b0+64, h*64:(h+1)*64],
                                  start=True, stop=True, tile_position=(b0, b0))
                    pe.flush()
                    for ci in (0, 1):
                        cl = slice(ci*512, (ci+1)*512)
                        bv = bce(32)
                        bv = bass.AP(bv.tensor, bv.offset + ci*8,
                                     [[128, 64], [1, 8], [0, 64]])
                        dv.tensor_tensor(out=TL["U"][hs, cl].rearrange("p (h s_) -> p h s_", s_=64),
                                         in0=pS[hs, cl].rearrange("p (h s_) -> p h s_", s_=64),
                                         in1=bv, op=ALU.mult)
                out.append(s_tapply)

                def s_qs0(cch=cch, b0=b0, hs=hs, css=css, bce=bce):
                    ob = 64 - b0
                    for h in list(range(0, 16, 2)) + [-1] + list(range(1, 16, 2)):
                        if h < 0:
                            pe.matmul(pS[ob:ob+64, 0:64], qfm[64:128, 0:64],
                                      S0b[64:128, 0:64], start=True, stop=True,
                                      tile_position=(64, ob))
                            continue
                        b = 64 * (h % 2)
                        j = h // 2
                        pe.matmul(pS[b0:b0+64, h*64:(h+1)*64],
                                  qfm[b:b+64, j*512 + cch*64:j*512 + (cch+1)*64],
                                  S0b[b:b+64, j*64:(j+1)*64],
                                  start=True, stop=True, tile_position=(b, b0))
                    pe.flush()
                    dv.tensor_tensor(out=TL["t2"][hs, :].rearrange("p (h s_) -> p h s_", s_=64),
                                     in0=pS[hs, :].rearrange("p (h s_) -> p h s_", s_=64),
                                     in1=bce(64), op=ALU.mult)
                out.append(s_qs0)

                def s_bc_ek(cch=cch, b0=b0, hs=hs, css=css, sl=sl, bce=bce):
                    gp.tensor_tensor(out=TL["Ku"][hs, :].rearrange("p (h s_) -> p h s_", s_=64),
                                     in0=ktm[sl % 2][hs, :].rearrange("p (h s_) -> p h s_", s_=64),
                                     in1=bce(96), op=ALU.mult)
                out.append(s_bc_ek)

                def s_supd(cch=cch, b0=b0, hs=hs, css=css, sl=sl):
                    sc.activation(out=sEg1[:, 0:1],
                                  in_=VV["g"][0:16, cch*64+63:cch*64+64], func=AF.Exp)
                    ebase = sEg1[0:16, 0:1]
                    eview = bass.AP(ebase.tensor, ebase.offset,
                                    [[ebase.ap[0][0], 16], [0, 512]])
                    dv.tensor_tensor(out=sEgr[:, :], in0=CN["epair"][:, :],
                                     in1=eview, op=ALU.mult)
                    pe.matmul(pS[:, 0:512], CN["parity"][:, :], sEgr[:, :],
                              start=True, stop=True)
                    pe.flush()
                    for h in range(16):
                        b = 64 * (h % 2)
                        j = h // 2
                        pe.matmul(pS[b:b+64, 512 + j*64:512 + (j+1)*64],
                                  TL["Ku"][b0:b0+64, h*64:(h+1)*64],
                                  TL["U"][b0:b0+64, h*64:(h+1)*64],
                                  start=True, stop=True, tile_position=(b0, b))
                    pe.flush()
                    dv.tensor_tensor(out=sSd[:, :], in0=S0[:, :], in1=pS[:, 0:512],
                                     op=ALU.mult)
                    dv.tensor_tensor(out=S0[:, :], in0=pS[:, 512:1024], in1=sSd[:, :],
                                     op=ALU.add)
                    sc.activation(out=S0b[:, :], in_=S0[:, :], func=AF.Identity)
                out.append(s_supd)

                def s_oapply(b0=b0, hs=hs, sl=sl, MT=MT):
                    for h in range(16):
                        pe.matmul(pS[b0:b0+64, h*64:(h+1)*64],
                                  MT[b0:b0+64, h*64:(h+1)*64],
                                  TL["U"][b0:b0+64, h*64:(h+1)*64],
                                  start=True, stop=True, tile_position=(b0, b0))
                    pe.flush()
                    dv.tensor_tensor(out=otm[sl % 2][hs, :], in0=pS[hs, :],
                                     in1=TL["t2"][hs, :], op=ALU.add)
                out.append(s_oapply)

            return out

        def epi1_stages(sl, it):
            t0 = it * TT
            xfm = xfm2[it % 2]
            om = otm[sl % 2]
            out = []
            def rms(om=om, sl=sl, t0=t0):
                gp.dma_start(xbsl[:, :], x_in[t0+sl*128:t0+(sl+1)*128, :])
                sc.activation(out=escr[:, :], in_=om[:, :], func=AF.Square)
                dv.tensor_reduce(out=colsc[:, 0:16],
                                 in_=escr[:, :].rearrange("p (h s_) -> p h s_", s_=64),
                                 axis=AX.X, op=ALU.add)
                dv.tensor_scalar(out=colsc[:, 16:32], in0=colsc[:, 0:16],
                                 scalar1=1.0/64.0, scalar2=1e-6,
                                 op0=ALU.mult, op1=ALU.add)
                sc.activation(out=colsc[:, 32:48], in_=colsc[:, 16:32], func=AF.Ln)
                sc.activation(out=colsc[:, 48:64], in_=colsc[:, 32:48], func=AF.Exp,
                              scale=-0.5)
                rbase = colsc[:, 48:64]
                rstdv = bass.AP(rbase.tensor, rbase.offset,
                                [[rbase.ap[0][0], 128], [1, 16], [0, 64]])
                dv.tensor_tensor(out=scrb[:, :].rearrange("p (h s_) -> p h s_", s_=64),
                                 in0=om[:, :].rearrange("p (h s_) -> p h s_", s_=64),
                                 in1=rstdv, op=ALU.mult)
            out.append(rms)
            def gates(om=om, sl=sl, t0=t0):
                for nh, bank in [(0, pP1), (1, pP2)]:
                    for kc in range(8):
                        pe.matmul(bank[:, :],
                                  xfm[:, kc*512 + sl*128:kc*512 + (sl+1)*128],
                                  W["wgate"][:, kc*1024 + nh*512:kc*1024 + (nh+1)*512],
                                  start=(kc == 0), stop=(kc == 7))
                    pe.flush()
                sc.activation(out=om[:, 0:512], in_=pP1[:, :], func=AF.Silu)
                sc.activation(out=om[:, 512:1024], in_=pP2[:, :], func=AF.Silu)
                gp.tensor_tensor(out=scrb[:, :], in0=scrb[:, :], in1=om[:, :],
                                 op=ALU.mult)
                sy.dma_start(og_h[t0+sl*128:t0+(sl+1)*128, :], scrb[:, :])
                for ec in range(8):
                    sy.dma_start(om[:, ec*128:(ec+1)*128],
                                 og_h[t0+sl*128:t0+(sl+1)*128, ec*128:(ec+1)*128],
                                 transpose=True)
            out.append(gates)
            return out

        def epi2_stages(sl, it):
            t0 = it * TT
            om = otm[sl % 2]
            out = []
            def wout(om=om, sl=sl, t0=t0):
                for nh, bank in [(0, pP1), (1, pP2)]:
                    for kc in range(8):
                        pe.matmul(bank[:, :],
                                  om[:, kc*128:(kc+1)*128],
                                  W["wout"][:, kc*1024 + nh*512:kc*1024 + (nh+1)*512],
                                  start=(kc == 0), stop=(kc == 7))
                    pe.flush()
                dv.tensor_tensor(out=escr[:, 0:512], in0=pP1[:, :],
                                 in1=xbsl[:, 0:512], op=ALU.add)
                dv.tensor_tensor(out=escr[:, 512:1024], in0=pP2[:, :],
                                 in1=xbsl[:, 512:1024], op=ALU.add)
                sc.activation(out=scrb[:, :], in_=escr[:, :], func=AF.Square,
                              accum_out=colsc[:, 0:1])
                dv.tensor_reduce(out=colsc[:, 1:2], in_=escr[:, :], axis=AX.X,
                                 op=ALU.add)
                dv.tensor_scalar_mul(out=colsc[:, 2:3], in0=colsc[:, 1:2],
                                     scalar1=1.0/1024.0)
                dv.tensor_scalar_mul(out=colsc[:, 3:4], in0=colsc[:, 2:3],
                                     scalar1=colsc[:, 2:3])
                dv.tensor_scalar(out=colsc[:, 4:5], in0=colsc[:, 0:1],
                                 scalar1=1.0/1024.0, scalar2=1e-5,
                                 op0=ALU.mult, op1=ALU.add)
                dv.tensor_tensor(out=colsc[:, 4:5], in0=colsc[:, 4:5],
                                 in1=colsc[:, 3:4], op=ALU.subtract)
                sc.activation(out=colsc[:, 5:6], in_=colsc[:, 4:5], func=AF.Ln)
                sc.activation(out=colsc[:, 6:7], in_=colsc[:, 5:6], func=AF.Exp,
                              scale=-0.5)
                dv.tensor_scalar(out=scrb[:, :], in0=escr[:, :],
                                 scalar1=colsc[:, 2:3], scalar2=colsc[:, 6:7],
                                 op0=ALU.subtract, op1=ALU.mult)
                sy.dma_start(x2_h[t0+sl*128:t0+(sl+1)*128, :], scrb[:, :])
            out.append(wout)
            return out

        def weave(A, B):
            if not A:
                for f in B:
                    f()
                return
            nb, na = len(B), len(A)
            bi = 0
            for i, fa in enumerate(A):
                want = ((i + 1) * nb) // na
                while bi < want:
                    B[bi]()
                    bi += 1
                fa()
            while bi < nb:
                B[bi]()
                bi += 1

        prep_next = None
        for it in range(_N_TILES):
            for f in t0a_stages(it, skip_load=(it > 0)):
                f()
            if it == 0:
                for nm in ["wv", "wgate", "wout"]:
                    gp.dma_start(W[nm][:, :].rearrange("p (kc n) -> p kc n", n=1024),
                                 w_in[nm].rearrange("(kc p) n -> p kc n", p=128))
                gp.dma_start(wa_s[:, :].rearrange("p (kc c_) -> p kc c_", c_=16),
                             wa_in.rearrange("(kc p) n -> p kc n", p=128))
                gp.dma_start(wb_s[:, :].rearrange("p (kc c_) -> p kc c_", c_=16),
                             wb_in.rearrange("(kc p) n -> p kc n", p=128))
                for kc in range(8):
                    dv.tensor_copy(scrb[:, :], W["wout"][:, kc*1024:(kc+1)*1024])
                    dv.tensor_scalar_mul(out=W["wout"][:, kc*1024:(kc+1)*1024],
                                         in0=scrb[:, :], scalar1=rmswc[:, :])
            if it > 0:
                for f in epi2_stages(3, it - 1):
                    f()
            stg, Pfin, MT = prep_stages(0, it)
            weave(t0b_stages(it), stg[0:3])
            for f in stg[3:]:
                f()
            for sl in range(NSL):
                QB = []
                if sl >= 1:
                    QB += epi1_stages(sl - 1, it)
                if sl < 3:
                    stg2, Pfin2, MT2 = prep_stages(sl + 1, it)
                    QB += stg2
                elif it + 1 < _N_TILES:
                    QB.append(xfm_load(it + 1))
                weave(seq_stages(sl, Pfin, MT), QB)
                if sl >= 1:
                    for f in epi2_stages(sl - 1, it):
                        f()
                if sl < 3:
                    Pfin, MT = Pfin2, MT2
            for f in epi1_stages(3, it):
                f()
            if it == _N_TILES - 1:
                for f in epi2_stages(3, it):
                    f()

        dve_A, act_A, pool_A, pe_A = cnt["dve"], cnt["act"], cnt["pool"], cnt["pe"]
        ldA_A = {s: cnt[s] for s in QSEMS["sy"]}
        ldB_A = {s: cnt[s] for s in QSEMS["gpL"]}
        wait("pe", "dve", dve_A)
        wait("pe", "act", act_A)
        inc(op("pe", "nop"), "bar", 1)
        wait("dv", "pe", pe_A)
        inc(op("dv", "engine_nop"), "bar", 1)
        wait("sc", "dve", dve_A)
        inc(op("sc", "nop"), "bar", 1)
        for s, v in ldB_A.items():
            wait("gp", s, v)
        wait("gp", "pool", pool_A)
        inc(op("gp", "engine_nop"), "bar", 1)
        for s, v in ldA_A.items():
            wait("sy", s, v)
        inc(op("sy", "nop"), "bar", 1)
        BAR = cnt["bar"]

        emit_all()
        stA.close()

        stB = contextlib.ExitStack()
        sbB = lambda n, s_, d: stB.enter_context(nc.sbuf_tensor(n, s_, d))
        wgT = sbB("wgT", [128, 8*3072], BF16)
        wuT = sbB("wuT", [128, 8*3072], BF16)
        wdT = sbB("wdT", [128, 24*1024], BF16)
        x2fm = sbB("x2fm", [128, 8*512], BF16)
        x2tm = sbB("x2tm", [128, 4*1024], BF16)
        h_s = sbB("h_s", [128, 24*512], BF16)
        b_scr1 = sbB("b_scr1", [128, 1024], F32)
        b_scr2 = sbB("b_scr2", [128, 1024], F32)
        b_scrb = sbB("b_scrb", [128, 512], BF16)
        b_col = sbB("b_col", [128, 16], F32)
        bpA = stB.enter_context(nc.psum_tensor("bpA", [128, 512], F32))
        bpB = stB.enter_context(nc.psum_tensor("bpB", [128, 512], F32))
        bpC = stB.enter_context(nc.psum_tensor("bpC", [128, 512], F32))
        bpD = stB.enter_context(nc.psum_tensor("bpD", [128, 512], F32))

        if _PHASEB_ON:
            wait("gp", "bar", BAR)
            for ec in range(8):
                gp.dma_start(wgT[:, ec*3072:(ec+1)*3072],
                             fgT_in[ec*128:(ec+1)*128, :])
            for ec in range(8):
                gp.dma_start(wuT[:, ec*3072:(ec+1)*3072],
                             fuT_in[ec*128:(ec+1)*128, :])

        for it in range(NT if _PHASEB_ON else 0):
            t0 = it * TT
            wait("sy", "bar", BAR)
            for ec in range(8):
                sy.dma_start(x2fm[:, ec*512:(ec+1)*512],
                             x2_h[t0:t0+TT, ec*128:(ec+1)*128], transpose=True)
            gp.dma_start(x2tm[:, :].rearrange("p (sl e) -> p sl e", e=1024),
                         x2_h[t0:t0+TT, :].rearrange("(sl p) e -> p sl e", p=128))
            if it == 0:
                for hc in range(24):
                    gp.dma_start(wdT[:, hc*1024:(hc+1)*1024],
                                 fdT_in[hc*128:(hc+1)*128, :])

            for hc in range(24):
                for ec in range(8):
                    pe.matmul(bpA[:, :], wgT[:, ec*3072 + hc*128:ec*3072 + (hc+1)*128],
                              x2fm[:, ec*512:(ec+1)*512], start=(ec == 0), stop=(ec == 7))
                pe.flush()
                for ec in range(8):
                    pe.matmul(bpB[:, :], wuT[:, ec*3072 + hc*128:ec*3072 + (hc+1)*128],
                              x2fm[:, ec*512:(ec+1)*512], start=(ec == 0), stop=(ec == 7))
                pe.flush()
                sc.activation(out=b_scrb[:, :], in_=bpA[:, :], func=AF.Silu, scale=RQ)
                dv.scalar_tensor_tensor(out=h_s[:, hc*512:(hc+1)*512], in0=bpB[:, :],
                                        scalar=RQ*RQ, in1=b_scrb[:, :],
                                        op0=ALU.mult, op1=ALU.mult)

            for sl in range(NSL):
                for nh, bank in [(0, bpC), (1, bpD)]:
                    for hc in range(24):
                        pe.matmul(bank[:, :],
                                  h_s[:, hc*512 + sl*128:hc*512 + (sl+1)*128],
                                  wdT[:, hc*1024 + nh*512:hc*1024 + (nh+1)*512],
                                  start=(hc == 0), stop=(hc == 23))
                    pe.flush()
                dv.tensor_tensor(out=b_scr1[:, 0:512], in0=bpC[:, :],
                                 in1=x2tm[:, sl*1024:sl*1024 + 512], op=ALU.add)
                dv.tensor_tensor(out=b_scr1[:, 512:1024], in0=bpD[:, :],
                                 in1=x2tm[:, sl*1024 + 512:(sl+1)*1024], op=ALU.add)
                sc.activation(out=b_scr2[:, :], in_=b_scr1[:, :], func=AF.Square,
                              accum_out=b_col[:, 0:1])
                dv.tensor_reduce(out=b_col[:, 1:2], in_=b_scr1[:, :], axis=AX.X,
                                 op=ALU.add)
                dv.tensor_scalar_mul(out=b_col[:, 2:3], in0=b_col[:, 1:2],
                                     scalar1=1.0/1024.0)
                dv.tensor_scalar_mul(out=b_col[:, 3:4], in0=b_col[:, 2:3],
                                     scalar1=b_col[:, 2:3])
                dv.tensor_scalar(out=b_col[:, 4:5], in0=b_col[:, 0:1],
                                 scalar1=1.0/1024.0, scalar2=1e-5,
                                 op0=ALU.mult, op1=ALU.add)
                dv.tensor_tensor(out=b_col[:, 4:5], in0=b_col[:, 4:5],
                                 in1=b_col[:, 3:4], op=ALU.subtract)
                sc.activation(out=b_col[:, 5:6], in_=b_col[:, 4:5], func=AF.Ln)
                sc.activation(out=b_col[:, 6:7], in_=b_col[:, 5:6], func=AF.Exp,
                              scale=-0.5)
                dv.tensor_scalar(out=b_scr2[:, :], in0=b_scr1[:, :],
                                 scalar1=b_col[:, 2:3], scalar2=b_col[:, 6:7],
                                 op0=ALU.subtract, op1=ALU.mult)
                DM_("gp", "gpS").dma_start(out_t[t0+sl*128:t0+(sl+1)*128, :],
                                           b_scr2[:, :])

        for s in QSEMS["gpS"]:
            wait("gp", s, cnt[s])

        emit_all()
        stB.close()

    return nc


def _shard_inputs(inputs):
    cns = host_consts()
    rmsw = np.asarray(inputs["L0_attn_rms_w"])
    base = {
        "wq": np.ascontiguousarray(inputs["L0_attn_Wq"]),
        "wk": np.ascontiguousarray(inputs["L0_attn_Wk"]),
        "wv": np.ascontiguousarray(inputs["L0_attn_Wv"]),
        "wgate": np.ascontiguousarray(inputs["L0_attn_Wgate"]),
        "wout": np.ascontiguousarray(inputs["L0_attn_Wout"]),
        "wa": np.ascontiguousarray(inputs["L0_attn_Walpha"]),
        "wb": np.ascontiguousarray(inputs["L0_attn_Wbeta"]),
        "alog": np.ascontiguousarray(np.asarray(inputs["L0_attn_A_log"]).reshape(16, 1)),
        "dtb": np.ascontiguousarray(np.asarray(inputs["L0_attn_dt_bias"]).reshape(16, 1)),
        "rmsw_c": np.ascontiguousarray(np.tile(rmsw, 2).reshape(128, 1).astype(np.float32)),
        "fgT": np.ascontiguousarray(np.asarray(inputs["L0_ffn_w_gate"]).T),
        "fuT": np.ascontiguousarray(np.asarray(inputs["L0_ffn_w_up"]).T),
        "fdT": np.ascontiguousarray(np.asarray(inputs["L0_ffn_w_down"]).T),
    }
    for k, v in cns.items():
        base[k] = v
    x = np.asarray(inputs["L0_attn_x"])
    in_maps = []
    for b in range(8):
        m = dict(base)
        m["x"] = np.ascontiguousarray(x[b])
        m["xT"] = np.ascontiguousarray(x[b].T)
        in_maps.append(m)
    return in_maps


def kernel(**inputs):
    in_maps = _shard_inputs(inputs)
    nc = build_program()
    res = run_bass_kernel_spmd(nc, in_maps, core_ids=list(range(8)))
    return np.stack([r["out"] for r in res.results], axis=0)


# version 19
# speedup vs baseline: 1.0305x; 1.0127x over previous
import contextlib
import numpy as np
import ml_dtypes
import concourse.bass as bass
import concourse.mybir as mybir
from concourse.bass_utils import run_bass_kernel_spmd

F32 = mybir.dt.float32
BF16 = mybir.dt.bfloat16
I32 = mybir.dt.int32
AF = mybir.ActivationFunctionType
ALU = mybir.AluOpType
AX = mybir.AxisListType
bfdt = ml_dtypes.bfloat16

S, E, H, D, C = 4096, 1024, 16, 64, 64
HID = 3072
TT = 512
NT = S // TT
NSL = TT // 128
NCH = TT // C
NEG = -1.0e30
RQ = 1.0 / 127.0

_DEBUG_TAPS = None
_N_TILES = NT
_CHUNK_ON = True
_CHUNK_SUB = 9
_SEQ_ON = True
_EPI_ON = True
_PHASEB_ON = True


def host_consts():
    c = {}
    eh = np.zeros((16, 1024), np.float32)
    for h in range(16):
        eh[h, h*64:(h+1)*64] = 1.0
    c["ehead16"] = eh.astype(bfdt)
    lt = np.zeros((48, 512), np.float32)
    lt[32:48] = -1.0
    c["lhs_tmpl"] = lt
    rt = np.zeros((48, 1024), np.float32)
    rt[0:16] = eh
    c["rhs_tmpl"] = rt
    t_idx = (np.arange(128) % 64)[:, None].astype(np.float32)
    s_idx = (np.arange(1024) % 64)[None, :].astype(np.float32)
    c["maskA"] = np.where(s_idx < t_idx, 0.0, NEG).astype(bfdt)
    c["maskAT"] = np.where(t_idx < s_idx, 0.0, NEG).astype(bfdt)
    c["maskMT"] = np.where(t_idx <= s_idx, 0.0, NEG).astype(bfdt)
    c["identP"] = np.where(s_idx == t_idx, 1.0, 0.0).astype(bfdt)
    sm = np.ones((16, 512), np.float32)
    sm[:, ::C] = 0.0
    c["scanmask"] = sm.astype(bfdt)
    ehm = np.zeros((128, 128), np.float32)
    for mc in range(8):
        ehm[0:64, mc*16 + 2*mc] = 1.0
        ehm[64:128, mc*16 + 2*mc + 1] = 1.0
    c["ehm"] = ehm
    c["identF"] = np.eye(128, dtype=np.float32)
    par = np.zeros((16, 128), np.float32)
    par[0::2, 0:64] = 1.0
    par[1::2, 64:128] = 1.0
    c["parity"] = par
    ep = np.zeros((16, 512), np.float32)
    for hp in range(16):
        ep[hp, (hp//2)*64:(hp//2+1)*64] = 1.0
    c["epair"] = ep.astype(bfdt)
    return c


_POS_NAMES = {
    "tensor_copy": ("out", "in_"),
    "tensor_tensor": ("out", "in0", "in1"),
    "tensor_scalar": ("out", "in0", "scalar1", "scalar2"),
    "tensor_scalar_mul": ("out", "in0", "scalar1"),
    "tensor_scalar_add": ("out", "in0", "scalar1"),
    "tensor_scalar_sub": ("out", "in0", "scalar1"),
    "tensor_scalar_min": ("out", "in0", "scalar1"),
    "tensor_scalar_max": ("out", "in0", "scalar1"),
    "scalar_tensor_tensor": ("out", "in0", "scalar", "in1"),
    "tensor_reduce": ("out", "in_"),
    "reciprocal": ("out", "in_"),
    "memset": ("out", "constant"),
    "tensor_tensor_scan": ("out", "data0", "data1", "initial"),
    "activation": ("out", "in_", "func", "bias", "scale", "alpha", "accum_out"),
    "copy": ("out", "in_"),
}
_WRITE_KEYS = {"out", "accum_out", "dest", "ap"}
_OPNAMES = set(_POS_NAMES)


def build_program():
    nc = bass.Bass("TRN2", target_bir_lowering=False)

    x_in = nc.dram_tensor("x", [S, E], F32, kind="ExternalInput").ap()
    xT_in = nc.dram_tensor("xT", [E, S], F32, kind="ExternalInput").ap()
    w_in = {}
    for nm in ["wq", "wk", "wv", "wgate", "wout"]:
        w_in[nm] = nc.dram_tensor(nm, [E, E], F32, kind="ExternalInput").ap()
    wa_in = nc.dram_tensor("wa", [E, 16], F32, kind="ExternalInput").ap()
    wb_in = nc.dram_tensor("wb", [E, 16], F32, kind="ExternalInput").ap()
    alog_in = nc.dram_tensor("alog", [16, 1], F32, kind="ExternalInput").ap()
    dtb_in = nc.dram_tensor("dtb", [16, 1], F32, kind="ExternalInput").ap()
    rmsw_in = nc.dram_tensor("rmsw_c", [128, 1], F32, kind="ExternalInput").ap()
    fgT_in = nc.dram_tensor("fgT", [E, HID], I32, kind="ExternalInput").ap()
    fuT_in = nc.dram_tensor("fuT", [E, HID], I32, kind="ExternalInput").ap()
    fdT_in = nc.dram_tensor("fdT", [HID, E], I32, kind="ExternalInput").ap()
    cns_in = {}
    for nm, shp, dt in [("ehead16", [16, 1024], BF16), ("lhs_tmpl", [48, 512], F32),
                        ("rhs_tmpl", [48, 1024], F32), ("maskA", [128, 1024], BF16),
                        ("maskAT", [128, 1024], BF16), ("maskMT", [128, 1024], BF16),
                        ("identP", [128, 1024], BF16), ("scanmask", [16, 512], BF16),
                        ("ehm", [128, 128], F32), ("parity", [16, 128], F32),
                        ("epair", [16, 512], BF16), ("identF", [128, 128], F32)]:
        cns_in[nm] = nc.dram_tensor(nm, shp, dt, kind="ExternalInput").ap()
    out_t = nc.dram_tensor("out", [S, E], F32, kind="ExternalOutput").ap()

    kfm_h = nc.dram_tensor("kfm_h", [E, S], BF16, kind="Internal").ap()
    og_h = nc.dram_tensor("og_h", [S, E], BF16, kind="Internal").ap()
    x2_h = nc.dram_tensor("x2_h", [S, E], BF16, kind="Internal").ap()

    sems, cnt, wm = {}, {}, {}
    WRT, RDS = {}, {}
    SCRIPT = {k: [] for k in ("sy", "gp", "pe", "dv", "sc")}

    def op(ename, method, *args, **kwargs):
        item = ["op", method, args, kwargs, []]
        SCRIPT[ename].append(item)
        return item

    def inc(item, name, step=1):
        item[4].append((name, step))
        cnt[name] += step

    def wait(ename, name, val):
        if val <= 0:
            return
        if wm.get((ename, name), -1) >= val:
            return
        SCRIPT[ename].append(("w", name, val))
        wm[(ename, name)] = val

    _REGIONAL = {"t_U", "t_NA", "t_Z0", "t_B1", "t_Z1", "t_P0", "t_P1", "t_P2",
                 "t_MTa", "t_MTb", "t_R", "kfm", "qfm", "bc4"}

    def regkeys(ap):
        name = ap.tensor.name
        if name not in _REGIONAL:
            return (name,)
        dims = ap.ap
        W = dims[0][0]
        pb = ap.offset // W
        pc = dims[0][1]
        cb = ap.offset % W
        ext = 1
        for st_, sz_ in dims[1:]:
            ext += (sz_ - 1) * abs(st_)
        phs = set()
        if pb < 64:
            phs.add(0)
        if pb + pc > 64:
            phs.add(1)
        chs = set()
        half = W // 2
        if cb < half:
            chs.add(0)
        if cb + ext > half:
            chs.add(1)
        return tuple(f"{name}/{p}{c}" for p in phs for c in chs)

    def dep_wait(ename, wnames, rnames):
        for n in rnames | wnames:
            t = WRT.get(n)
            if t:
                wait(ename, t[0], t[1])
        for n in wnames:
            for s_, v_ in RDS.get(n, {}).items():
                wait(ename, s_, v_)

    def dep_record(sem, val, wnames, rnames):
        for n in wnames:
            WRT[n] = (sem, val)
            RDS[n] = {}
        for n in rnames:
            d = RDS.setdefault(n, {})
            if d.get(sem, -1) < val:
                d[sem] = val

    def classify(opname, args, kwargs):
        wnames, rnames = set(), set()
        names = _POS_NAMES.get(opname, ())
        items = list(zip(names, args)) + list(kwargs.items())
        for k, v in items:
            if isinstance(v, bass.AP):
                (wnames if k in _WRITE_KEYS else rnames).update(regkeys(v))
        return wnames, rnames

    _pe_win = []
    _pe_gid = [0]
    _pe_idx = [0]

    class P_:
        def __init__(self, ename, sem):
            self._ename = ename
            self._sem = sem

        def __getattr__(self, name):
            if name in _OPNAMES:
                def wrapped(*a, **k):
                    wn, rn = classify(name, a, k)
                    dep_wait(self._ename, wn, rn)
                    item = op(self._ename, name, *a, **k)
                    inc(item, self._sem, 1)
                    dep_record(self._sem, cnt[self._sem], wn, rn)
                    return item
                return wrapped
            raise AttributeError(name)

    class PE_:
        def __init__(self):
            self._pw, self._pr = set(), set()
            self._last = None

        def matmul(self, out, lhsT, rhs, **k):
            wn = set(regkeys(out))
            rn = set(regkeys(lhsT)) | set(regkeys(rhs))
            dep_wait("pe", wn, rn)
            oF = out.ap[0][0]
            pb, pc = out.offset // oF, out.ap[0][1]
            cb = out.offset % oF
            ncol = 1
            for _, sz in out.ap[1:]:
                ncol *= sz
            lF = lhsT.ap[0][0]
            rb, rc = lhsT.offset // lF, lhsT.ap[0][1]
            banks = set(range(cb // 512, (cb + ncol - 1) // 512 + 1))
            gid = _pe_gid[0]
            idx = _pe_idx[0]
            _pe_idx[0] += 1
            keep, drain = [], False
            for X in _pe_win:
                xt, xp0, xp1, xr0, xr1, xb, xg, xi = X
                if xi < idx - 4:
                    continue
                if not (xr1 <= rb or rb + rc <= xr0):
                    continue
                if (xt == out.tensor.name and not (xp1 <= pb or pb + pc <= xp0)
                        and (xb & banks)):
                    if xg != gid:
                        drain = True
                    else:
                        raise RuntimeError(
                            f"intra-group PE hazard on {xt} parts {pb}:{pb+pc}")
                keep.append(X)
            if drain:
                wait("pe", "pe", cnt["pe"])
                keep = []
            del _pe_win[:]
            _pe_win.extend(keep)
            _pe_win.append((out.tensor.name, pb, pb + pc, rb, rb + rc, banks, gid, idx))
            item = op("pe", "matmul", out, lhsT, rhs, **k)
            self._last = item
            self._pw |= wn
            self._pr |= rn
            return item

        def transpose(self, out, in_, identity, **k):
            wn = set(regkeys(out))
            rn = set(regkeys(in_)) | set(regkeys(identity))
            dep_wait("pe", wn, rn)
            oF = out.ap[0][0]
            pb, pc = out.offset // oF, out.ap[0][1]
            cb = out.offset % oF
            ncol = 1
            for _, sz in out.ap[1:]:
                ncol *= sz
            banks = set(range(cb // 512, (cb + ncol - 1) // 512 + 1))
            item = op("pe", "transpose", out, in_, identity, **k)
            self._last = item
            del _pe_win[:]
            _pe_win.append((out.tensor.name, pb, pb + pc, 0, 128, banks,
                            _pe_gid[0], _pe_idx[0]))
            _pe_idx[0] += 1
            self._pw |= wn
            self._pr |= rn
            return item

        def flush(self):
            if self._last is None:
                return
            inc(self._last, "pe", 1)
            dep_record("pe", cnt["pe"], self._pw, self._pr)
            self._pw, self._pr = set(), set()
            self._last = None
            _pe_gid[0] += 1

    QSEMS = {"sy": ["ldA0", "ldA1", "ldA2", "ldA3"],
             "gpL": ["ldB0", "ldB1", "ldB2"],
             "gpS": ["so0", "so1"]}
    qidx = {k: 0 for k in QSEMS}

    class DM_:
        def __init__(self, ename, qkey):
            self._ename = ename
            self._qkey = qkey

        def dma_start(self, out, in_, **k):
            grp = QSEMS[self._qkey]
            sem = grp[qidx[self._qkey] % len(grp)]
            qidx[self._qkey] += 1
            if cnt[sem] > 0:
                wait(self._ename, sem, cnt[sem])
            wn, rn = set(regkeys(out)), set(regkeys(in_))
            dep_wait(self._ename, wn, rn)
            item = op(self._ename, "dma_start", out=out, in_=in_, **k)
            inc(item, sem, 16)
            dep_record(sem, cnt[sem], wn, rn)
            return item

    class DMGP:
        def __init__(self):
            self._dma = DM_("gp", "gpL")
            self._cmp = P_("gp", "pool")

        def dma_start(self, out, in_, **k):
            return self._dma.dma_start(out, in_, **k)

        def __getattr__(self, name):
            if name in _OPNAMES:
                return getattr(self._cmp, name)
            raise AttributeError(name)

    with contextlib.ExitStack() as st:
        for s in ["ldA0", "ldA1", "ldA2", "ldA3", "ldB0", "ldB1", "ldB2",
                  "so0", "so1", "pe", "dve", "act", "pool", "bar"]:
            sems[s] = st.enter_context(nc.semaphore(s))
            cnt[s] = 0

        blk = st.enter_context(nc.Block())

        sy, gp = DM_("sy", "sy"), DMGP()
        pe, dv, sc = PE_(), P_("dv", "dve"), P_("sc", "act")

        def replay(eng, script):
            for it_ in script:
                if it_[0] == "w":
                    eng.wait_ge(sems[it_[1]], it_[2])
                else:
                    _, name, args, kwargs, incs = it_
                    instr = getattr(eng, name)(*args, **kwargs)
                    for s_, step in incs:
                        instr.then_inc(sems[s_], step)

        def emit_all():
            blk.sync(lambda e: replay(e, SCRIPT["sy"]))
            blk.gpsimd(lambda e: replay(e, SCRIPT["gp"]))
            blk.tensor(lambda e: replay(e, SCRIPT["pe"]))
            blk.vector(lambda e: replay(e, SCRIPT["dv"]))
            blk.scalar(lambda e: replay(e, SCRIPT["sc"]))
            for k in SCRIPT:
                SCRIPT[k] = []

        def tap(name, sb_ap, shape, dtype=F32):
            if not _DEBUG_TAPS or name not in _DEBUG_TAPS:
                return
            dr = nc.dram_tensor("tap_" + name, shape, dtype, kind="ExternalOutput").ap()
            DM_("gp", "gpS").dma_start(dr, sb_ap)

        stA = contextlib.ExitStack()
        sbA = lambda n, s_, d: stA.enter_context(nc.sbuf_tensor(n, s_, d))
        W = {}
        for nm in ["wq", "wk", "wv", "wgate", "wout"]:
            W[nm] = sbA(nm + "_s", [128, 8*1024], BF16)
        wa_s = sbA("wa_s", [128, 8*16], BF16)
        wb_s = sbA("wb_s", [128, 8*16], BF16)
        CN = {}
        for nm, shp, dt in [("ehead16", [16, 1024], BF16),
                            ("maskA", [128, 1024], BF16), ("maskAT", [128, 1024], BF16),
                            ("maskMT", [128, 1024], BF16), ("identP", [128, 1024], BF16),
                            ("scanmask", [16, 512], BF16), ("ehm", [128, 128], F32),
                            ("parity", [16, 128], F32), ("epair", [16, 512], BF16),
                            ("identF", [128, 128], F32)]:
            CN[nm] = sbA("c_" + nm, shp, dt)
        ST = {}
        for nm in ["p", "m", "n"]:
            ST[nm] = sbA("st_" + nm, [48, 512], F32)
        RH = sbA("rh", [48, 1024], F32)
        rmswc = sbA("rmswc", [128, 1], F32)
        SM = {}
        for nm in ["alog", "dtb", "negea"]:
            SM[nm] = sbA("s_" + nm, [16, 1], F32)
        S0 = sbA("S0", [128, 512], F32)
        S0b = sbA("S0b", [128, 512], BF16)
        xbsl = sbA("xbsl", [128, 1024], BF16)
        xfm2 = [sbA("xfmA", [128, 8*512], BF16), sbA("xfmB", [128, 8*512], BF16)]
        qfm = sbA("qfm", [128, 8*512], BF16)
        kfm = sbA("kfm", [128, 8*512], BF16)
        ktm = [sbA("ktmA", [128, 1024], BF16), sbA("ktmB", [128, 1024], BF16)]
        vtm2 = [sbA("vtmA", [128, 1024], BF16), sbA("vtmB", [128, 1024], BF16)]
        otm = [sbA("otmA", [128, 1024], BF16), sbA("otmB", [128, 1024], BF16)]
        scr1 = sbA("scr1", [128, 1024], F32)
        escr = sbA("escr", [128, 1024], F32)
        scrb = sbA("scrb", [128, 1024], BF16)
        colsc = sbA("colsc", [128, 64], F32)
        sSd = sbA("sSd", [128, 512], F32)
        vec4 = sbA("vec4", [128, 512], F32)
        bc4 = sbA("bc4", [128, 128], F32)
        sEg1 = sbA("sEg1", [16, 1], F32)
        VV = {}
        for nm in ["g", "lnk", "lnb", "r", "ek", "s1", "s2"]:
            VV[nm] = sbA("v_" + nm, [16, 512], F32)
        sEgr = VV["s2"]
        TL = {}
        for nm in ["NA", "Z0", "B1", "Z1", "P0", "P1", "P2",
                   "MTa", "MTb", "E", "R", "U"]:
            TL[nm] = sbA("t_" + nm, [128, 1024], BF16)
        TL["B2"] = TL["NA"]
        TL["Z2"] = TL["Z0"]
        TL["t2"] = TL["E"]
        TL["Ku"] = TL["R"]
        TL["Eg"] = TL["B1"]
        t_Gm = scr1

        p1 = stA.enter_context(nc.psum_tensor("p1", [128, 1024], F32))
        p2 = stA.enter_context(nc.psum_tensor("p2", [128, 1024], F32))
        pS = stA.enter_context(nc.psum_tensor("pS", [128, 1024], F32))
        pP1 = stA.enter_context(nc.psum_tensor("pP1", [128, 512], F32))
        pP2 = stA.enter_context(nc.psum_tensor("pP2", [128, 512], F32))

        for nm in ["wq", "wk"]:
            gp.dma_start(W[nm][:, :].rearrange("p (kc n) -> p kc n", n=1024),
                         w_in[nm].rearrange("(kc p) n -> p kc n", p=128))
        for nm in ["ehead16", "maskA", "maskAT", "maskMT", "identP", "scanmask",
                   "ehm", "parity", "epair", "identF"]:
            sy.dma_start(CN[nm][:, :], cns_in[nm])
        for nm in ["p", "m", "n"]:
            sy.dma_start(ST[nm][:, :], cns_in["lhs_tmpl"])
        sy.dma_start(RH[:, :], cns_in["rhs_tmpl"])
        sy.dma_start(rmswc[:, :], rmsw_in)
        sy.dma_start(SM["alog"][:, :], alog_in)
        sy.dma_start(SM["dtb"][:, :], dtb_in)

        sc.activation(out=SM["negea"][:, :], in_=SM["alog"][:, :], func=AF.Exp)
        dv.tensor_scalar_mul(out=SM["negea"][:, :], in0=SM["negea"][:, :], scalar1=-1.0)
        dv.memset(S0[:, :], 0.0)
        dv.memset(S0b[:, :], 0.0)
        dv.memset(vec4[:, :], 0.0)

        def xfm_load(it):
            t0 = it * TT
            xf = xfm2[it % 2]
            def ld(xf=xf, t0=t0):
                for ec in range(8):
                    gp.dma_start(xf[:, ec*512:(ec+1)*512],
                                 xT_in[ec*128:(ec+1)*128, t0:t0+TT])
            return ld

        def t0a_stages(it, skip_load=False):
            t0 = it * TT
            xfm = xfm2[it % 2]
            out = []
            if not skip_load:
                out.append(xfm_load(it))
            for gi, (wsrc, dstf) in enumerate([(W["wq"], qfm), (W["wk"], kfm)]):
                for mc in range(8):
                    if gi == 0 and mc < 2:
                        bank = p1[:, 0:512] if mc == 0 else p1[:, 512:1024]
                    else:
                        bank = pP1[:, :] if mc % 2 == 0 else pP2[:, :]
                    def f(wsrc=wsrc, mc=mc, bank=bank, dstf=dstf):
                        for kc in range(8):
                            pe.matmul(bank,
                                      wsrc[:, kc*1024 + mc*128:kc*1024 + (mc+1)*128],
                                      xfm[:, kc*512:(kc+1)*512],
                                      start=(kc == 0), stop=(kc == 7))
                        pe.flush()
                        dv.tensor_copy(dstf[:, mc*512:(mc+1)*512], bank)
                    out.append(f)
            def kx():
                sy.dma_start(kfm_h[:, t0:t0+TT].rearrange("(mc p) t -> p mc t", p=128),
                             kfm[:, :].rearrange("p (mc t) -> p mc t", t=512))
            out.append(kx)
            return out

        def t0b_stages(it):
            t0 = it * TT
            xfm = xfm2[it % 2]
            out = []
            g, lnb = VV["g"], VV["lnb"]
            lnk, r, ek = VV["lnk"], VV["r"], VV["ek"]
            s1, s2 = VV["s1"], VV["s2"]
            def dt_chain():
                for kc in range(8):
                    pe.matmul(pP1[0:16, :], wa_s[:, kc*16:(kc+1)*16],
                              xfm[:, kc*512:(kc+1)*512], start=(kc == 0), stop=(kc == 7))
                pe.flush()
                dv.tensor_copy(s1[:, :], pP1[0:16, :])
                sc.activation(out=s2[:, :], in_=s1[:, :], func=AF.Abs, bias=SM["dtb"][:, :])
                sc.activation(out=ek[:, :], in_=s2[:, :], func=AF.Exp, scale=-1.0)
                sc.activation(out=s2[:, :], in_=ek[:, :], func=AF.Ln, bias=1.0)
                sc.activation(out=ek[:, :], in_=s1[:, :], func=AF.Relu, bias=SM["dtb"][:, :])
                dv.tensor_tensor(out=s2[:, :], in0=s2[:, :], in1=ek[:, :], op=ALU.add)
                dv.tensor_scalar_mul(out=s1[:, :], in0=s2[:, :], scalar1=SM["negea"][:, :])
                dv.tensor_tensor_scan(out=g[:, :], data0=CN["scanmask"][:, :],
                                      data1=s1[:, :], initial=0.0,
                                      op0=ALU.mult, op1=ALU.add)
            out.append(dt_chain)
            def beta_chain():
                for kc in range(8):
                    pe.matmul(pP2[0:16, :], wb_s[:, kc*16:(kc+1)*16],
                              xfm[:, kc*512:(kc+1)*512], start=(kc == 0), stop=(kc == 7))
                pe.flush()
                dv.tensor_copy(s1[:, :], pP2[0:16, :])
                sc.activation(out=s2[:, :], in_=s1[:, :], func=AF.Abs)
                sc.activation(out=r[:, :], in_=s2[:, :], func=AF.Exp, scale=-1.0)
                sc.activation(out=s2[:, :], in_=r[:, :], func=AF.Ln, bias=1.0)
                dv.tensor_scalar_min(out=r[:, :], in0=s1[:, :], scalar1=0.0)
                dv.tensor_tensor(out=lnb[:, :], in0=r[:, :], in1=s2[:, :], op=ALU.subtract)
            out.append(beta_chain)

            def norms():
                for which, srcf in [("k", kfm), ("q", qfm)]:
                    for mc in range(8):
                        sq = scr1[:, 0:512] if mc % 2 == 0 else scr1[:, 512:1024]
                        sc.activation(out=sq, in_=srcf[:, mc*512:(mc+1)*512],
                                      func=AF.Square)
                        pe.matmul(pP1[0:16, :], CN["ehm"][:, mc*16:(mc+1)*16], sq,
                                  start=(mc == 0), stop=(mc == 7))
                        pe.flush()
                    sc.activation(out=VV["s2"][:, :], in_=pP1[0:16, :], func=AF.Ln)
                    if which == "k":
                        dv.tensor_scalar_mul(out=VV["lnk"][:, :], in0=VV["s2"][:, :],
                                             scalar1=0.5)
                    else:
                        dv.tensor_scalar(out=VV["s1"][:, :], in0=VV["s2"][:, :],
                                         scalar1=0.5, scalar2=float(0.5*np.log(D)),
                                         op0=ALU.mult, op1=ALU.add)
                        dv.tensor_tensor(out=VV["r"][:, :], in0=VV["g"][:, :],
                                         in1=VV["s1"][:, :], op=ALU.subtract)
            out.append(norms)
            def stacks():
                dv.tensor_tensor(out=ST["p"][0:16, :], in0=g[:, :], in1=lnk[:, :],
                                 op=ALU.subtract)
                dv.tensor_tensor(out=ST["n"][0:16, :], in0=g[:, :], in1=lnk[:, :],
                                 op=ALU.add)
                dv.tensor_tensor(out=ST["m"][0:16, :], in0=ST["n"][0:16, :], in1=lnb[:, :],
                                 op=ALU.subtract)
                base = g[0:16, 63:64]
                ge_view = bass.AP(base.tensor, base.offset, [[512, 16], [64, 8], [0, 64]])
                dv.tensor_tensor(out=ek[:, :].rearrange("p (c_ s_) -> p c_ s_", c_=8),
                                 in0=ge_view,
                                 in1=g[:, :].rearrange("p (c_ s_) -> p c_ s_", c_=8),
                                 op=ALU.subtract)
                dv.tensor_tensor(out=ek[:, :], in0=ek[:, :], in1=lnk[:, :], op=ALU.subtract)
                gp.tensor_copy(vec4[0:16, :], ST["p"][0:16, :])
                gp.tensor_copy(vec4[32:48, :], VV["lnb"][:, :])
                gp.tensor_copy(vec4[64:80, :], VV["r"][:, :])
                gp.tensor_copy(vec4[96:112, :], VV["ek"][:, :])
            out.append(stacks)
            return out

        def prep_stages(sl, it):
            t0 = it * TT
            c0, c1 = 2*sl, 2*sl + 1
            MT = TL["MTa"] if sl % 2 == 0 else TL["MTb"]
            PX = TL["P1"] if sl % 2 == 0 else TL["P2"]
            out = []

            def vproj(sl=sl, t0=t0):
                xfm = xfm2[it % 2]
                for nh in range(2):
                    for kc in range(8):
                        pe.matmul(p1[:, nh*512:(nh+1)*512],
                                  xfm[:, kc*512 + sl*128:kc*512 + (sl+1)*128],
                                  W["wv"][:, kc*1024 + nh*512:kc*1024 + (nh+1)*512],
                                  start=(kc == 0), stop=(kc == 7))
                    pe.flush()
                dv.tensor_copy(vtm2[sl % 2][:, :], p1[:, :])
            out.append(vproj)

            def ktx(sl=sl, t0=t0):
                sy.dma_start(ktm[sl % 2][:, :],
                             kfm_h[:, t0+sl*128:t0+(sl+1)*128], transpose=True)
            out.append(ktx)

            def kk():
                order = ([(h, half) for h in range(0, 16, 2) for half in (0, 1)]
                         + [(h, half) for h in range(1, 16, 2) for half in (1, 0)])
                for h, half in order:
                    b = 64 * (h % 2)
                    j = h // 2
                    cch = c0 if half == 0 else c1
                    pe.matmul(p1[64*half:64*half + 64, h*64:(h+1)*64],
                              kfm[b:b+64, j*512 + cch*64:j*512 + (cch+1)*64],
                              kfm[b:b+64, j*512 + cch*64:j*512 + (cch+1)*64],
                              start=True, stop=True, tile_position=(b, 64*half))
                pe.flush()
            out.append(kk)

            def gbuild(lhs_nm, rbvec, mask_nm, rev, kk_ap, dst):
                for half, cch in [(0, c0), (1, c1)]:
                    if rbvec == "r":
                        srcv = VV["r"][0:16, cch*64:(cch+1)*64]
                    else:
                        srcv = ST[rbvec][0:16, cch*64:(cch+1)*64]
                    vw = bass.AP(srcv.tensor, srcv.offset,
                                 [[512, 16], [0, 16], [1, 64]])
                    gp.tensor_tensor(
                        out=RH[32:48, :].rearrange("p (hh s_) -> p hh s_", hh=16),
                        in0=vw,
                        in1=CN["ehead16"][:, :].rearrange("p (hh s_) -> p hh s_", hh=16),
                        op=ALU.mult)
                    for nh in range(2):
                        pe.matmul(p2[64*half:64*half + 64, nh*512:(nh+1)*512],
                                  ST[lhs_nm][:, cch*64:(cch+1)*64],
                                  RH[:, nh*512:(nh+1)*512],
                                  start=True, stop=True, tile_position=(0, 64*half))
                    pe.flush()
                if rev:
                    dv.tensor_tensor(out=t_Gm[:, :], in0=CN[mask_nm][:, :],
                                     in1=p2[:, :], op=ALU.subtract)
                else:
                    dv.tensor_tensor(out=t_Gm[:, :], in0=p2[:, :],
                                     in1=CN[mask_nm][:, :], op=ALU.add)
                sc.activation(out=TL["Eg"][:, 0:512], in_=t_Gm[:, 0:512], func=AF.Exp)
                sc.activation(out=TL["Eg"][:, 512:1024], in_=t_Gm[:, 512:1024],
                              func=AF.Exp)
                for cl in (slice(0, 512), slice(512, 1024)):
                    if dst is MT:
                        dv.tensor_tensor(out=dst[:, cl], in0=kk_ap[:, cl],
                                         in1=TL["Eg"][:, cl], op=ALU.mult)
                    else:
                        dv.scalar_tensor_tensor(out=dst[:, cl], in0=kk_ap[:, cl],
                                                scalar=-1.0, in1=TL["Eg"][:, cl],
                                                op0=ALU.mult, op1=ALU.mult)

            out.append(lambda: gbuild("p", "m", "maskA", False, p1, TL["NA"]))
            out.append(lambda: gbuild("m", "p", "maskAT", True, p1, TL["Z0"]))

            def qkt():
                order = ([(h, half) for h in range(0, 16, 2) for half in (0, 1)]
                         + [(h, half) for h in range(1, 16, 2) for half in (1, 0)])
                for h, half in order:
                    b = 64 * (h % 2)
                    j = h // 2
                    cch = c0 if half == 0 else c1
                    pe.matmul(p1[64*half:64*half + 64, h*64:(h+1)*64],
                              kfm[b:b+64, j*512 + cch*64:j*512 + (cch+1)*64],
                              qfm[b:b+64, j*512 + cch*64:j*512 + (cch+1)*64],
                              start=True, stop=True, tile_position=(b, 64*half))
                pe.flush()
            out.append(qkt)
            out.append(lambda: gbuild("n", "r", "maskMT", True, p1, MT))

            def p0init():
                gp.tensor_tensor(out=TL["P0"][:, :], in0=TL["Z0"][:, :],
                                 in1=CN["identP"][:, :], op=ALU.add)
            out.append(p0init)

            seqs = []
            Bc, Zc, Pc = TL["NA"], TL["Z0"], TL["P0"]
            Bn, Zn, Pn = TL["B1"], TL["Z1"], PX
            for lev in range(1, 6):
                def fb(Bc=Bc, Zc=Zc, Bn=Bn):
                    for h in range(16):
                        for half in (0, 1):
                            b = 64 * half
                            pe.matmul(p1[b:b+64, h*64:(h+1)*64],
                                      Zc[b:b+64, h*64:(h+1)*64],
                                      Bc[b:b+64, h*64:(h+1)*64],
                                      start=True, stop=True, tile_position=(b, b))
                    pe.flush()
                    dv.tensor_copy(Bn[:, 0:512], p1[:, 0:512])
                    dv.tensor_copy(Bn[:, 512:1024], p1[:, 512:1024])
                out.append(fb)
                if lev < 5:
                    def fz(Bc=Bc, Zc=Zc, Zn=Zn):
                        for h in range(16):
                            for half in (0, 1):
                                b = 64 * half
                                pe.matmul(p1[b:b+64, h*64:(h+1)*64],
                                          Bc[b:b+64, h*64:(h+1)*64],
                                          Zc[b:b+64, h*64:(h+1)*64],
                                          start=True, stop=True, tile_position=(b, b))
                        pe.flush()
                        sc.activation(out=Zn[:, 0:512], in_=p1[:, 0:512],
                                      func=AF.Identity)
                        sc.activation(out=Zn[:, 512:1024], in_=p1[:, 512:1024],
                                      func=AF.Identity)
                    out.append(fz)
                def fp(Bn=Bn, Pc=Pc, Pn=Pn):
                    for h in range(16):
                        for half in (0, 1):
                            b = 64 * half
                            pe.matmul(p2[b:b+64, h*64:(h+1)*64],
                                      Bn[b:b+64, h*64:(h+1)*64],
                                      Pc[b:b+64, h*64:(h+1)*64],
                                      start=True, stop=True, tile_position=(b, b))
                    pe.flush()
                    dv.tensor_tensor(out=Pn[:, 0:512], in0=p2[:, 0:512],
                                     in1=Pc[:, 0:512], op=ALU.add)
                    dv.tensor_tensor(out=Pn[:, 512:1024], in0=p2[:, 512:1024],
                                     in1=Pc[:, 512:1024], op=ALU.add)
                out.append(fp)
                Bc, Bn = Bn, (TL["B2"] if Bn is TL["B1"] else TL["B1"])
                Zc, Zn = Zn, (TL["Z2"] if Zn is TL["Z1"] else TL["Z1"])
                Pc, Pn = Pn, (TL["P0"] if Pn is PX else PX)
            return out, Pc, MT

        def seq_stages(sl, Pfin, MT):
            c0, c1 = 2*sl, 2*sl + 1
            out = []

            def s_tr(c0=c0):
                pe.transpose(pS[:, 0:128], vec4[:, c0*64:c0*64 + 128],
                             CN["identF"][:, :])
                pe.flush()
                sc.activation(out=bc4[:, 0:64], in_=pS[:, 0:64], func=AF.Exp)
                sc.activation(out=bc4[:, 64:128], in_=pS[:, 64:128], func=AF.Exp)
            out.append(s_tr)

            for hf, cch in [(0, c0), (1, c1)]:
                def bce(grp, b0=64*hf):
                    b = bc4[b0:b0+64, grp:grp+16]
                    return bass.AP(b.tensor, b.offset, [[128, 64], [1, 16], [0, 64]])

                b0 = 64 * hf
                hs = slice(b0, b0 + 64)
                css = slice(cch*64, (cch+1)*64)

                def s_ks0(cch=cch, b0=b0, hs=hs, css=css, sl=sl, bce=bce):
                    ob = 64 - b0
                    for h in list(range(0, 16, 2)) + [-1] + list(range(1, 16, 2)):
                        if h < 0:
                            pe.matmul(pS[ob:ob+64, 0:64], kfm[64:128, 0:64],
                                      S0b[64:128, 0:64], start=True, stop=True,
                                      tile_position=(64, ob))
                            continue
                        b = 64 * (h % 2)
                        j = h // 2
                        pe.matmul(pS[b0:b0+64, h*64:(h+1)*64],
                                  kfm[b:b+64, j*512 + cch*64:j*512 + (cch+1)*64],
                                  S0b[b:b+64, j*64:(j+1)*64],
                                  start=True, stop=True, tile_position=(b, b0))
                    pe.flush()
                    dv.tensor_tensor(out=TL["t2"][hs, :].rearrange("p (h s_) -> p h s_", s_=64),
                                     in0=pS[hs, :].rearrange("p (h s_) -> p h s_", s_=64),
                                     in1=bce(0), op=ALU.mult)
                    dv.tensor_tensor(out=TL["R"][hs, :],
                                     in0=vtm2[sl % 2][hs, :],
                                     in1=TL["t2"][hs, :], op=ALU.subtract)
                out.append(s_ks0)

                def s_tapply(b0=b0, hs=hs, css=css, Pfin=Pfin, bce=bce):
                    for h in range(16):
                        pe.matmul(pS[b0:b0+64, h*64:(h+1)*64],
                                  Pfin[b0:b0+64, h*64:(h+1)*64],
                                  TL["R"][b0:b0+64, h*64:(h+1)*64],
                                  start=True, stop=True, tile_position=(b0, b0))
                    pe.flush()
                    for ci in (0, 1):
                        cl = slice(ci*512, (ci+1)*512)
                        bv = bce(32)
                        bv = bass.AP(bv.tensor, bv.offset + ci*8,
                                     [[128, 64], [1, 8], [0, 64]])
                        dv.tensor_tensor(out=TL["U"][hs, cl].rearrange("p (h s_) -> p h s_", s_=64),
                                         in0=pS[hs, cl].rearrange("p (h s_) -> p h s_", s_=64),
                                         in1=bv, op=ALU.mult)
                out.append(s_tapply)

                def s_qs0(cch=cch, b0=b0, hs=hs, css=css, bce=bce):
                    ob = 64 - b0
                    for h in list(range(0, 16, 2)) + [-1] + list(range(1, 16, 2)):
                        if h < 0:
                            pe.matmul(pS[ob:ob+64, 0:64], qfm[64:128, 0:64],
                                      S0b[64:128, 0:64], start=True, stop=True,
                                      tile_position=(64, ob))
                            continue
                        b = 64 * (h % 2)
                        j = h // 2
                        pe.matmul(pS[b0:b0+64, h*64:(h+1)*64],
                                  qfm[b:b+64, j*512 + cch*64:j*512 + (cch+1)*64],
                                  S0b[b:b+64, j*64:(j+1)*64],
                                  start=True, stop=True, tile_position=(b, b0))
                    pe.flush()
                    dv.tensor_tensor(out=TL["t2"][hs, :].rearrange("p (h s_) -> p h s_", s_=64),
                                     in0=pS[hs, :].rearrange("p (h s_) -> p h s_", s_=64),
                                     in1=bce(64), op=ALU.mult)
                out.append(s_qs0)

                def s_oapply(b0=b0, hs=hs, sl=sl, MT=MT):
                    for h in range(16):
                        pe.matmul(pS[b0:b0+64, h*64:(h+1)*64],
                                  MT[b0:b0+64, h*64:(h+1)*64],
                                  TL["U"][b0:b0+64, h*64:(h+1)*64],
                                  start=True, stop=True, tile_position=(b0, b0))
                    pe.flush()
                    dv.tensor_tensor(out=otm[sl % 2][hs, :], in0=pS[hs, :],
                                     in1=TL["t2"][hs, :], op=ALU.add)
                out.append(s_oapply)

                def s_bc_ek(cch=cch, b0=b0, hs=hs, css=css, sl=sl, bce=bce):
                    gp.tensor_tensor(out=TL["Ku"][hs, :].rearrange("p (h s_) -> p h s_", s_=64),
                                     in0=ktm[sl % 2][hs, :].rearrange("p (h s_) -> p h s_", s_=64),
                                     in1=bce(96), op=ALU.mult)
                out.append(s_bc_ek)

                def s_supd(cch=cch, b0=b0, hs=hs, css=css, sl=sl):
                    sc.activation(out=sEg1[:, 0:1],
                                  in_=VV["g"][0:16, cch*64+63:cch*64+64], func=AF.Exp)
                    ebase = sEg1[0:16, 0:1]
                    eview = bass.AP(ebase.tensor, ebase.offset,
                                    [[ebase.ap[0][0], 16], [0, 512]])
                    dv.tensor_tensor(out=sEgr[:, :], in0=CN["epair"][:, :],
                                     in1=eview, op=ALU.mult)
                    pe.matmul(pS[:, 0:512], CN["parity"][:, :], sEgr[:, :],
                              start=True, stop=True)
                    pe.flush()
                    for h in range(16):
                        b = 64 * (h % 2)
                        j = h // 2
                        pe.matmul(pS[b:b+64, 512 + j*64:512 + (j+1)*64],
                                  TL["Ku"][b0:b0+64, h*64:(h+1)*64],
                                  TL["U"][b0:b0+64, h*64:(h+1)*64],
                                  start=True, stop=True, tile_position=(b0, b))
                    pe.flush()
                    dv.tensor_tensor(out=sSd[:, :], in0=S0[:, :], in1=pS[:, 0:512],
                                     op=ALU.mult)
                    dv.tensor_tensor(out=S0[:, :], in0=pS[:, 512:1024], in1=sSd[:, :],
                                     op=ALU.add)
                    sc.activation(out=S0b[:, :], in_=S0[:, :], func=AF.Identity)
                out.append(s_supd)
            return out

        def epi1_stages(sl, it):
            t0 = it * TT
            xfm = xfm2[it % 2]
            om = otm[sl % 2]
            out = []
            def rms(om=om, sl=sl, t0=t0):
                gp.dma_start(xbsl[:, :], x_in[t0+sl*128:t0+(sl+1)*128, :])
                sc.activation(out=escr[:, :], in_=om[:, :], func=AF.Square)
                dv.tensor_reduce(out=colsc[:, 0:16],
                                 in_=escr[:, :].rearrange("p (h s_) -> p h s_", s_=64),
                                 axis=AX.X, op=ALU.add)
                dv.tensor_scalar(out=colsc[:, 16:32], in0=colsc[:, 0:16],
                                 scalar1=1.0/64.0, scalar2=1e-6,
                                 op0=ALU.mult, op1=ALU.add)
                sc.activation(out=colsc[:, 32:48], in_=colsc[:, 16:32], func=AF.Ln)
                sc.activation(out=colsc[:, 48:64], in_=colsc[:, 32:48], func=AF.Exp,
                              scale=-0.5)
                rbase = colsc[:, 48:64]
                rstdv = bass.AP(rbase.tensor, rbase.offset,
                                [[rbase.ap[0][0], 128], [1, 16], [0, 64]])
                dv.tensor_tensor(out=scrb[:, :].rearrange("p (h s_) -> p h s_", s_=64),
                                 in0=om[:, :].rearrange("p (h s_) -> p h s_", s_=64),
                                 in1=rstdv, op=ALU.mult)
            out.append(rms)
            def gates(om=om, sl=sl, t0=t0):
                for nh, bank in [(0, pP1), (1, pP2)]:
                    for kc in range(8):
                        pe.matmul(bank[:, :],
                                  xfm[:, kc*512 + sl*128:kc*512 + (sl+1)*128],
                                  W["wgate"][:, kc*1024 + nh*512:kc*1024 + (nh+1)*512],
                                  start=(kc == 0), stop=(kc == 7))
                    pe.flush()
                sc.activation(out=om[:, 0:512], in_=pP1[:, :], func=AF.Silu)
                sc.activation(out=om[:, 512:1024], in_=pP2[:, :], func=AF.Silu)
                gp.tensor_tensor(out=scrb[:, :], in0=scrb[:, :], in1=om[:, :],
                                 op=ALU.mult)
                sy.dma_start(og_h[t0+sl*128:t0+(sl+1)*128, :], scrb[:, :])
                for ec in range(8):
                    sy.dma_start(om[:, ec*128:(ec+1)*128],
                                 og_h[t0+sl*128:t0+(sl+1)*128, ec*128:(ec+1)*128],
                                 transpose=True)
            out.append(gates)
            return out

        def epi2_stages(sl, it):
            t0 = it * TT
            om = otm[sl % 2]
            out = []
            def wout(om=om, sl=sl, t0=t0):
                for nh, bank in [(0, pP1), (1, pP2)]:
                    for kc in range(8):
                        pe.matmul(bank[:, :],
                                  om[:, kc*128:(kc+1)*128],
                                  W["wout"][:, kc*1024 + nh*512:kc*1024 + (nh+1)*512],
                                  start=(kc == 0), stop=(kc == 7))
                    pe.flush()
                dv.tensor_tensor(out=escr[:, 0:512], in0=pP1[:, :],
                                 in1=xbsl[:, 0:512], op=ALU.add)
                dv.tensor_tensor(out=escr[:, 512:1024], in0=pP2[:, :],
                                 in1=xbsl[:, 512:1024], op=ALU.add)
                sc.activation(out=scrb[:, :], in_=escr[:, :], func=AF.Square,
                              accum_out=colsc[:, 0:1])
                dv.tensor_reduce(out=colsc[:, 1:2], in_=escr[:, :], axis=AX.X,
                                 op=ALU.add)
                dv.tensor_scalar_mul(out=colsc[:, 2:3], in0=colsc[:, 1:2],
                                     scalar1=1.0/1024.0)
                dv.tensor_scalar_mul(out=colsc[:, 3:4], in0=colsc[:, 2:3],
                                     scalar1=colsc[:, 2:3])
                dv.tensor_scalar(out=colsc[:, 4:5], in0=colsc[:, 0:1],
                                 scalar1=1.0/1024.0, scalar2=1e-5,
                                 op0=ALU.mult, op1=ALU.add)
                dv.tensor_tensor(out=colsc[:, 4:5], in0=colsc[:, 4:5],
                                 in1=colsc[:, 3:4], op=ALU.subtract)
                sc.activation(out=colsc[:, 5:6], in_=colsc[:, 4:5], func=AF.Ln)
                sc.activation(out=colsc[:, 6:7], in_=colsc[:, 5:6], func=AF.Exp,
                              scale=-0.5)
                dv.tensor_scalar(out=scrb[:, :], in0=escr[:, :],
                                 scalar1=colsc[:, 2:3], scalar2=colsc[:, 6:7],
                                 op0=ALU.subtract, op1=ALU.mult)
                sy.dma_start(x2_h[t0+sl*128:t0+(sl+1)*128, :], scrb[:, :])
            out.append(wout)
            return out

        def weave(A, B):
            if not A:
                for f in B:
                    f()
                return
            nb, na = len(B), len(A)
            bi = 0
            for i, fa in enumerate(A):
                want = ((i + 1) * nb) // na
                while bi < want:
                    B[bi]()
                    bi += 1
                fa()
            while bi < nb:
                B[bi]()
                bi += 1

        prep_next = None
        for it in range(_N_TILES):
            for f in t0a_stages(it, skip_load=(it > 0)):
                f()
            if it == 0:
                for nm in ["wv", "wgate", "wout"]:
                    gp.dma_start(W[nm][:, :].rearrange("p (kc n) -> p kc n", n=1024),
                                 w_in[nm].rearrange("(kc p) n -> p kc n", p=128))
                gp.dma_start(wa_s[:, :].rearrange("p (kc c_) -> p kc c_", c_=16),
                             wa_in.rearrange("(kc p) n -> p kc n", p=128))
                gp.dma_start(wb_s[:, :].rearrange("p (kc c_) -> p kc c_", c_=16),
                             wb_in.rearrange("(kc p) n -> p kc n", p=128))
                for kc in range(8):
                    dv.tensor_copy(scrb[:, :], W["wout"][:, kc*1024:(kc+1)*1024])
                    dv.tensor_scalar_mul(out=W["wout"][:, kc*1024:(kc+1)*1024],
                                         in0=scrb[:, :], scalar1=rmswc[:, :])
            if it > 0:
                for f in epi2_stages(3, it - 1):
                    f()
            stg, Pfin, MT = prep_stages(0, it)
            weave(t0b_stages(it), stg[0:3])
            for f in stg[3:]:
                f()
            for sl in range(NSL):
                QB = []
                if sl >= 1:
                    QB += epi1_stages(sl - 1, it)
                if sl < 3:
                    stg2, Pfin2, MT2 = prep_stages(sl + 1, it)
                    QB += stg2
                elif it + 1 < _N_TILES:
                    QB.append(xfm_load(it + 1))
                weave(seq_stages(sl, Pfin, MT), QB)
                if it == _N_TILES - 1 and sl == 2:
                    P2star = cnt["pe"]
                if sl >= 1:
                    for f in epi2_stages(sl - 1, it):
                        f()
                if sl < 3:
                    Pfin, MT = Pfin2, MT2
            for f in epi1_stages(3, it):
                f()
            if it == _N_TILES - 1:
                for f in epi2_stages(3, it):
                    f()

        dve_A, act_A, pool_A, pe_A = cnt["dve"], cnt["act"], cnt["pool"], cnt["pe"]
        ldA_A = {s: cnt[s] for s in QSEMS["sy"]}
        ldB_A = {s: cnt[s] for s in QSEMS["gpL"]}
        wait("pe", "dve", dve_A)
        wait("pe", "act", act_A)
        inc(op("pe", "nop"), "bar", 1)
        wait("dv", "pe", pe_A)
        inc(op("dv", "engine_nop"), "bar", 1)
        wait("sc", "dve", dve_A)
        inc(op("sc", "nop"), "bar", 1)
        for s, v in ldB_A.items():
            wait("gp", s, v)
        wait("gp", "pool", pool_A)
        inc(op("gp", "engine_nop"), "bar", 1)
        for s, v in ldA_A.items():
            wait("sy", s, v)
        inc(op("sy", "nop"), "bar", 1)
        BAR = cnt["bar"]

        emit_all()
        stA.close()

        stB = contextlib.ExitStack()
        sbB = lambda n, s_, d: stB.enter_context(nc.sbuf_tensor(n, s_, d))
        wgT = sbB("wgT", [128, 8*3072], BF16)
        wuT = sbB("wuT", [128, 8*3072], BF16)
        wdT = sbB("wdT", [128, 24*1024], BF16)
        x2fm = sbB("x2fm", [128, 8*512], BF16)
        x2tm = sbB("x2tm", [128, 4*1024], BF16)
        h_s = sbB("h_s", [128, 24*512], BF16)
        b_scr1 = sbB("b_scr1", [128, 1024], F32)
        b_scr2 = sbB("b_scr2", [128, 1024], F32)
        b_scrb = sbB("b_scrb", [128, 512], BF16)
        b_col = sbB("b_col", [128, 16], F32)
        bpA = stB.enter_context(nc.psum_tensor("bpA", [128, 512], F32))
        bpB = stB.enter_context(nc.psum_tensor("bpB", [128, 512], F32))
        bpC = stB.enter_context(nc.psum_tensor("bpC", [128, 512], F32))
        bpD = stB.enter_context(nc.psum_tensor("bpD", [128, 512], F32))

        if _PHASEB_ON:
            wait("gp", "pe", P2star)
            for ec in range(8):
                gp.dma_start(wgT[:, ec*3072:(ec+1)*3072],
                             fgT_in[ec*128:(ec+1)*128, :])
            wait("gp", "pe", pe_A)
            wait("gp", "dve", dve_A)
            for ec in range(8):
                gp.dma_start(wuT[:, ec*3072:(ec+1)*3072],
                             fuT_in[ec*128:(ec+1)*128, :])
            wait("gp", "bar", BAR)

        for it in range(NT if _PHASEB_ON else 0):
            t0 = it * TT
            wait("sy", "bar", BAR)
            for ec in range(8):
                sy.dma_start(x2fm[:, ec*512:(ec+1)*512],
                             x2_h[t0:t0+TT, ec*128:(ec+1)*128], transpose=True)
            gp.dma_start(x2tm[:, :].rearrange("p (sl e) -> p sl e", e=1024),
                         x2_h[t0:t0+TT, :].rearrange("(sl p) e -> p sl e", p=128))
            if it == 0:
                for hc in range(24):
                    gp.dma_start(wdT[:, hc*1024:(hc+1)*1024],
                                 fdT_in[hc*128:(hc+1)*128, :])

            for hc in range(24):
                for ec in range(8):
                    pe.matmul(bpA[:, :], wgT[:, ec*3072 + hc*128:ec*3072 + (hc+1)*128],
                              x2fm[:, ec*512:(ec+1)*512], start=(ec == 0), stop=(ec == 7))
                pe.flush()
                for ec in range(8):
                    pe.matmul(bpB[:, :], wuT[:, ec*3072 + hc*128:ec*3072 + (hc+1)*128],
                              x2fm[:, ec*512:(ec+1)*512], start=(ec == 0), stop=(ec == 7))
                pe.flush()
                sc.activation(out=b_scrb[:, :], in_=bpA[:, :], func=AF.Silu, scale=RQ)
                dv.scalar_tensor_tensor(out=h_s[:, hc*512:(hc+1)*512], in0=bpB[:, :],
                                        scalar=RQ*RQ, in1=b_scrb[:, :],
                                        op0=ALU.mult, op1=ALU.mult)

            for sl in range(NSL):
                for nh, bank in [(0, bpC), (1, bpD)]:
                    for hc in range(24):
                        pe.matmul(bank[:, :],
                                  h_s[:, hc*512 + sl*128:hc*512 + (sl+1)*128],
                                  wdT[:, hc*1024 + nh*512:hc*1024 + (nh+1)*512],
                                  start=(hc == 0), stop=(hc == 23))
                    pe.flush()
                dv.tensor_tensor(out=b_scr1[:, 0:512], in0=bpC[:, :],
                                 in1=x2tm[:, sl*1024:sl*1024 + 512], op=ALU.add)
                dv.tensor_tensor(out=b_scr1[:, 512:1024], in0=bpD[:, :],
                                 in1=x2tm[:, sl*1024 + 512:(sl+1)*1024], op=ALU.add)
                sc.activation(out=b_scr2[:, :], in_=b_scr1[:, :], func=AF.Square,
                              accum_out=b_col[:, 0:1])
                dv.tensor_reduce(out=b_col[:, 1:2], in_=b_scr1[:, :], axis=AX.X,
                                 op=ALU.add)
                dv.tensor_scalar_mul(out=b_col[:, 2:3], in0=b_col[:, 1:2],
                                     scalar1=1.0/1024.0)
                dv.tensor_scalar_mul(out=b_col[:, 3:4], in0=b_col[:, 2:3],
                                     scalar1=b_col[:, 2:3])
                dv.tensor_scalar(out=b_col[:, 4:5], in0=b_col[:, 0:1],
                                 scalar1=1.0/1024.0, scalar2=1e-5,
                                 op0=ALU.mult, op1=ALU.add)
                dv.tensor_tensor(out=b_col[:, 4:5], in0=b_col[:, 4:5],
                                 in1=b_col[:, 3:4], op=ALU.subtract)
                sc.activation(out=b_col[:, 5:6], in_=b_col[:, 4:5], func=AF.Ln)
                sc.activation(out=b_col[:, 6:7], in_=b_col[:, 5:6], func=AF.Exp,
                              scale=-0.5)
                dv.tensor_scalar(out=b_scr2[:, :], in0=b_scr1[:, :],
                                 scalar1=b_col[:, 2:3], scalar2=b_col[:, 6:7],
                                 op0=ALU.subtract, op1=ALU.mult)
                DM_("gp", "gpS").dma_start(out_t[t0+sl*128:t0+(sl+1)*128, :],
                                           b_scr2[:, :])

        for s in QSEMS["gpS"]:
            wait("gp", s, cnt[s])

        emit_all()
        stB.close()

    return nc


def _shard_inputs(inputs):
    cns = host_consts()
    rmsw = np.asarray(inputs["L0_attn_rms_w"])
    base = {
        "wq": np.ascontiguousarray(inputs["L0_attn_Wq"]),
        "wk": np.ascontiguousarray(inputs["L0_attn_Wk"]),
        "wv": np.ascontiguousarray(inputs["L0_attn_Wv"]),
        "wgate": np.ascontiguousarray(inputs["L0_attn_Wgate"]),
        "wout": np.ascontiguousarray(inputs["L0_attn_Wout"]),
        "wa": np.ascontiguousarray(inputs["L0_attn_Walpha"]),
        "wb": np.ascontiguousarray(inputs["L0_attn_Wbeta"]),
        "alog": np.ascontiguousarray(np.asarray(inputs["L0_attn_A_log"]).reshape(16, 1)),
        "dtb": np.ascontiguousarray(np.asarray(inputs["L0_attn_dt_bias"]).reshape(16, 1)),
        "rmsw_c": np.ascontiguousarray(np.tile(rmsw, 2).reshape(128, 1).astype(np.float32)),
        "fgT": np.ascontiguousarray(np.asarray(inputs["L0_ffn_w_gate"]).T),
        "fuT": np.ascontiguousarray(np.asarray(inputs["L0_ffn_w_up"]).T),
        "fdT": np.ascontiguousarray(np.asarray(inputs["L0_ffn_w_down"]).T),
    }
    for k, v in cns.items():
        base[k] = v
    x = np.asarray(inputs["L0_attn_x"])
    in_maps = []
    for b in range(8):
        m = dict(base)
        m["x"] = np.ascontiguousarray(x[b])
        m["xT"] = np.ascontiguousarray(x[b].T)
        in_maps.append(m)
    return in_maps


def kernel(**inputs):
    in_maps = _shard_inputs(inputs)
    nc = build_program()
    res = run_bass_kernel_spmd(nc, in_maps, core_ids=list(range(8)))
    return np.stack([r["out"] for r in res.results], axis=0)


# version 20 (final)
# speedup vs baseline: 1.0440x; 1.0131x over previous
import contextlib
import numpy as np
import ml_dtypes
import concourse.bass as bass
import concourse.mybir as mybir
from concourse.bass_utils import run_bass_kernel_spmd

F32 = mybir.dt.float32
BF16 = mybir.dt.bfloat16
I32 = mybir.dt.int32
AF = mybir.ActivationFunctionType
ALU = mybir.AluOpType
AX = mybir.AxisListType
bfdt = ml_dtypes.bfloat16

S, E, H, D, C = 4096, 1024, 16, 64, 64
HID = 3072
TT = 512
NT = S // TT
NSL = TT // 128
NCH = TT // C
NEG = -1.0e30
RQ = 1.0 / 127.0

_DEBUG_TAPS = None
_N_TILES = NT
_CHUNK_ON = True
_CHUNK_SUB = 9
_SEQ_ON = True
_EPI_ON = True
_PHASEB_ON = True


def host_consts():
    c = {}
    eh = np.zeros((16, 1024), np.float32)
    for h in range(16):
        eh[h, h*64:(h+1)*64] = 1.0
    c["ehead16"] = eh.astype(bfdt)
    lt = np.zeros((48, 512), np.float32)
    lt[32:48] = -1.0
    c["lhs_tmpl"] = lt
    rt = np.zeros((48, 1024), np.float32)
    rt[0:16] = eh
    c["rhs_tmpl"] = rt
    t_idx = (np.arange(128) % 64)[:, None].astype(np.float32)
    s_idx = (np.arange(1024) % 64)[None, :].astype(np.float32)
    c["maskA"] = np.where(s_idx < t_idx, 0.0, NEG).astype(bfdt)
    c["maskAT"] = np.where(t_idx < s_idx, 0.0, NEG).astype(bfdt)
    c["maskMT"] = np.where(t_idx <= s_idx, 0.0, NEG).astype(bfdt)
    c["identP"] = np.where(s_idx == t_idx, 1.0, 0.0).astype(bfdt)
    sm = np.ones((16, 512), np.float32)
    sm[:, ::C] = 0.0
    c["scanmask"] = sm.astype(bfdt)
    ehm = np.zeros((128, 128), np.float32)
    for mc in range(8):
        ehm[0:64, mc*16 + 2*mc] = 1.0
        ehm[64:128, mc*16 + 2*mc + 1] = 1.0
    c["ehm"] = ehm
    c["identF"] = np.eye(128, dtype=np.float32)
    par = np.zeros((16, 128), np.float32)
    par[0::2, 0:64] = 1.0
    par[1::2, 64:128] = 1.0
    c["parity"] = par
    ep = np.zeros((16, 512), np.float32)
    for hp in range(16):
        ep[hp, (hp//2)*64:(hp//2+1)*64] = 1.0
    c["epair"] = ep.astype(bfdt)
    return c


_POS_NAMES = {
    "tensor_copy": ("out", "in_"),
    "tensor_tensor": ("out", "in0", "in1"),
    "tensor_scalar": ("out", "in0", "scalar1", "scalar2"),
    "tensor_scalar_mul": ("out", "in0", "scalar1"),
    "tensor_scalar_add": ("out", "in0", "scalar1"),
    "tensor_scalar_sub": ("out", "in0", "scalar1"),
    "tensor_scalar_min": ("out", "in0", "scalar1"),
    "tensor_scalar_max": ("out", "in0", "scalar1"),
    "scalar_tensor_tensor": ("out", "in0", "scalar", "in1"),
    "tensor_reduce": ("out", "in_"),
    "reciprocal": ("out", "in_"),
    "memset": ("out", "constant"),
    "tensor_tensor_scan": ("out", "data0", "data1", "initial"),
    "activation": ("out", "in_", "func", "bias", "scale", "alpha", "accum_out"),
    "copy": ("out", "in_"),
}
_WRITE_KEYS = {"out", "accum_out", "dest", "ap"}
_OPNAMES = set(_POS_NAMES)


def build_program():
    nc = bass.Bass("TRN2", target_bir_lowering=False)

    x_in = nc.dram_tensor("x", [S, E], F32, kind="ExternalInput").ap()
    xT_in = nc.dram_tensor("xT", [E, S], F32, kind="ExternalInput").ap()
    w_in = {}
    for nm in ["wq", "wk", "wv", "wgate", "wout"]:
        w_in[nm] = nc.dram_tensor(nm, [E, E], F32, kind="ExternalInput").ap()
    wa_in = nc.dram_tensor("wa", [E, 16], F32, kind="ExternalInput").ap()
    wb_in = nc.dram_tensor("wb", [E, 16], F32, kind="ExternalInput").ap()
    alog_in = nc.dram_tensor("alog", [16, 1], F32, kind="ExternalInput").ap()
    dtb_in = nc.dram_tensor("dtb", [16, 1], F32, kind="ExternalInput").ap()
    rmsw_in = nc.dram_tensor("rmsw_c", [128, 1], F32, kind="ExternalInput").ap()
    fgT_in = nc.dram_tensor("fgT", [E, HID], I32, kind="ExternalInput").ap()
    fuT_in = nc.dram_tensor("fuT", [E, HID], I32, kind="ExternalInput").ap()
    fdT_in = nc.dram_tensor("fdT", [HID, E], I32, kind="ExternalInput").ap()
    cns_in = {}
    for nm, shp, dt in [("ehead16", [16, 1024], BF16), ("lhs_tmpl", [48, 512], F32),
                        ("rhs_tmpl", [48, 1024], F32), ("maskA", [128, 1024], BF16),
                        ("maskAT", [128, 1024], BF16), ("maskMT", [128, 1024], BF16),
                        ("identP", [128, 1024], BF16), ("scanmask", [16, 512], BF16),
                        ("ehm", [128, 128], F32), ("parity", [16, 128], F32),
                        ("epair", [16, 512], BF16), ("identF", [128, 128], F32)]:
        cns_in[nm] = nc.dram_tensor(nm, shp, dt, kind="ExternalInput").ap()
    out_t = nc.dram_tensor("out", [S, E], F32, kind="ExternalOutput").ap()

    kfm_h = nc.dram_tensor("kfm_h", [E, S], BF16, kind="Internal").ap()
    og_h = nc.dram_tensor("og_h", [S, E], BF16, kind="Internal").ap()
    x2_h = nc.dram_tensor("x2_h", [S, E], BF16, kind="Internal").ap()

    sems, cnt, wm = {}, {}, {}
    WRT, RDS = {}, {}
    SCRIPT = {k: [] for k in ("sy", "gp", "pe", "dv", "sc")}

    def op(ename, method, *args, **kwargs):
        item = ["op", method, args, kwargs, []]
        SCRIPT[ename].append(item)
        return item

    def inc(item, name, step=1):
        item[4].append((name, step))
        cnt[name] += step

    def wait(ename, name, val):
        if val <= 0:
            return
        if wm.get((ename, name), -1) >= val:
            return
        SCRIPT[ename].append(("w", name, val))
        wm[(ename, name)] = val

    _REGIONAL = {"t_U", "t_NA", "t_Z0", "t_B1", "t_Z1", "t_P0", "t_P1", "t_P2",
                 "t_MTa", "t_MTb", "t_R", "kfm", "qfm", "bc4"}

    def regkeys(ap):
        name = ap.tensor.name
        if name not in _REGIONAL:
            return (name,)
        dims = ap.ap
        W = dims[0][0]
        pb = ap.offset // W
        pc = dims[0][1]
        cb = ap.offset % W
        ext = 1
        for st_, sz_ in dims[1:]:
            ext += (sz_ - 1) * abs(st_)
        phs = set()
        if pb < 64:
            phs.add(0)
        if pb + pc > 64:
            phs.add(1)
        chs = set()
        half = W // 2
        if cb < half:
            chs.add(0)
        if cb + ext > half:
            chs.add(1)
        return tuple(f"{name}/{p}{c}" for p in phs for c in chs)

    def dep_wait(ename, wnames, rnames):
        for n in rnames | wnames:
            t = WRT.get(n)
            if t:
                wait(ename, t[0], t[1])
        for n in wnames:
            for s_, v_ in RDS.get(n, {}).items():
                wait(ename, s_, v_)

    def dep_record(sem, val, wnames, rnames):
        for n in wnames:
            WRT[n] = (sem, val)
            RDS[n] = {}
        for n in rnames:
            d = RDS.setdefault(n, {})
            if d.get(sem, -1) < val:
                d[sem] = val

    def classify(opname, args, kwargs):
        wnames, rnames = set(), set()
        names = _POS_NAMES.get(opname, ())
        items = list(zip(names, args)) + list(kwargs.items())
        for k, v in items:
            if isinstance(v, bass.AP):
                (wnames if k in _WRITE_KEYS else rnames).update(regkeys(v))
        return wnames, rnames

    _pe_win = []
    _pe_gid = [0]
    _pe_idx = [0]

    class P_:
        def __init__(self, ename, sem):
            self._ename = ename
            self._sem = sem

        def __getattr__(self, name):
            if name in _OPNAMES:
                def wrapped(*a, **k):
                    wn, rn = classify(name, a, k)
                    dep_wait(self._ename, wn, rn)
                    item = op(self._ename, name, *a, **k)
                    inc(item, self._sem, 1)
                    dep_record(self._sem, cnt[self._sem], wn, rn)
                    return item
                return wrapped
            raise AttributeError(name)

    class PE_:
        def __init__(self):
            self._pw, self._pr = set(), set()
            self._last = None

        def matmul(self, out, lhsT, rhs, **k):
            wn = set(regkeys(out))
            rn = set(regkeys(lhsT)) | set(regkeys(rhs))
            dep_wait("pe", wn, rn)
            oF = out.ap[0][0]
            pb, pc = out.offset // oF, out.ap[0][1]
            cb = out.offset % oF
            ncol = 1
            for _, sz in out.ap[1:]:
                ncol *= sz
            lF = lhsT.ap[0][0]
            rb, rc = lhsT.offset // lF, lhsT.ap[0][1]
            banks = set(range(cb // 512, (cb + ncol - 1) // 512 + 1))
            gid = _pe_gid[0]
            idx = _pe_idx[0]
            _pe_idx[0] += 1
            keep, drain = [], False
            for X in _pe_win:
                xt, xp0, xp1, xr0, xr1, xb, xg, xi = X
                if xi < idx - 4:
                    continue
                if not (xr1 <= rb or rb + rc <= xr0):
                    continue
                if (xt == out.tensor.name and not (xp1 <= pb or pb + pc <= xp0)
                        and (xb & banks)):
                    if xg != gid:
                        drain = True
                    else:
                        raise RuntimeError(
                            f"intra-group PE hazard on {xt} parts {pb}:{pb+pc}")
                keep.append(X)
            if drain:
                wait("pe", "pe", cnt["pe"])
                keep = []
            del _pe_win[:]
            _pe_win.extend(keep)
            _pe_win.append((out.tensor.name, pb, pb + pc, rb, rb + rc, banks, gid, idx))
            item = op("pe", "matmul", out, lhsT, rhs, **k)
            self._last = item
            self._pw |= wn
            self._pr |= rn
            return item

        def transpose(self, out, in_, identity, **k):
            wn = set(regkeys(out))
            rn = set(regkeys(in_)) | set(regkeys(identity))
            dep_wait("pe", wn, rn)
            oF = out.ap[0][0]
            pb, pc = out.offset // oF, out.ap[0][1]
            cb = out.offset % oF
            ncol = 1
            for _, sz in out.ap[1:]:
                ncol *= sz
            banks = set(range(cb // 512, (cb + ncol - 1) // 512 + 1))
            item = op("pe", "transpose", out, in_, identity, **k)
            self._last = item
            del _pe_win[:]
            _pe_win.append((out.tensor.name, pb, pb + pc, 0, 128, banks,
                            _pe_gid[0], _pe_idx[0]))
            _pe_idx[0] += 1
            self._pw |= wn
            self._pr |= rn
            return item

        def flush(self):
            if self._last is None:
                return
            inc(self._last, "pe", 1)
            dep_record("pe", cnt["pe"], self._pw, self._pr)
            self._pw, self._pr = set(), set()
            self._last = None
            _pe_gid[0] += 1

    QSEMS = {"sy": ["ldA0", "ldA1", "ldA2", "ldA3"],
             "gpL": ["ldB0", "ldB1", "ldB2"],
             "gpS": ["so0", "so1"]}
    qidx = {k: 0 for k in QSEMS}

    class DM_:
        def __init__(self, ename, qkey):
            self._ename = ename
            self._qkey = qkey

        def dma_start(self, out, in_, **k):
            grp = QSEMS[self._qkey]
            sem = grp[qidx[self._qkey] % len(grp)]
            qidx[self._qkey] += 1
            if cnt[sem] > 0:
                wait(self._ename, sem, cnt[sem])
            wn, rn = set(regkeys(out)), set(regkeys(in_))
            dep_wait(self._ename, wn, rn)
            item = op(self._ename, "dma_start", out=out, in_=in_, **k)
            inc(item, sem, 16)
            dep_record(sem, cnt[sem], wn, rn)
            return item

    class DMGP:
        def __init__(self):
            self._dma = DM_("gp", "gpL")
            self._cmp = P_("gp", "pool")

        def dma_start(self, out, in_, **k):
            return self._dma.dma_start(out, in_, **k)

        def __getattr__(self, name):
            if name in _OPNAMES:
                return getattr(self._cmp, name)
            raise AttributeError(name)

    with contextlib.ExitStack() as st:
        for s in ["ldA0", "ldA1", "ldA2", "ldA3", "ldB0", "ldB1", "ldB2",
                  "so0", "so1", "pe", "dve", "act", "pool", "bar"]:
            sems[s] = st.enter_context(nc.semaphore(s))
            cnt[s] = 0

        blk = st.enter_context(nc.Block())

        sy, gp = DM_("sy", "sy"), DMGP()
        pe, dv, sc = PE_(), P_("dv", "dve"), P_("sc", "act")

        def replay(eng, script):
            for it_ in script:
                if it_[0] == "w":
                    eng.wait_ge(sems[it_[1]], it_[2])
                else:
                    _, name, args, kwargs, incs = it_
                    instr = getattr(eng, name)(*args, **kwargs)
                    for s_, step in incs:
                        instr.then_inc(sems[s_], step)

        def emit_all():
            blk.sync(lambda e: replay(e, SCRIPT["sy"]))
            blk.gpsimd(lambda e: replay(e, SCRIPT["gp"]))
            blk.tensor(lambda e: replay(e, SCRIPT["pe"]))
            blk.vector(lambda e: replay(e, SCRIPT["dv"]))
            blk.scalar(lambda e: replay(e, SCRIPT["sc"]))
            for k in SCRIPT:
                SCRIPT[k] = []

        def tap(name, sb_ap, shape, dtype=F32):
            if not _DEBUG_TAPS or name not in _DEBUG_TAPS:
                return
            dr = nc.dram_tensor("tap_" + name, shape, dtype, kind="ExternalOutput").ap()
            DM_("gp", "gpS").dma_start(dr, sb_ap)

        stA = contextlib.ExitStack()
        sbA = lambda n, s_, d: stA.enter_context(nc.sbuf_tensor(n, s_, d))
        W = {}
        for nm in ["wq", "wk", "wv", "wgate", "wout"]:
            W[nm] = sbA(nm + "_s", [128, 8*1024], BF16)
        wa_s = sbA("wa_s", [128, 8*16], BF16)
        wb_s = sbA("wb_s", [128, 8*16], BF16)
        CN = {}
        for nm, shp, dt in [("ehead16", [16, 1024], BF16),
                            ("maskA", [128, 1024], BF16), ("maskAT", [128, 1024], BF16),
                            ("maskMT", [128, 1024], BF16), ("identP", [128, 1024], BF16),
                            ("scanmask", [16, 512], BF16), ("ehm", [128, 128], F32),
                            ("parity", [16, 128], F32), ("epair", [16, 512], BF16),
                            ("identF", [128, 128], F32)]:
            CN[nm] = sbA("c_" + nm, shp, dt)
        ST = {}
        for nm in ["p", "m", "n"]:
            ST[nm] = sbA("st_" + nm, [48, 512], F32)
        RH = sbA("rh", [48, 1024], F32)
        rmswc = sbA("rmswc", [128, 1], F32)
        SM = {}
        for nm in ["alog", "dtb", "negea"]:
            SM[nm] = sbA("s_" + nm, [16, 1], F32)
        S0 = sbA("S0", [128, 512], F32)
        S0b = sbA("S0b", [128, 512], BF16)
        xbsl = sbA("xbsl", [128, 1024], BF16)
        xfm2 = [sbA("xfmA", [128, 8*512], BF16), sbA("xfmB", [128, 8*512], BF16)]
        qfm = sbA("qfm", [128, 8*512], BF16)
        kfm = sbA("kfm", [128, 8*512], BF16)
        ktm = [sbA("ktmA", [128, 1024], BF16), sbA("ktmB", [128, 1024], BF16)]
        vtm2 = [sbA("vtmA", [128, 1024], BF16), sbA("vtmB", [128, 1024], BF16)]
        otm = [sbA("otmA", [128, 1024], BF16), sbA("otmB", [128, 1024], BF16)]
        scr1 = sbA("scr1", [128, 1024], F32)
        escr = sbA("escr", [128, 1024], F32)
        scrb = sbA("scrb", [128, 1024], BF16)
        colsc = sbA("colsc", [128, 64], F32)
        sSd = sbA("sSd", [128, 512], F32)
        vec4 = sbA("vec4", [128, 512], F32)
        bc4 = sbA("bc4", [128, 128], F32)
        sEg1 = sbA("sEg1", [16, 1], F32)
        VV = {}
        for nm in ["g", "lnk", "lnb", "r", "ek", "s1", "s2"]:
            VV[nm] = sbA("v_" + nm, [16, 512], F32)
        sEgr = VV["s2"]
        TL = {}
        for nm in ["NA", "Z0", "B1", "Z1", "P0", "P1", "P2",
                   "MTa", "MTb", "E", "R", "U"]:
            TL[nm] = sbA("t_" + nm, [128, 1024], BF16)
        TL["B2"] = TL["NA"]
        TL["Z2"] = TL["Z0"]
        TL["t2"] = TL["E"]
        TL["Ku"] = TL["R"]
        TL["Eg"] = TL["B1"]
        t_Gm = scr1

        p1 = stA.enter_context(nc.psum_tensor("p1", [128, 1024], F32))
        p2 = stA.enter_context(nc.psum_tensor("p2", [128, 1024], F32))
        pS = stA.enter_context(nc.psum_tensor("pS", [128, 1024], F32))
        pP1 = stA.enter_context(nc.psum_tensor("pP1", [128, 512], F32))
        pP2 = stA.enter_context(nc.psum_tensor("pP2", [128, 512], F32))

        for nm in ["wq", "wk"]:
            gp.dma_start(W[nm][:, :].rearrange("p (kc n) -> p kc n", n=1024),
                         w_in[nm].rearrange("(kc p) n -> p kc n", p=128))
        for nm in ["ehead16", "maskA", "maskAT", "maskMT", "identP", "scanmask",
                   "ehm", "parity", "epair", "identF"]:
            sy.dma_start(CN[nm][:, :], cns_in[nm])
        for nm in ["p", "m", "n"]:
            sy.dma_start(ST[nm][:, :], cns_in["lhs_tmpl"])
        sy.dma_start(RH[:, :], cns_in["rhs_tmpl"])
        sy.dma_start(rmswc[:, :], rmsw_in)
        sy.dma_start(SM["alog"][:, :], alog_in)
        sy.dma_start(SM["dtb"][:, :], dtb_in)

        sc.activation(out=SM["negea"][:, :], in_=SM["alog"][:, :], func=AF.Exp)
        dv.tensor_scalar_mul(out=SM["negea"][:, :], in0=SM["negea"][:, :], scalar1=-1.0)
        dv.memset(S0[:, :], 0.0)
        dv.memset(S0b[:, :], 0.0)
        dv.memset(vec4[:, :], 0.0)

        def xfm_load(it):
            t0 = it * TT
            xf = xfm2[it % 2]
            def ld(xf=xf, t0=t0):
                gp.dma_start(xf[:, :].rearrange("p (ec t) -> p ec t", t=TT),
                             xT_in[:, t0:t0+TT].rearrange("(ec p) t -> p ec t", p=128))
            return ld

        def t0a_stages(it, skip_load=False):
            t0 = it * TT
            xfm = xfm2[it % 2]
            out = []
            if not skip_load:
                out.append(xfm_load(it))
            for gi, (wsrc, dstf) in enumerate([(W["wq"], qfm), (W["wk"], kfm)]):
                for mc in range(8):
                    if gi == 0 and mc < 2:
                        bank = p1[:, 0:512] if mc == 0 else p1[:, 512:1024]
                    else:
                        bank = pP1[:, :] if mc % 2 == 0 else pP2[:, :]
                    def f(wsrc=wsrc, mc=mc, bank=bank, dstf=dstf):
                        for kc in range(8):
                            pe.matmul(bank,
                                      wsrc[:, kc*1024 + mc*128:kc*1024 + (mc+1)*128],
                                      xfm[:, kc*512:(kc+1)*512],
                                      start=(kc == 0), stop=(kc == 7))
                        pe.flush()
                        dv.tensor_copy(dstf[:, mc*512:(mc+1)*512], bank)
                    out.append(f)
            def kx():
                sy.dma_start(kfm_h[:, t0:t0+TT].rearrange("(mc p) t -> p mc t", p=128),
                             kfm[:, :].rearrange("p (mc t) -> p mc t", t=512))
            out.append(kx)
            return out

        def t0b_stages(it):
            t0 = it * TT
            xfm = xfm2[it % 2]
            out = []
            g, lnb = VV["g"], VV["lnb"]
            lnk, r, ek = VV["lnk"], VV["r"], VV["ek"]
            s1, s2 = VV["s1"], VV["s2"]
            def dt_chain():
                for kc in range(8):
                    pe.matmul(pP1[0:16, :], wa_s[:, kc*16:(kc+1)*16],
                              xfm[:, kc*512:(kc+1)*512], start=(kc == 0), stop=(kc == 7))
                pe.flush()
                dv.tensor_copy(s1[:, :], pP1[0:16, :])
                sc.activation(out=s2[:, :], in_=s1[:, :], func=AF.Abs, bias=SM["dtb"][:, :])
                sc.activation(out=ek[:, :], in_=s2[:, :], func=AF.Exp, scale=-1.0)
                sc.activation(out=s2[:, :], in_=ek[:, :], func=AF.Ln, bias=1.0)
                sc.activation(out=ek[:, :], in_=s1[:, :], func=AF.Relu, bias=SM["dtb"][:, :])
                dv.tensor_tensor(out=s2[:, :], in0=s2[:, :], in1=ek[:, :], op=ALU.add)
                dv.tensor_scalar_mul(out=s1[:, :], in0=s2[:, :], scalar1=SM["negea"][:, :])
                dv.tensor_tensor_scan(out=g[:, :], data0=CN["scanmask"][:, :],
                                      data1=s1[:, :], initial=0.0,
                                      op0=ALU.mult, op1=ALU.add)
            out.append(dt_chain)
            def beta_chain():
                for kc in range(8):
                    pe.matmul(pP2[0:16, :], wb_s[:, kc*16:(kc+1)*16],
                              xfm[:, kc*512:(kc+1)*512], start=(kc == 0), stop=(kc == 7))
                pe.flush()
                dv.tensor_copy(s1[:, :], pP2[0:16, :])
                sc.activation(out=s2[:, :], in_=s1[:, :], func=AF.Abs)
                sc.activation(out=r[:, :], in_=s2[:, :], func=AF.Exp, scale=-1.0)
                sc.activation(out=s2[:, :], in_=r[:, :], func=AF.Ln, bias=1.0)
                dv.tensor_scalar_min(out=r[:, :], in0=s1[:, :], scalar1=0.0)
                dv.tensor_tensor(out=lnb[:, :], in0=r[:, :], in1=s2[:, :], op=ALU.subtract)
            out.append(beta_chain)

            def norms():
                for which, srcf in [("k", kfm), ("q", qfm)]:
                    for mc in range(8):
                        sq = scr1[:, 0:512] if mc % 2 == 0 else scr1[:, 512:1024]
                        sc.activation(out=sq, in_=srcf[:, mc*512:(mc+1)*512],
                                      func=AF.Square)
                        pe.matmul(pP1[0:16, :], CN["ehm"][:, mc*16:(mc+1)*16], sq,
                                  start=(mc == 0), stop=(mc == 7))
                        pe.flush()
                    sc.activation(out=VV["s2"][:, :], in_=pP1[0:16, :], func=AF.Ln)
                    if which == "k":
                        dv.tensor_scalar_mul(out=VV["lnk"][:, :], in0=VV["s2"][:, :],
                                             scalar1=0.5)
                    else:
                        dv.tensor_scalar(out=VV["s1"][:, :], in0=VV["s2"][:, :],
                                         scalar1=0.5, scalar2=float(0.5*np.log(D)),
                                         op0=ALU.mult, op1=ALU.add)
                        dv.tensor_tensor(out=VV["r"][:, :], in0=VV["g"][:, :],
                                         in1=VV["s1"][:, :], op=ALU.subtract)
            out.append(norms)
            def stacks():
                dv.tensor_tensor(out=ST["p"][0:16, :], in0=g[:, :], in1=lnk[:, :],
                                 op=ALU.subtract)
                dv.tensor_tensor(out=ST["n"][0:16, :], in0=g[:, :], in1=lnk[:, :],
                                 op=ALU.add)
                dv.tensor_tensor(out=ST["m"][0:16, :], in0=ST["n"][0:16, :], in1=lnb[:, :],
                                 op=ALU.subtract)
                base = g[0:16, 63:64]
                ge_view = bass.AP(base.tensor, base.offset, [[512, 16], [64, 8], [0, 64]])
                dv.tensor_tensor(out=ek[:, :].rearrange("p (c_ s_) -> p c_ s_", c_=8),
                                 in0=ge_view,
                                 in1=g[:, :].rearrange("p (c_ s_) -> p c_ s_", c_=8),
                                 op=ALU.subtract)
                dv.tensor_tensor(out=ek[:, :], in0=ek[:, :], in1=lnk[:, :], op=ALU.subtract)
                gp.tensor_copy(vec4[0:16, :], ST["p"][0:16, :])
                gp.tensor_copy(vec4[32:48, :], VV["lnb"][:, :])
                gp.tensor_copy(vec4[64:80, :], VV["r"][:, :])
                gp.tensor_copy(vec4[96:112, :], VV["ek"][:, :])
            out.append(stacks)
            return out

        def prep_stages(sl, it):
            t0 = it * TT
            c0, c1 = 2*sl, 2*sl + 1
            MT = TL["MTa"] if sl % 2 == 0 else TL["MTb"]
            PX = TL["P1"] if sl % 2 == 0 else TL["P2"]
            out = []

            def vproj(sl=sl, t0=t0):
                xfm = xfm2[it % 2]
                for nh in range(2):
                    for kc in range(8):
                        pe.matmul(p1[:, nh*512:(nh+1)*512],
                                  xfm[:, kc*512 + sl*128:kc*512 + (sl+1)*128],
                                  W["wv"][:, kc*1024 + nh*512:kc*1024 + (nh+1)*512],
                                  start=(kc == 0), stop=(kc == 7))
                    pe.flush()
                dv.tensor_copy(vtm2[sl % 2][:, :], p1[:, :])
            out.append(vproj)

            def ktx(sl=sl, t0=t0):
                sy.dma_start(ktm[sl % 2][:, :],
                             kfm_h[:, t0+sl*128:t0+(sl+1)*128], transpose=True)
            out.append(ktx)

            def kk():
                order = ([(h, half) for h in range(0, 16, 2) for half in (0, 1)]
                         + [(h, half) for h in range(1, 16, 2) for half in (1, 0)])
                for h, half in order:
                    b = 64 * (h % 2)
                    j = h // 2
                    cch = c0 if half == 0 else c1
                    pe.matmul(p1[64*half:64*half + 64, h*64:(h+1)*64],
                              kfm[b:b+64, j*512 + cch*64:j*512 + (cch+1)*64],
                              kfm[b:b+64, j*512 + cch*64:j*512 + (cch+1)*64],
                              start=True, stop=True, tile_position=(b, 64*half))
                pe.flush()
            out.append(kk)

            def gbuild(lhs_nm, rbvec, mask_nm, rev, kk_ap, dst):
                for half, cch in [(0, c0), (1, c1)]:
                    if rbvec == "r":
                        srcv = VV["r"][0:16, cch*64:(cch+1)*64]
                    else:
                        srcv = ST[rbvec][0:16, cch*64:(cch+1)*64]
                    vw = bass.AP(srcv.tensor, srcv.offset,
                                 [[512, 16], [0, 16], [1, 64]])
                    gp.tensor_tensor(
                        out=RH[32:48, :].rearrange("p (hh s_) -> p hh s_", hh=16),
                        in0=vw,
                        in1=CN["ehead16"][:, :].rearrange("p (hh s_) -> p hh s_", hh=16),
                        op=ALU.mult)
                    for nh in range(2):
                        pe.matmul(p2[64*half:64*half + 64, nh*512:(nh+1)*512],
                                  ST[lhs_nm][:, cch*64:(cch+1)*64],
                                  RH[:, nh*512:(nh+1)*512],
                                  start=True, stop=True, tile_position=(0, 64*half))
                    pe.flush()
                if rev:
                    dv.tensor_tensor(out=t_Gm[:, :], in0=CN[mask_nm][:, :],
                                     in1=p2[:, :], op=ALU.subtract)
                else:
                    dv.tensor_tensor(out=t_Gm[:, :], in0=p2[:, :],
                                     in1=CN[mask_nm][:, :], op=ALU.add)
                sc.activation(out=TL["Eg"][:, 0:512], in_=t_Gm[:, 0:512], func=AF.Exp)
                sc.activation(out=TL["Eg"][:, 512:1024], in_=t_Gm[:, 512:1024],
                              func=AF.Exp)
                for cl in (slice(0, 512), slice(512, 1024)):
                    if dst is MT:
                        dv.tensor_tensor(out=dst[:, cl], in0=kk_ap[:, cl],
                                         in1=TL["Eg"][:, cl], op=ALU.mult)
                    else:
                        dv.scalar_tensor_tensor(out=dst[:, cl], in0=kk_ap[:, cl],
                                                scalar=-1.0, in1=TL["Eg"][:, cl],
                                                op0=ALU.mult, op1=ALU.mult)

            out.append(lambda: gbuild("p", "m", "maskA", False, p1, TL["NA"]))
            out.append(lambda: gbuild("m", "p", "maskAT", True, p1, TL["Z0"]))

            def qkt():
                order = ([(h, half) for h in range(0, 16, 2) for half in (0, 1)]
                         + [(h, half) for h in range(1, 16, 2) for half in (1, 0)])
                for h, half in order:
                    b = 64 * (h % 2)
                    j = h // 2
                    cch = c0 if half == 0 else c1
                    pe.matmul(p1[64*half:64*half + 64, h*64:(h+1)*64],
                              kfm[b:b+64, j*512 + cch*64:j*512 + (cch+1)*64],
                              qfm[b:b+64, j*512 + cch*64:j*512 + (cch+1)*64],
                              start=True, stop=True, tile_position=(b, 64*half))
                pe.flush()
            out.append(qkt)
            out.append(lambda: gbuild("n", "r", "maskMT", True, p1, MT))

            def p0init():
                gp.tensor_tensor(out=TL["P0"][:, :], in0=TL["Z0"][:, :],
                                 in1=CN["identP"][:, :], op=ALU.add)
            out.append(p0init)

            seqs = []
            Bc, Zc, Pc = TL["NA"], TL["Z0"], TL["P0"]
            Bn, Zn, Pn = TL["B1"], TL["Z1"], PX
            for lev in range(1, 6):
                def fb(Bc=Bc, Zc=Zc, Bn=Bn):
                    for h in range(16):
                        for half in (0, 1):
                            b = 64 * half
                            pe.matmul(p1[b:b+64, h*64:(h+1)*64],
                                      Zc[b:b+64, h*64:(h+1)*64],
                                      Bc[b:b+64, h*64:(h+1)*64],
                                      start=True, stop=True, tile_position=(b, b))
                    pe.flush()
                    dv.tensor_copy(Bn[:, 0:512], p1[:, 0:512])
                    dv.tensor_copy(Bn[:, 512:1024], p1[:, 512:1024])
                out.append(fb)
                if lev < 5:
                    def fz(Bc=Bc, Zc=Zc, Zn=Zn):
                        for h in range(16):
                            for half in (0, 1):
                                b = 64 * half
                                pe.matmul(p1[b:b+64, h*64:(h+1)*64],
                                          Bc[b:b+64, h*64:(h+1)*64],
                                          Zc[b:b+64, h*64:(h+1)*64],
                                          start=True, stop=True, tile_position=(b, b))
                        pe.flush()
                        sc.activation(out=Zn[:, 0:512], in_=p1[:, 0:512],
                                      func=AF.Identity)
                        sc.activation(out=Zn[:, 512:1024], in_=p1[:, 512:1024],
                                      func=AF.Identity)
                    out.append(fz)
                def fp(Bn=Bn, Pc=Pc, Pn=Pn):
                    for h in range(16):
                        for half in (0, 1):
                            b = 64 * half
                            pe.matmul(p2[b:b+64, h*64:(h+1)*64],
                                      Bn[b:b+64, h*64:(h+1)*64],
                                      Pc[b:b+64, h*64:(h+1)*64],
                                      start=True, stop=True, tile_position=(b, b))
                    pe.flush()
                    dv.tensor_tensor(out=Pn[:, 0:512], in0=p2[:, 0:512],
                                     in1=Pc[:, 0:512], op=ALU.add)
                    dv.tensor_tensor(out=Pn[:, 512:1024], in0=p2[:, 512:1024],
                                     in1=Pc[:, 512:1024], op=ALU.add)
                out.append(fp)
                Bc, Bn = Bn, (TL["B2"] if Bn is TL["B1"] else TL["B1"])
                Zc, Zn = Zn, (TL["Z2"] if Zn is TL["Z1"] else TL["Z1"])
                Pc, Pn = Pn, (TL["P0"] if Pn is PX else PX)
            return out, Pc, MT

        def seq_stages(sl, Pfin, MT):
            c0, c1 = 2*sl, 2*sl + 1
            out = []

            def s_tr(c0=c0):
                pe.transpose(pS[:, 0:128], vec4[:, c0*64:c0*64 + 128],
                             CN["identF"][:, :])
                pe.flush()
                sc.activation(out=bc4[:, 0:64], in_=pS[:, 0:64], func=AF.Exp)
                sc.activation(out=bc4[:, 64:128], in_=pS[:, 64:128], func=AF.Exp)
            out.append(s_tr)

            for hf, cch in [(0, c0), (1, c1)]:
                def bce(grp, b0=64*hf):
                    b = bc4[b0:b0+64, grp:grp+16]
                    return bass.AP(b.tensor, b.offset, [[128, 64], [1, 16], [0, 64]])

                b0 = 64 * hf
                hs = slice(b0, b0 + 64)
                css = slice(cch*64, (cch+1)*64)

                def s_ks0(cch=cch, b0=b0, hs=hs, css=css, sl=sl, bce=bce):
                    ob = 64 - b0
                    for h in list(range(0, 16, 2)) + [-1] + list(range(1, 16, 2)):
                        if h < 0:
                            pe.matmul(pS[ob:ob+64, 0:64], kfm[64:128, 0:64],
                                      S0b[64:128, 0:64], start=True, stop=True,
                                      tile_position=(64, ob))
                            continue
                        b = 64 * (h % 2)
                        j = h // 2
                        pe.matmul(pS[b0:b0+64, h*64:(h+1)*64],
                                  kfm[b:b+64, j*512 + cch*64:j*512 + (cch+1)*64],
                                  S0b[b:b+64, j*64:(j+1)*64],
                                  start=True, stop=True, tile_position=(b, b0))
                    pe.flush()
                    dv.tensor_tensor(out=TL["t2"][hs, :].rearrange("p (h s_) -> p h s_", s_=64),
                                     in0=pS[hs, :].rearrange("p (h s_) -> p h s_", s_=64),
                                     in1=bce(0), op=ALU.mult)
                    dv.tensor_tensor(out=TL["R"][hs, :],
                                     in0=vtm2[sl % 2][hs, :],
                                     in1=TL["t2"][hs, :], op=ALU.subtract)
                out.append(s_ks0)

                def s_tapply(b0=b0, hs=hs, css=css, Pfin=Pfin, bce=bce):
                    for h in range(16):
                        pe.matmul(pS[b0:b0+64, h*64:(h+1)*64],
                                  Pfin[b0:b0+64, h*64:(h+1)*64],
                                  TL["R"][b0:b0+64, h*64:(h+1)*64],
                                  start=True, stop=True, tile_position=(b0, b0))
                    pe.flush()
                    for ci in (0, 1):
                        cl = slice(ci*512, (ci+1)*512)
                        bv = bce(32)
                        bv = bass.AP(bv.tensor, bv.offset + ci*8,
                                     [[128, 64], [1, 8], [0, 64]])
                        dv.tensor_tensor(out=TL["U"][hs, cl].rearrange("p (h s_) -> p h s_", s_=64),
                                         in0=pS[hs, cl].rearrange("p (h s_) -> p h s_", s_=64),
                                         in1=bv, op=ALU.mult)
                out.append(s_tapply)

                def s_qs0(cch=cch, b0=b0, hs=hs, css=css, bce=bce):
                    ob = 64 - b0
                    for h in list(range(0, 16, 2)) + [-1] + list(range(1, 16, 2)):
                        if h < 0:
                            pe.matmul(pS[ob:ob+64, 0:64], qfm[64:128, 0:64],
                                      S0b[64:128, 0:64], start=True, stop=True,
                                      tile_position=(64, ob))
                            continue
                        b = 64 * (h % 2)
                        j = h // 2
                        pe.matmul(pS[b0:b0+64, h*64:(h+1)*64],
                                  qfm[b:b+64, j*512 + cch*64:j*512 + (cch+1)*64],
                                  S0b[b:b+64, j*64:(j+1)*64],
                                  start=True, stop=True, tile_position=(b, b0))
                    pe.flush()
                    dv.tensor_tensor(out=TL["t2"][hs, :].rearrange("p (h s_) -> p h s_", s_=64),
                                     in0=pS[hs, :].rearrange("p (h s_) -> p h s_", s_=64),
                                     in1=bce(64), op=ALU.mult)
                out.append(s_qs0)

                def s_oapply(b0=b0, hs=hs, sl=sl, MT=MT):
                    for h in range(16):
                        pe.matmul(pS[b0:b0+64, h*64:(h+1)*64],
                                  MT[b0:b0+64, h*64:(h+1)*64],
                                  TL["U"][b0:b0+64, h*64:(h+1)*64],
                                  start=True, stop=True, tile_position=(b0, b0))
                    pe.flush()
                    dv.tensor_tensor(out=otm[sl % 2][hs, :], in0=pS[hs, :],
                                     in1=TL["t2"][hs, :], op=ALU.add)
                out.append(s_oapply)

                def s_bc_ek(cch=cch, b0=b0, hs=hs, css=css, sl=sl, bce=bce):
                    gp.tensor_tensor(out=TL["Ku"][hs, :].rearrange("p (h s_) -> p h s_", s_=64),
                                     in0=ktm[sl % 2][hs, :].rearrange("p (h s_) -> p h s_", s_=64),
                                     in1=bce(96), op=ALU.mult)
                out.append(s_bc_ek)

                def s_supd(cch=cch, b0=b0, hs=hs, css=css, sl=sl):
                    sc.activation(out=sEg1[:, 0:1],
                                  in_=VV["g"][0:16, cch*64+63:cch*64+64], func=AF.Exp)
                    ebase = sEg1[0:16, 0:1]
                    eview = bass.AP(ebase.tensor, ebase.offset,
                                    [[ebase.ap[0][0], 16], [0, 512]])
                    dv.tensor_tensor(out=sEgr[:, :], in0=CN["epair"][:, :],
                                     in1=eview, op=ALU.mult)
                    pe.matmul(pS[:, 0:512], CN["parity"][:, :], sEgr[:, :],
                              start=True, stop=True)
                    pe.flush()
                    for h in range(16):
                        b = 64 * (h % 2)
                        j = h // 2
                        pe.matmul(pS[b:b+64, 512 + j*64:512 + (j+1)*64],
                                  TL["Ku"][b0:b0+64, h*64:(h+1)*64],
                                  TL["U"][b0:b0+64, h*64:(h+1)*64],
                                  start=True, stop=True, tile_position=(b0, b))
                    pe.flush()
                    dv.tensor_tensor(out=sSd[:, :], in0=S0[:, :], in1=pS[:, 0:512],
                                     op=ALU.mult)
                    dv.tensor_tensor(out=S0[:, :], in0=pS[:, 512:1024], in1=sSd[:, :],
                                     op=ALU.add)
                    sc.activation(out=S0b[:, :], in_=S0[:, :], func=AF.Identity)
                out.append(s_supd)
            return out

        def epi1_stages(sl, it):
            t0 = it * TT
            xfm = xfm2[it % 2]
            om = otm[sl % 2]
            out = []
            def rms(om=om, sl=sl, t0=t0):
                gp.dma_start(xbsl[:, :], x_in[t0+sl*128:t0+(sl+1)*128, :])
                sc.activation(out=escr[:, :], in_=om[:, :], func=AF.Square)
                dv.tensor_reduce(out=colsc[:, 0:16],
                                 in_=escr[:, :].rearrange("p (h s_) -> p h s_", s_=64),
                                 axis=AX.X, op=ALU.add)
                dv.tensor_scalar(out=colsc[:, 16:32], in0=colsc[:, 0:16],
                                 scalar1=1.0/64.0, scalar2=1e-6,
                                 op0=ALU.mult, op1=ALU.add)
                sc.activation(out=colsc[:, 32:48], in_=colsc[:, 16:32], func=AF.Ln)
                sc.activation(out=colsc[:, 48:64], in_=colsc[:, 32:48], func=AF.Exp,
                              scale=-0.5)
                rbase = colsc[:, 48:64]
                rstdv = bass.AP(rbase.tensor, rbase.offset,
                                [[rbase.ap[0][0], 128], [1, 16], [0, 64]])
                dv.tensor_tensor(out=scrb[:, :].rearrange("p (h s_) -> p h s_", s_=64),
                                 in0=om[:, :].rearrange("p (h s_) -> p h s_", s_=64),
                                 in1=rstdv, op=ALU.mult)
            out.append(rms)
            def gates(om=om, sl=sl, t0=t0):
                for nh, bank in [(0, pP1), (1, pP2)]:
                    for kc in range(8):
                        pe.matmul(bank[:, :],
                                  xfm[:, kc*512 + sl*128:kc*512 + (sl+1)*128],
                                  W["wgate"][:, kc*1024 + nh*512:kc*1024 + (nh+1)*512],
                                  start=(kc == 0), stop=(kc == 7))
                    pe.flush()
                sc.activation(out=om[:, 0:512], in_=pP1[:, :], func=AF.Silu)
                sc.activation(out=om[:, 512:1024], in_=pP2[:, :], func=AF.Silu)
                gp.tensor_tensor(out=scrb[:, :], in0=scrb[:, :], in1=om[:, :],
                                 op=ALU.mult)
                sy.dma_start(og_h[t0+sl*128:t0+(sl+1)*128, :], scrb[:, :])
                for ec in range(8):
                    sy.dma_start(om[:, ec*128:(ec+1)*128],
                                 og_h[t0+sl*128:t0+(sl+1)*128, ec*128:(ec+1)*128],
                                 transpose=True)
            out.append(gates)
            return out

        def epi2_stages(sl, it):
            t0 = it * TT
            om = otm[sl % 2]
            out = []
            def wout(om=om, sl=sl, t0=t0):
                for nh, bank in [(0, pP1), (1, pP2)]:
                    for kc in range(8):
                        pe.matmul(bank[:, :],
                                  om[:, kc*128:(kc+1)*128],
                                  W["wout"][:, kc*1024 + nh*512:kc*1024 + (nh+1)*512],
                                  start=(kc == 0), stop=(kc == 7))
                    pe.flush()
                dv.tensor_tensor(out=escr[:, 0:512], in0=pP1[:, :],
                                 in1=xbsl[:, 0:512], op=ALU.add)
                dv.tensor_tensor(out=escr[:, 512:1024], in0=pP2[:, :],
                                 in1=xbsl[:, 512:1024], op=ALU.add)
                sc.activation(out=scrb[:, :], in_=escr[:, :], func=AF.Square,
                              accum_out=colsc[:, 0:1])
                dv.tensor_reduce(out=colsc[:, 1:2], in_=escr[:, :], axis=AX.X,
                                 op=ALU.add)
                dv.tensor_scalar_mul(out=colsc[:, 2:3], in0=colsc[:, 1:2],
                                     scalar1=1.0/1024.0)
                dv.tensor_scalar_mul(out=colsc[:, 3:4], in0=colsc[:, 2:3],
                                     scalar1=colsc[:, 2:3])
                dv.tensor_scalar(out=colsc[:, 4:5], in0=colsc[:, 0:1],
                                 scalar1=1.0/1024.0, scalar2=1e-5,
                                 op0=ALU.mult, op1=ALU.add)
                dv.tensor_tensor(out=colsc[:, 4:5], in0=colsc[:, 4:5],
                                 in1=colsc[:, 3:4], op=ALU.subtract)
                sc.activation(out=colsc[:, 5:6], in_=colsc[:, 4:5], func=AF.Ln)
                sc.activation(out=colsc[:, 6:7], in_=colsc[:, 5:6], func=AF.Exp,
                              scale=-0.5)
                dv.tensor_scalar(out=scrb[:, :], in0=escr[:, :],
                                 scalar1=colsc[:, 2:3], scalar2=colsc[:, 6:7],
                                 op0=ALU.subtract, op1=ALU.mult)
                sy.dma_start(x2_h[t0+sl*128:t0+(sl+1)*128, :], scrb[:, :])
            out.append(wout)
            return out

        def weave(A, B):
            if not A:
                for f in B:
                    f()
                return
            nb, na = len(B), len(A)
            bi = 0
            for i, fa in enumerate(A):
                want = ((i + 1) * nb) // na
                while bi < want:
                    B[bi]()
                    bi += 1
                fa()
            while bi < nb:
                B[bi]()
                bi += 1

        prep_next = None
        for it in range(_N_TILES):
            for f in t0a_stages(it, skip_load=(it > 0)):
                f()
            if it == 0:
                for nm in ["wv", "wgate", "wout"]:
                    gp.dma_start(W[nm][:, :].rearrange("p (kc n) -> p kc n", n=1024),
                                 w_in[nm].rearrange("(kc p) n -> p kc n", p=128))
                gp.dma_start(wa_s[:, :].rearrange("p (kc c_) -> p kc c_", c_=16),
                             wa_in.rearrange("(kc p) n -> p kc n", p=128))
                gp.dma_start(wb_s[:, :].rearrange("p (kc c_) -> p kc c_", c_=16),
                             wb_in.rearrange("(kc p) n -> p kc n", p=128))
                for kc in range(8):
                    dv.tensor_copy(scrb[:, :], W["wout"][:, kc*1024:(kc+1)*1024])
                    dv.tensor_scalar_mul(out=W["wout"][:, kc*1024:(kc+1)*1024],
                                         in0=scrb[:, :], scalar1=rmswc[:, :])
            if it > 0:
                for f in epi2_stages(3, it - 1):
                    f()
            stg, Pfin, MT = prep_stages(0, it)
            weave(t0b_stages(it), stg[0:3])
            for f in stg[3:]:
                f()
            for sl in range(NSL):
                QB = []
                if sl >= 1:
                    QB += epi1_stages(sl - 1, it)
                if sl < 3:
                    stg2, Pfin2, MT2 = prep_stages(sl + 1, it)
                    QB += stg2
                elif it + 1 < _N_TILES:
                    QB.append(xfm_load(it + 1))
                weave(seq_stages(sl, Pfin, MT), QB)
                if sl >= 1:
                    for f in epi2_stages(sl - 1, it):
                        f()
                if sl < 3:
                    Pfin, MT = Pfin2, MT2
            for f in epi1_stages(3, it):
                f()
            if it == _N_TILES - 1:
                for f in epi2_stages(3, it):
                    f()

        dve_A, act_A, pool_A, pe_A = cnt["dve"], cnt["act"], cnt["pool"], cnt["pe"]
        ldA_A = {s: cnt[s] for s in QSEMS["sy"]}
        ldB_A = {s: cnt[s] for s in QSEMS["gpL"]}
        wait("pe", "dve", dve_A)
        wait("pe", "act", act_A)
        inc(op("pe", "nop"), "bar", 1)
        wait("dv", "pe", pe_A)
        inc(op("dv", "engine_nop"), "bar", 1)
        wait("sc", "dve", dve_A)
        inc(op("sc", "nop"), "bar", 1)
        for s, v in ldB_A.items():
            wait("gp", s, v)
        wait("gp", "pool", pool_A)
        inc(op("gp", "engine_nop"), "bar", 1)
        for s, v in ldA_A.items():
            wait("sy", s, v)
        inc(op("sy", "nop"), "bar", 1)
        BAR = cnt["bar"]

        emit_all()
        stA.close()

        stB = contextlib.ExitStack()
        sbB = lambda n, s_, d: stB.enter_context(nc.sbuf_tensor(n, s_, d))
        wgT = sbB("wgT", [128, 8*3072], BF16)
        wuT = sbB("wuT", [128, 8*3072], BF16)
        wdT = sbB("wdT", [128, 24*1024], BF16)
        x2fm = sbB("x2fm", [128, 8*512], BF16)
        x2tm = sbB("x2tm", [128, 4*1024], BF16)
        h_s = sbB("h_s", [128, 24*512], BF16)
        b_scr1 = sbB("b_scr1", [128, 1024], F32)
        b_scr2 = sbB("b_scr2", [128, 1024], F32)
        b_scrb = sbB("b_scrb", [128, 512], BF16)
        b_col = sbB("b_col", [128, 16], F32)
        bpA = stB.enter_context(nc.psum_tensor("bpA", [128, 512], F32))
        bpB = stB.enter_context(nc.psum_tensor("bpB", [128, 512], F32))
        bpC = stB.enter_context(nc.psum_tensor("bpC", [128, 512], F32))
        bpD = stB.enter_context(nc.psum_tensor("bpD", [128, 512], F32))

        if _PHASEB_ON:
            wait("gp", "bar", BAR)
            gp.dma_start(wgT[:, :].rearrange("p (ec n) -> p ec n", n=3072),
                         fgT_in.rearrange("(ec p) n -> p ec n", p=128))
            gp.dma_start(wuT[:, :].rearrange("p (ec n) -> p ec n", n=3072),
                         fuT_in.rearrange("(ec p) n -> p ec n", p=128))

        for it in range(NT if _PHASEB_ON else 0):
            t0 = it * TT
            wait("sy", "bar", BAR)
            for ec in range(8):
                sy.dma_start(x2fm[:, ec*512:(ec+1)*512],
                             x2_h[t0:t0+TT, ec*128:(ec+1)*128], transpose=True)
            gp.dma_start(x2tm[:, :].rearrange("p (sl e) -> p sl e", e=1024),
                         x2_h[t0:t0+TT, :].rearrange("(sl p) e -> p sl e", p=128))
            if it == 0:
                gp.dma_start(wdT[:, :].rearrange("p (hc n) -> p hc n", n=1024),
                             fdT_in.rearrange("(hc p) n -> p hc n", p=128))

            for hc in range(24):
                for ec in range(8):
                    pe.matmul(bpA[:, :], wgT[:, ec*3072 + hc*128:ec*3072 + (hc+1)*128],
                              x2fm[:, ec*512:(ec+1)*512], start=(ec == 0), stop=(ec == 7))
                pe.flush()
                for ec in range(8):
                    pe.matmul(bpB[:, :], wuT[:, ec*3072 + hc*128:ec*3072 + (hc+1)*128],
                              x2fm[:, ec*512:(ec+1)*512], start=(ec == 0), stop=(ec == 7))
                pe.flush()
                sc.activation(out=b_scrb[:, :], in_=bpA[:, :], func=AF.Silu, scale=RQ)
                dv.scalar_tensor_tensor(out=h_s[:, hc*512:(hc+1)*512], in0=bpB[:, :],
                                        scalar=RQ*RQ, in1=b_scrb[:, :],
                                        op0=ALU.mult, op1=ALU.mult)

            for sl in range(NSL):
                for nh, bank in [(0, bpC), (1, bpD)]:
                    for hc in range(24):
                        pe.matmul(bank[:, :],
                                  h_s[:, hc*512 + sl*128:hc*512 + (sl+1)*128],
                                  wdT[:, hc*1024 + nh*512:hc*1024 + (nh+1)*512],
                                  start=(hc == 0), stop=(hc == 23))
                    pe.flush()
                dv.tensor_tensor(out=b_scr1[:, 0:512], in0=bpC[:, :],
                                 in1=x2tm[:, sl*1024:sl*1024 + 512], op=ALU.add)
                dv.tensor_tensor(out=b_scr1[:, 512:1024], in0=bpD[:, :],
                                 in1=x2tm[:, sl*1024 + 512:(sl+1)*1024], op=ALU.add)
                sc.activation(out=b_scr2[:, :], in_=b_scr1[:, :], func=AF.Square,
                              accum_out=b_col[:, 0:1])
                dv.tensor_reduce(out=b_col[:, 1:2], in_=b_scr1[:, :], axis=AX.X,
                                 op=ALU.add)
                dv.tensor_scalar_mul(out=b_col[:, 2:3], in0=b_col[:, 1:2],
                                     scalar1=1.0/1024.0)
                dv.tensor_scalar_mul(out=b_col[:, 3:4], in0=b_col[:, 2:3],
                                     scalar1=b_col[:, 2:3])
                dv.tensor_scalar(out=b_col[:, 4:5], in0=b_col[:, 0:1],
                                 scalar1=1.0/1024.0, scalar2=1e-5,
                                 op0=ALU.mult, op1=ALU.add)
                dv.tensor_tensor(out=b_col[:, 4:5], in0=b_col[:, 4:5],
                                 in1=b_col[:, 3:4], op=ALU.subtract)
                sc.activation(out=b_col[:, 5:6], in_=b_col[:, 4:5], func=AF.Ln)
                sc.activation(out=b_col[:, 6:7], in_=b_col[:, 5:6], func=AF.Exp,
                              scale=-0.5)
                dv.tensor_scalar(out=b_scr2[:, :], in0=b_scr1[:, :],
                                 scalar1=b_col[:, 2:3], scalar2=b_col[:, 6:7],
                                 op0=ALU.subtract, op1=ALU.mult)
                DM_("gp", "gpS").dma_start(out_t[t0+sl*128:t0+(sl+1)*128, :],
                                           b_scr2[:, :])

        for s in QSEMS["gpS"]:
            wait("gp", s, cnt[s])

        emit_all()
        stB.close()

    return nc


def _shard_inputs(inputs):
    cns = host_consts()
    rmsw = np.asarray(inputs["L0_attn_rms_w"])
    base = {
        "wq": np.ascontiguousarray(inputs["L0_attn_Wq"]),
        "wk": np.ascontiguousarray(inputs["L0_attn_Wk"]),
        "wv": np.ascontiguousarray(inputs["L0_attn_Wv"]),
        "wgate": np.ascontiguousarray(inputs["L0_attn_Wgate"]),
        "wout": np.ascontiguousarray(inputs["L0_attn_Wout"]),
        "wa": np.ascontiguousarray(inputs["L0_attn_Walpha"]),
        "wb": np.ascontiguousarray(inputs["L0_attn_Wbeta"]),
        "alog": np.ascontiguousarray(np.asarray(inputs["L0_attn_A_log"]).reshape(16, 1)),
        "dtb": np.ascontiguousarray(np.asarray(inputs["L0_attn_dt_bias"]).reshape(16, 1)),
        "rmsw_c": np.ascontiguousarray(np.tile(rmsw, 2).reshape(128, 1).astype(np.float32)),
        "fgT": np.ascontiguousarray(np.asarray(inputs["L0_ffn_w_gate"]).T),
        "fuT": np.ascontiguousarray(np.asarray(inputs["L0_ffn_w_up"]).T),
        "fdT": np.ascontiguousarray(np.asarray(inputs["L0_ffn_w_down"]).T),
    }
    for k, v in cns.items():
        base[k] = v
    x = np.asarray(inputs["L0_attn_x"])
    in_maps = []
    for b in range(8):
        m = dict(base)
        m["x"] = np.ascontiguousarray(x[b])
        m["xT"] = np.ascontiguousarray(x[b].T)
        in_maps.append(m)
    return in_maps


def kernel(**inputs):
    in_maps = _shard_inputs(inputs)
    nc = build_program()
    res = run_bass_kernel_spmd(nc, in_maps, core_ids=list(range(8)))
    return np.stack([r["out"] for r in res.results], axis=0)
